# version 12
# baseline (speedup 1.0000x reference)
import numpy as np
import concourse.bass as bass
import concourse.mybir as mybir
from concourse.bass_utils import run_bass_kernel_spmd
from concourse.aot_env import allow_ici_all_to_all
from contextlib import ExitStack

F32 = mybir.dt.float32
BF16 = mybir.dt.bfloat16
I32 = mybir.dt.int32
ALU = mybir.AluOpType
ACTF = mybir.ActivationFunctionType
AX = mybir.AxisListType

NCORES = 8
TOK = 1024
HALO = 3
POS = 1152
NT = POS // 128
DM = 2048
DIN = 4096
DST = 16
DTR = 128
DCH = DIN // 128
KC = DM // 128
CPC = DIN // NCORES
BL = CPC // 128
TQ = 8
TQW = 1024
MAGIC = 12582912.0
EPS_AMAX = 1e-18
LN127 = 4.844187086458591

_DEBUG = False


def build_nc():
    nc = bass.Bass("TRN2", target_bir_lowering=False)

    x_in = nc.dram_tensor("x", [POS, DM], F32, kind="ExternalInput").ap()
    w_in_4 = nc.dram_tensor("w_in_4", [128, 2 * DCH, KC, 128], F32, kind="ExternalInput").ap()
    conv_w = nc.dram_tensor("conv_w", [DIN, 4], F32, kind="ExternalInput").ap()
    conv_b = nc.dram_tensor("conv_b", [DIN, 1], F32, kind="ExternalInput").ap()
    w_x_t = nc.dram_tensor("w_x_t", [DIN, 160], F32, kind="ExternalInput").ap()
    w_dt_t = nc.dram_tensor("w_dt_t", [DTR, DIN], F32, kind="ExternalInput").ap()
    b_dt = nc.dram_tensor("b_dt", [DIN, 1], F32, kind="ExternalInput").ap()
    a_log_sl = nc.dram_tensor("a_log_sl", [CPC, DST], F32, kind="ExternalInput").ap()
    d_vec = nc.dram_tensor("d_vec", [DIN, 1], F32, kind="ExternalInput").ap()
    w_out_t = nc.dram_tensor("w_out_t", [DIN, DM], F32, kind="ExternalInput").ap()
    ffn_k_t = nc.dram_tensor("ffn_k_t", [DM, DM], F32, kind="ExternalInput").ap()
    ffn_k = nc.dram_tensor("ffn_k", [DM, DM], F32, kind="ExternalInput").ap()
    ffn_b = nc.dram_tensor("ffn_b", [1, DM], F32, kind="ExternalInput").ap()
    out_d = nc.dram_tensor("out", [TOK, DM], F32, kind="ExternalOutput").ap()

    ROWS_SH = 2 * CPC + 2 * DST
    a2a_in = nc.dram_tensor("a2a_in", [NCORES, ROWS_SH, TOK], F32)
    a2a_out = nc.dram_tensor("a2a_out", [NCORES, ROWS_SH, TOK], F32)
    y_in = nc.dram_tensor("y_in", [NCORES, CPC, TOK], F32)
    y_out = nc.dram_tensor("y_out", [NCORES, CPC, TOK], F32)
    u_scr = nc.dram_tensor("u_scr", [DIN, TOK], F32)
    sz_scr = nc.dram_tensor("sz_scr", [DIN, POS], F32)
    bq_scr = nc.dram_tensor("bq_scr", [1, DM], F32)
    o_scr = nc.dram_tensor("o_scr", [TOK, DM], F32)

    dbg = {}
    if _DEBUG:
        dbg["xn"] = nc.dram_tensor("dbg_xn", [POS, DM], F32, kind="ExternalOutput").ap()
        dbg["xs"] = nc.dram_tensor("dbg_xs", [DIN, POS], F32, kind="ExternalOutput").ap()
        dbg["xdbl"] = nc.dram_tensor("dbg_xdbl", [160, TOK], F32, kind="ExternalOutput").ap()
        dbg["dt"] = nc.dram_tensor("dbg_dt", [DIN, TOK], F32, kind="ExternalOutput").ap()
        dbg["yown"] = nc.dram_tensor("dbg_yown", [DIN, TOK], F32, kind="ExternalOutput").ap()
        dbg["ffnq"] = nc.dram_tensor("dbg_ffnq", [TOK, DM], F32, kind="ExternalOutput").ap()
        dbg["xqt"] = nc.dram_tensor("dbg_xqt", [128, KC * POS], BF16, kind="ExternalOutput").ap()
        dbg["xqt_e"] = nc.dram_tensor("dbg_xqt_e", [128, KC * POS], BF16, kind="ExternalOutput").ap()
        dbg["xqb"] = nc.dram_tensor("dbg_xqb", [POS, DM], BF16, kind="ExternalOutput").ap()
        dbg["qd"] = nc.dram_tensor("dbg_qd", [128, 4], F32, kind="ExternalOutput").ap()
        dbg["sq0"] = nc.dram_tensor("dbg_sq0", [128, DM], F32, kind="ExternalOutput").ap()
        dbg["kq"] = nc.dram_tensor("dbg_kq", [128, KC * DM], BF16, kind="ExternalOutput").ap()
        dbg["bcolq"] = nc.dram_tensor("dbg_bcolq", [128, 128], F32, kind="ExternalOutput").ap()
        dbg["dgq0"] = nc.dram_tensor("dbg_dgq0", [128, 1024], BF16, kind="ExternalOutput").ap()
        dbg["dtut"] = nc.dram_tensor("dbg_dtut", [128, 4096], BF16, kind="ExternalOutput").ap()
        dbg["dbu0"] = nc.dram_tensor("dbg_dbu0", [128, 1024], F32, kind="ExternalOutput").ap()

    cnt = {k: 0 for k in ("QS", "QC", "QG", "SV", "SS", "ST", "SG")}
    es = ExitStack()
    sem = {k: es.enter_context(nc.semaphore(k)) for k in cnt}

    idb = es.enter_context(nc.sbuf_tensor("idb", [128, 128], BF16))
    a_sb = es.enter_context(nc.sbuf_tensor("a_sb", [128, BL * DST], F32))
    carry = es.enter_context(nc.sbuf_tensor("carry", [128, BL * DST], F32))
    ln127c = es.enter_context(nc.sbuf_tensor("ln127c", [128, 1], F32))
    idf32 = es.enter_context(nc.sbuf_tensor("idf32", [128, 128], F32))

    block = es.enter_context(nc.Block())

    def snap():
        return dict(cnt)

    def wait_all(eng, snp, skip=()):
        for k, v in snp.items():
            if v > 0 and k not in skip:
                eng.wait_ge(sem[k], v)

    def dma(eng, q, out, in_):
        eng.dma_start(out=out, in_=in_).then_inc(sem[q], 16)
        cnt[q] += 16

    def inc(instr, name, n=1):
        instr.then_inc(sem[name], n)
        cnt[name] += n
        return instr

    def maxtree(v, buf, width, out_col):
        w = width // 2
        while w >= 1:
            inc(v.tensor_tensor(buf[:, 0:w], buf[:, 0:w], buf[:, w:2 * w], op=ALU.max), "SV")
            w //= 2
        inc(v.tensor_scalar(out_col, buf[:, 0:1], EPS_AMAX, None, op0=ALU.max), "SV")

    def preload_ln_exp_set(eng):
        from concourse.hw_specs import get_activation_tables
        for idx, funcs in enumerate(get_activation_tables(nc.m.arch).values()):
            if ACTF.Exp in funcs and ACTF.Ln in funcs:
                eng.add_instruction(mybir.InstLoadActFuncSet(
                    name=nc.get_next_instruction_name(),
                    act_func_set_id=idx, ins=[], outs=[]))
                return
        raise ValueError("no set with Exp+Ln")

    def sfence(s):
        s.drain()
        inc(s.nop(), "SS")

    with (
        nc.sbuf_tensor("pi", [128, 128], I32) as pi,
        nc.sbuf_tensor("fi", [128, 128], I32) as fi,
        nc.sbuf_tensor("idf0", [128, 128], F32) as idf0,
        nc.sbuf_tensor("alog", [128, BL * DST], F32) as alog,
    ):
        @block.gpsimd
        def _(g):
            inc(g.iota(pi[:, :], [[0, 128]], base=0, channel_multiplier=1), "SG")
            inc(g.iota(fi[:, :], [[1, 128]], base=0, channel_multiplier=0), "SG")

        @block.sync
        def _(s):
            for b in range(BL):
                dma(s, "QS", alog[:, b * DST:(b + 1) * DST],
                    a_log_sl[b * 128:(b + 1) * 128, :])

        @block.vector
        def _(v):
            v.wait_ge(sem["SG"], cnt["SG"])
            inc(v.memset(ln127c[:, :], LN127), "SV")
            inc(v.tensor_tensor(idf32[:, :], pi[:, :], fi[:, :], op=ALU.is_equal), "SV")
            inc(v.tensor_copy(idb[:, :], idf32[:, :]), "SV")
            inc(v.memset(carry[:, :], 0.0), "SV")

        @block.scalar
        def _(s):
            s.wait_ge(sem["QS"], cnt["QS"])
            s.activation(a_sb[:, :], alog[:, :], ACTF.Exp, bias=0.0, scale=1.0)
            sfence(s)

        @block.vector
        def _(v):
            v.wait_ge(sem["SS"], cnt["SS"])
            inc(v.tensor_scalar(a_sb[:, :], a_sb[:, :], -1.0, None, op0=ALU.mult), "SV")

        s0 = snap()

    es_xq = ExitStack()
    xqT = es_xq.enter_context(nc.sbuf_tensor("xqT", [128, KC * POS], BF16))
    es1 = ExitStack()
    xnT = es1.enter_context(nc.sbuf_tensor("xnT", [128, KC * POS], BF16))
    with (
        nc.sbuf_tensor("xt", [128, DM], F32) as xt,
        nc.sbuf_tensor("xn_f", [128, DM], F32) as xn_f,
        nc.sbuf_tensor("xn_b", [128, DM], BF16) as xn_b,
        nc.sbuf_tensor("xq_b", [128, DM], BF16) as xq_b,
        nc.sbuf_tensor("sq", [128, DM], F32) as sq,
        nc.sbuf_tensor("c0", [128, 1], F32) as c0,
        nc.sbuf_tensor("c1", [128, 1], F32) as c1,
        nc.sbuf_tensor("c2", [128, 1], F32) as c2,
        nc.sbuf_tensor("c3", [128, 1], F32) as c3,
        nc.sbuf_tensor("dbgq", [128, 4], F32) as dbgq,
        nc.sbuf_tensor("spc", [128, 256], F32) as spc,
        nc.psum_tensor("ptr", [128, 128], BF16) as ptr,
        nc.psum_tensor("ptr2", [128, 128], BF16) as ptr2,
        nc.psum_tensor("ptr3", [128, 128], BF16) as ptr3,
        nc.psum_tensor("ptr4", [128, 128], BF16) as ptr4,
    ):
        ptrs = [(ptr, ptr2), (ptr3, ptr4)]
        ev_s1evac = {}
        for tt in range(NT):
            base_v, base_s, base_t, base_q = cnt["SV"], cnt["SS"], cnt["ST"], cnt["QS"]

            @block.sync
            def _(s, tt=tt):
                if tt == 0:
                    wait_all(s, s0, skip=("QS",))
                else:
                    s.wait_ge(sem["ST"], base_t)
                dma(s, "QS", xt[:, :], x_in[tt * 128:(tt + 1) * 128, :])

            @block.scalar
            def _(s, tt=tt):
                s.wait_ge(sem["QS"], cnt["QS"])
                s.activation(sq[:, :], xt[:, :], ACTF.Square, bias=0.0, scale=1.0)
                sfence(s)

            @block.vector
            def _(v, tt=tt):
                v.wait_ge(sem["QS"], cnt["QS"])
                inc(v.reduce_sum(c0[:, :], xt[:, :], axis=AX.X), "SV")
                v.wait_ge(sem["SS"], base_s + 1)
                inc(v.reduce_sum(c1[:, :], sq[:, :], axis=AX.X), "SV")
                inc(v.tensor_scalar(c0[:, :], c0[:, :], 1.0 / DM, None, op0=ALU.mult), "SV")
                inc(v.tensor_scalar(c1[:, :], c1[:, :], 1.0 / DM, 1e-5,
                                    op0=ALU.mult, op1=ALU.add), "SV")
                inc(v.tensor_copy(spc[:, :], xt[:, 0:256]), "SV")
                inc(v.tensor_tensor(c2[:, :], c0[:, :], c0[:, :], op=ALU.mult), "SV")
                inc(v.tensor_copy(spc[:, :], xt[:, 256:512]), "SV")
                inc(v.tensor_tensor(c1[:, :], c1[:, :], c2[:, :], op=ALU.subtract), "SV")

            @block.scalar
            def _(s, tt=tt):
                s.wait_ge(sem["SV"], base_v + 8)
                inc(s.activation(c1[:, :], c1[:, :], ACTF.Ln, bias=0.0, scale=1.0), "SS")
                s.activation(c1[:, :], c1[:, :], ACTF.Exp, bias=0.0, scale=-0.5)
                sfence(s)

            @block.vector
            def _(v, tt=tt):
                v.wait_ge(sem["SS"], base_s + 3)
                inc(v.tensor_scalar(xn_f[:, :], xt[:, :], c0[:, :], c1[:, :],
                                    op0=ALU.subtract, op1=ALU.mult), "SV")
                inc(v.tensor_scalar(sq[:, :], xn_f[:, :], -1.0, None, op0=ALU.mult), "SV")
                inc(v.tensor_tensor(sq[:, :], sq[:, :], xn_f[:, :], op=ALU.max), "SV")
                inc(v.reduce_max(c2[:, :], sq[:, :], axis=AX.X), "SV")
                inc(v.tensor_copy(xn_b[:, :], xn_f[:, :]), "SV")
                inc(v.tensor_scalar(c2[:, :], c2[:, :], EPS_AMAX, None, op0=ALU.max), "SV")
                inc(v.tensor_copy(spc[:, :], xn_f[:, 0:256]), "SV")
                inc(v.reciprocal(c3[:, :], c2[:, :]), "SV")
                inc(v.tensor_copy(spc[:, :], xn_f[:, 256:512]), "SV")
                inc(v.tensor_scalar(sq[:, :], xn_f[:, :], c3[:, :], None, op0=ALU.mult), "SV")
                inc(v.tensor_scalar(sq[:, :], sq[:, :], 127.0, MAGIC,
                                    op0=ALU.mult, op1=ALU.add), "SV")
                inc(v.tensor_scalar(sq[:, :], sq[:, :], MAGIC + 127.0, MAGIC - 127.0,
                                    op0=ALU.min, op1=ALU.max), "SV")
                inc(v.tensor_scalar(sq[:, :], sq[:, :], MAGIC, c2[:, :],
                                    op0=ALU.subtract, op1=ALU.mult), "SV")
                inc(v.tensor_scalar(xq_b[:, :], sq[:, :], 1.0 / 127.0, None, op0=ALU.mult), "SV")

            if _DEBUG:
                @block.gpsimd
                def _(g, tt=tt):
                    g.wait_ge(sem["SV"], base_v + 10)
                    dma(g, "QG", dbg["xn"][tt * 128:(tt + 1) * 128, :], xn_f[:, :])
                    g.wait_ge(sem["SV"], cnt["SV"])
                    dma(g, "QG", dbg["xqb"][tt * 128:(tt + 1) * 128, :], xq_b[:, :])

            @block.tensor
            def _(t, tt=tt):
                t.wait_ge(sem["SV"], cnt["SV"])
                for kcc in range(KC):
                    pa, pb = ptrs[kcc % 2]
                    gi = tt * KC + kcc - 2
                    if gi >= 0 and gi in ev_s1evac:
                        t.wait_ge(sem["SS"], ev_s1evac[gi])
                    t.transpose(pa[:, :], xn_b[:, kcc * 128:(kcc + 1) * 128], idb[:, :])
                    inc(t.transpose(pb[:, :], xq_b[:, kcc * 128:(kcc + 1) * 128], idb[:, :]), "ST")

                    @block.scalar
                    def _(s, tt=tt, kcc=kcc, pa=pa, pb=pb):
                        s.wait_ge(sem["ST"], cnt["ST"])
                        inc(s.copy(xnT[:, kcc * POS + tt * 128: kcc * POS + (tt + 1) * 128],
                                   pa[:, :]), "SS")
                        inc(s.copy(xqT[:, kcc * POS + tt * 128: kcc * POS + (tt + 1) * 128],
                                   pb[:, :]), "SS")
                        ev_s1evac[tt * KC + kcc] = cnt["SS"]

        if _DEBUG:
            @block.gpsimd
            def _(g):
                g.wait_ge(sem["SS"], cnt["SS"])
                dma(g, "QG", dbg["xqt_e"], xqT[:, :])

        s1 = snap()

    es2 = ExitStack()
    _e2 = es2.enter_context
    wsl = [_e2(nc.sbuf_tensor("wslA", [128, DM], F32)),
           _e2(nc.sbuf_tensor("wslB", [128, DM], F32))]
    wslb = [_e2(nc.sbuf_tensor("wslbA", [128, DM], BF16)),
            _e2(nc.sbuf_tensor("wslbB", [128, DM], BF16))]
    xs_sb = [_e2(nc.sbuf_tensor("xs_sbA", [128, POS], F32)),
             _e2(nc.sbuf_tensor("xs_sbB", [128, POS], F32))]
    zt = [_e2(nc.sbuf_tensor("ztA", [128, POS], F32)),
          _e2(nc.sbuf_tensor("ztB", [128, POS], F32))]
    cv1 = [_e2(nc.sbuf_tensor("cv1A", [128, TOK], F32)),
           _e2(nc.sbuf_tensor("cv1B", [128, TOK], F32))]
    cv2 = [_e2(nc.sbuf_tensor("cv2A", [128, TOK], F32)),
           _e2(nc.sbuf_tensor("cv2B", [128, TOK], F32))]
    u_f = [_e2(nc.sbuf_tensor("u_fA", [128, TOK], F32)),
           _e2(nc.sbuf_tensor("u_fB", [128, TOK], F32))]
    cwc = [_e2(nc.sbuf_tensor("cwcA", [128, 4], F32)),
           _e2(nc.sbuf_tensor("cwcB", [128, 4], F32))]
    cbc = [_e2(nc.sbuf_tensor("cbcA", [128, 1], F32)),
           _e2(nc.sbuf_tensor("cbcB", [128, 1], F32))]
    wx_f = _e2(nc.sbuf_tensor("wx_f", [128, DCH * 160], F32))
    wdt_b = _e2(nc.sbuf_tensor("wdt_b", [128, DIN], BF16))
    dtr_b = _e2(nc.sbuf_tensor("dtr_b", [128, TOK], BF16))
    bc_sb = _e2(nc.sbuf_tensor("bc_sb", [32, TOK], F32))
    dt_sb = [_e2(nc.sbuf_tensor("dt_sbA", [128, TOK], F32)),
             _e2(nc.sbuf_tensor("dt_sbB", [128, TOK], F32))]
    et = _e2(nc.sbuf_tensor("et", [128, TOK], F32))
    bdtc = [_e2(nc.sbuf_tensor("bdtcA", [128, 1], F32)),
            _e2(nc.sbuf_tensor("bdtcB", [128, 1], F32))]
    uld = [_e2(nc.sbuf_tensor("uldA", [128, TOK], F32)),
           _e2(nc.sbuf_tensor("uldB", [128, TOK], F32))]
    dtu_sb = [_e2(nc.sbuf_tensor("dtu_sbA", [128, TOK], F32)),
              _e2(nc.sbuf_tensor("dtu_sbB", [128, TOK], F32))]
    pp = _e2(nc.psum_tensor("pp", [128, 512], F32))
    pq = _e2(nc.psum_tensor("pq", [128, 512], F32))
    pr = _e2(nc.psum_tensor("pr", [128, 512], F32))
    pd0 = _e2(nc.psum_tensor("pd0", [128, 512], F32))
    px1 = _e2(nc.psum_tensor("px1", [128, TOK], F32))
    px2 = _e2(nc.psum_tensor("px2", [32, TOK], F32))
    if True:
        @block.sync
        def _(s):
            wait_all(s, s1, skip=("QS",))
            dma(s, "QS", wx_f.ap().rearrange("p (c f) -> p c f", c=DCH),
                w_x_t.rearrange("(c p) f -> p c f", p=128))
            for k, stg in enumerate([uld[0], uld[1], cv1[0], cv1[1]]):
                dma(s, "QS", stg[:, :], w_dt_t[:, k * 1024:(k + 1) * 1024])

        for k, stg in enumerate([uld[0], uld[1], cv1[0], cv1[1]]):
            @block.vector
            def _(v, k=k, stg=stg):
                v.wait_ge(sem["QS"], cnt["QS"])
                inc(v.tensor_copy(wdt_b[:, k * 1024:(k + 1) * 1024], stg[:, :]), "SV")

        ev_cast = {}
        ev_mm = {}
        ev_evac = {}
        ev_conv = {}
        ev_ub = {}
        ev_zspill = {}
        ev_silu = {}

        NSLICES = [(0, 512), (512, 512), (1024, 128)]

        for c in range(2 * DCH):
            is_xs = c < DCH
            par = c % 2
            Q = "QS" if par == 0 else "QC"

            @block.sync if par == 0 else block.scalar
            def _(s, c=c, par=par, Q=Q, is_xs=is_xs):
                if c - 2 in ev_cast:
                    s.wait_ge(sem["SV"], ev_cast[c - 2])
                if is_xs and c - 2 >= 0:
                    s.wait_ge(sem["SV"], ev_conv[c - 2])
                dma(s, Q, wsl[par].ap().rearrange("p (kc ch) -> p kc ch", kc=KC),
                    w_in_4[:, c, :, :])
                if is_xs:
                    dma(s, Q, cwc[par][:, :], conv_w[c * 128:(c + 1) * 128, :])
                    dma(s, Q, cbc[par][:, :], conv_b[c * 128:(c + 1) * 128, :])

            q_slab = cnt[Q]

            @block.vector
            def _(v, c=c, par=par, Q=Q, q_slab=q_slab, is_xs=is_xs):
                v.wait_ge(sem[Q], q_slab)
                if c - 2 in ev_mm:
                    v.wait_ge(sem["ST"], ev_mm[c - 2])
                inc(v.tensor_copy(wslb[par][:, :], wsl[par][:, :]), "SV")
                ev_cast[c] = cnt["SV"]

            NPS = [pp, pq, pr]
            for isl, (n0, nw) in enumerate(NSLICES):
                ps = NPS[isl]

                @block.tensor
                def _(t, c=c, par=par, n0=n0, nw=nw, ps=ps, isl=isl):
                    t.wait_ge(sem["SV"], ev_cast[c])
                    if (c - 1, isl) in ev_evac:
                        k, v_ = ev_evac[(c - 1, isl)]
                        t.wait_ge(sem[k], v_)
                    last = None
                    for kcc in range(KC):
                        last = t.matmul(ps[:, 0:nw], wslb[par][:, kcc * 128:(kcc + 1) * 128],
                                        xnT[:, kcc * POS + n0: kcc * POS + n0 + nw],
                                        start=(kcc == 0), stop=(kcc == KC - 1))
                    inc(last, "ST")
                    if isl == len(NSLICES) - 1:
                        ev_mm[c] = cnt["ST"]

                if is_xs:
                    @block.scalar
                    def _(se, c=c, par=par, n0=n0, nw=nw, ps=ps, isl=isl):
                        se.wait_ge(sem["ST"], cnt["ST"])
                        if isl == 0 and c - 2 in ev_conv:
                            se.wait_ge(sem["SV"], ev_conv[c - 2])
                        ins = se.copy(xs_sb[par][:, n0:n0 + nw], ps[:, 0:nw])
                        if isl < 2:
                            inc(ins, "SS")
                        else:
                            sfence(se)
                        ev_evac[(c, isl)] = ("SS", cnt["SS"])
                else:
                    @block.scalar
                    def _(se, c=c, par=par, n0=n0, nw=nw, ps=ps, isl=isl):
                        se.wait_ge(sem["ST"], cnt["ST"])
                        if isl == 0 and c - 2 >= DCH:
                            se.wait_ge(sem["QG"], ev_zspill[c - 2])
                        inc(se.activation(zt[par][:, n0:n0 + nw], ps[:, 0:nw], ACTF.Silu,
                                          bias=0.0, scale=1.0), "SS")
                        ev_evac[(c, isl)] = ("SS", cnt["SS"])

            if is_xs:
                @block.vector
                def _(v, c=c, par=par):
                    v.wait_ge(sem["SS"], ev_evac[(c, 2)][1])
                    if c - 2 in ev_silu:
                        v.wait_ge(sem["SS"], ev_silu[c - 2])
                    inc(v.tensor_scalar(cv1[par][:, :], xs_sb[par][:, 0:TOK], cwc[par][:, 0:1],
                                        cbc[par][:, :], op0=ALU.mult, op1=ALU.add), "SV")
                    inc(v.scalar_tensor_tensor(cv2[par][:, :], xs_sb[par][:, 1:TOK + 1],
                                               cwc[par][:, 1:2],
                                               cv1[par][:, :], op0=ALU.mult, op1=ALU.add), "SV")
                    inc(v.scalar_tensor_tensor(cv1[par][:, :], xs_sb[par][:, 2:TOK + 2],
                                               cwc[par][:, 2:3],
                                               cv2[par][:, :], op0=ALU.mult, op1=ALU.add), "SV")
                    inc(v.scalar_tensor_tensor(cv2[par][:, :], xs_sb[par][:, 3:TOK + 3],
                                               cwc[par][:, 3:4],
                                               cv1[par][:, :], op0=ALU.mult, op1=ALU.add), "SV")
                    ev_conv[c] = cnt["SV"]

                @block.scalar
                def _(se, c=c, par=par):
                    se.wait_ge(sem["SV"], ev_conv[c])
                    if c - 2 in ev_ub:
                        se.wait_ge(sem["QG"], ev_ub[c - 2])
                    se.activation(u_f[par][:, :], cv2[par][:, :], ACTF.Silu, bias=0.0, scale=1.0)
                    sfence(se)
                    ev_silu[c] = cnt["SS"]

                @block.gpsimd
                def _(g, c=c, par=par):
                    g.wait_ge(sem["SS"], ev_silu[c])
                    dma(g, "QG", u_scr.ap()[c * 128:(c + 1) * 128, :], u_f[par][:, :])
                    ev_ub[c] = cnt["QG"]
            else:
                @block.gpsimd
                def _(g, c=c, par=par):
                    g.wait_ge(sem["SS"], ev_evac[(c, 2)][1])
                    dma(g, "QG", sz_scr.ap()[(c - DCH) * 128:(c - DCH + 1) * 128, :],
                        zt[par][:, :])
                    ev_zspill[c] = cnt["QG"]

        ev_uxld = {}
        ev_uxmm = {}
        for c in range(DCH):
            upar = c % 2

            @block.sync
            def _(s, c=c, upar=upar):
                if c - 2 in ev_uxmm:
                    s.wait_ge(sem["ST"], ev_uxmm[c - 2])
                s.wait_ge(sem["QG"], ev_ub[c])
                dma(s, "QS", uld[upar][:, :], u_scr.ap()[c * 128:(c + 1) * 128, :])
            ev_uxld[c] = cnt["QS"]

            @block.tensor
            def _(t, c=c, upar=upar):
                t.wait_ge(sem["QS"], ev_uxld[c])
                for nh in range(2):
                    sl = slice(nh * 512, (nh + 1) * 512)
                    t.matmul(px1[:, sl], wx_f[:, c * 160:c * 160 + 128],
                             uld[upar][:, sl],
                             start=(c == 0), stop=(c == DCH - 1), skip_group_check=True)
                    ins = t.matmul(px2[:, sl], wx_f[:, c * 160 + 128:c * 160 + 160],
                                   uld[upar][:, sl],
                                   start=(c == 0), stop=(c == DCH - 1), skip_group_check=True)
                inc(ins, "ST")
            ev_uxmm[c] = cnt["ST"]

        @block.vector
        def _(v):
            v.wait_ge(sem["ST"], cnt["ST"])
            inc(v.tensor_copy(dtr_b[:, :], px1[:, :]), "SV")
            inc(v.tensor_copy(bc_sb[:, :], px2[:, :]), "SV")

        if _DEBUG:
            @block.gpsimd
            def _(g):
                g.wait_ge(sem["SV"], cnt["SV"])
                dma(g, "QG", dbg["xdbl"][128:160, :], bc_sb[:, :])

        @block.gpsimd
        def _(g):
            g.wait_ge(sem["SV"], cnt["SV"])
            for j in range(NCORES):
                dma(g, "QG", a2a_in.ap()[j, 2 * CPC:2 * CPC + 2 * DST, :], bc_sb[:, :])

        ss_dt0 = cnt["SS"]
        sv_dt0 = cnt["SV"]
        qg_dt0 = cnt["QG"]
        ev_dtmm, ev_dtexp, ev_dtu, ev_dtq, ev_dtg = {}, {}, {}, {}, {}
        for c in range(DCH):
            par = c % 2
            pda, pdb = (pp, pq) if par == 0 else (pr, pd0)

            @block.sync
            def _(s, c=c, par=par):
                if c - 2 in ev_dtu:
                    s.wait_ge(sem["SV"], ev_dtu[c - 2])
                elif c < 2:
                    s.wait_ge(sem["ST"], ev_uxmm[DCH - 2 + c])
                if c - 2 in ev_dtexp:
                    s.wait_ge(sem["SS"], ev_dtexp[c - 2])
                if c == 0:
                    s.wait_ge(sem["QG"], qg_dt0)
                dma(s, "QS", bdtc[par][:, :], b_dt[c * 128:(c + 1) * 128, :])
                dma(s, "QS", uld[par][:, :], u_scr.ap()[c * 128:(c + 1) * 128, :])
            ev_dtq[c] = cnt["QS"]

            @block.tensor
            def _(t, c=c, par=par, pda=pda, pdb=pdb):
                if c == 0:
                    t.wait_ge(sem["SV"], cnt["SV"])
                if c < 2:
                    t.wait_ge(sem["SS"], ss_dt0)
                elif c - 2 in ev_dtexp:
                    t.wait_ge(sem["SS"], ev_dtexp[c - 2])
                t.matmul(pda[:, 0:512], wdt_b[:, c * 128:(c + 1) * 128], dtr_b[:, 0:512],
                         start=True, stop=True, skip_group_check=True)
                inc(t.matmul(pdb[:, 0:512], wdt_b[:, c * 128:(c + 1) * 128], dtr_b[:, 512:1024],
                             start=True, stop=True, skip_group_check=True), "ST")
            ev_dtmm[c] = cnt["ST"]

            @block.scalar
            def _(s, c=c, par=par, pda=pda, pdb=pdb):
                s.wait_ge(sem["ST"], ev_dtmm[c])
                s.wait_ge(sem["QS"], ev_dtq[c])
                if c - 2 in ev_dtg:
                    s.wait_ge(sem["QG"], ev_dtg[c - 2])
                if c - 2 in ev_dtu:
                    s.wait_ge(sem["SV"], ev_dtu[c - 2])
                inc(s.activation(et[:, 0:512], pda[:, 0:512], ACTF.Exp,
                                 bias=bdtc[par][:, :], scale=1.0), "SS")
                inc(s.activation(et[:, 512:1024], pdb[:, 0:512], ACTF.Exp,
                                 bias=bdtc[par][:, :], scale=1.0), "SS")
                s.activation(dt_sb[par][:, :], et[:, :], ACTF.Ln, bias=1.0, scale=1.0)
                sfence(s)
            ev_dtexp[c] = cnt["SS"]

            @block.vector
            def _(v, c=c, par=par):
                v.wait_ge(sem["SS"], ev_dtexp[c])
                v.wait_ge(sem["QS"], ev_dtq[c])
                if c - 2 in ev_dtg:
                    v.wait_ge(sem["QG"], ev_dtg[c - 2])
                inc(v.tensor_tensor(dtu_sb[par][:, :], dt_sb[par][:, :], uld[par][:, :],
                                    op=ALU.mult), "SV")
            ev_dtu[c] = cnt["SV"]

            @block.gpsimd
            def _(g, c=c, par=par):
                j, r = c // BL, (c % BL) * 128
                g.wait_ge(sem["SS"], ev_dtexp[c])
                dma(g, "QG", a2a_in.ap()[j, r:r + 128, :], dt_sb[par][:, :])
                g.wait_ge(sem["SV"], ev_dtu[c])
                dma(g, "QG", a2a_in.ap()[j, CPC + r:CPC + r + 128, :], dtu_sb[par][:, :])
                if _DEBUG:
                    dma(g, "QG", dbg["dt"][c * 128:(c + 1) * 128, :], dt_sb[par][:, :])
            ev_dtg[c] = cnt["QG"]

        s2 = snap()

    es2.close()
    es1.close()


    with (
        nc.sbuf_tensor("kqb", [128, DM], BF16) as kqb,
        nc.sbuf_tensor("kq", [128, KC * DM], BF16) as kq,
        nc.sbuf_tensor("bq_rep", [128, DM], F32) as bq_rep,
        nc.sbuf_tensor("scr8", [128, DM], F32) as scr8,
        nc.sbuf_tensor("kab", [128, DM], F32) as kab,
        nc.sbuf_tensor("k0", [128, 1], F32) as k0,
        nc.sbuf_tensor("k1", [128, 1], F32) as k1,
        nc.psum_tensor("ptk", [128, 128], BF16) as ptk,
        nc.psum_tensor("ptk2", [128, 128], BF16) as ptk2,
    ):
        kqc = scr8
        with (
            nc.sbuf_tensor("bqr", [1, DM], F32) as bqr,
            nc.sbuf_tensor("b0", [1, 1], F32) as b0,
            nc.sbuf_tensor("b1", [1, 1], F32) as b1,
        ):
            @block.sync
            def _(s):
                wait_all(s, s2, skip=("QS", "QC"))
                dma(s, "QS", bqr[:, :], ffn_b)

            @block.vector
            def _(v):
                v.wait_ge(sem["QS"], cnt["QS"])
                inc(v.tensor_scalar(bq_rep[0:1, :], bqr[:, :], -1.0, None, op0=ALU.mult), "SV")
                inc(v.tensor_tensor(bq_rep[0:1, :], bq_rep[0:1, :], bqr[:, :], op=ALU.max), "SV")
                inc(v.reduce_max(b0[:, :], bq_rep[0:1, :], axis=AX.X), "SV")
                inc(v.tensor_copy(kab[0:1, 0:256], bqr[:, 0:256]), "SV")
                inc(v.tensor_scalar(b0[:, :], b0[:, :], EPS_AMAX, None, op0=ALU.max), "SV")
                inc(v.tensor_copy(kab[0:1, 256:512], bqr[:, 0:256]), "SV")
                inc(v.reciprocal(b1[:, :], b0[:, :]), "SV")
                inc(v.tensor_copy(kab[0:1, 512:768], bqr[:, 0:256]), "SV")
                inc(v.tensor_scalar(bqr[:, :], bqr[:, :], b1[:, :], None, op0=ALU.mult), "SV")
                inc(v.tensor_scalar(bqr[:, :], bqr[:, :], 127.0, MAGIC,
                                    op0=ALU.mult, op1=ALU.add), "SV")
                inc(v.tensor_scalar(bqr[:, :], bqr[:, :], MAGIC + 127.0, MAGIC - 127.0,
                                    op0=ALU.min, op1=ALU.max), "SV")
                inc(v.tensor_scalar(bqr[:, :], bqr[:, :], MAGIC, b0[:, :],
                                    op0=ALU.subtract, op1=ALU.mult), "SV")
                inc(v.tensor_scalar(bqr[:, :], bqr[:, :], 1.0 / 127.0, None, op0=ALU.mult), "SV")

            @block.gpsimd
            def _(g):
                g.wait_ge(sem["SV"], cnt["SV"])
                dma(g, "QG", bq_scr.ap(), bqr[:, :])

            @block.sync
            def _(s):
                s.wait_ge(sem["QG"], cnt["QG"])
                dma(s, "QS", bq_rep[:, :], bq_scr.ap().broadcast_to([128, DM]))

        for fc in range(KC):
            base_v, base_t, base_s = cnt["SV"], cnt["ST"], cnt["SS"]

            @block.sync
            def _(s, fc=fc, base_v=base_v):
                s.wait_ge(sem["SV"], base_v)
                dma(s, "QS", kqc[:, :], ffn_k_t[fc * 128:(fc + 1) * 128, :])

            @block.vector
            def _(v, fc=fc):
                v.wait_ge(sem["QS"], cnt["QS"])
                inc(v.tensor_scalar(kab[:, :], kqc[:, :], -1.0, None, op0=ALU.mult), "SV")
                inc(v.tensor_tensor(kab[:, :], kab[:, :], kqc[:, :], op=ALU.max), "SV")
                inc(v.reduce_max(k0[:, :], kab[:, :], axis=AX.X), "SV")
                inc(v.tensor_copy(kab[:, 0:256], kqc[:, 0:256]), "SV")
                inc(v.tensor_scalar(k0[:, :], k0[:, :], EPS_AMAX, None, op0=ALU.max), "SV")
                inc(v.tensor_copy(kab[:, 256:512], kqc[:, 0:256]), "SV")
                inc(v.reciprocal(k1[:, :], k0[:, :]), "SV")
                inc(v.tensor_copy(kab[:, 512:768], kqc[:, 0:256]), "SV")
                inc(v.tensor_scalar(kqc[:, :], kqc[:, :], k1[:, :], None, op0=ALU.mult), "SV")
                inc(v.tensor_scalar(kqc[:, :], kqc[:, :], 127.0, MAGIC,
                                    op0=ALU.mult, op1=ALU.add), "SV")
                inc(v.tensor_scalar(kqc[:, :], kqc[:, :], MAGIC + 127.0, MAGIC - 127.0,
                                    op0=ALU.min, op1=ALU.max), "SV")
                inc(v.tensor_scalar(kqc[:, :], kqc[:, :], MAGIC, k0[:, :],
                                    op0=ALU.subtract, op1=ALU.mult), "SV")
                inc(v.tensor_scalar(kqb[:, :], kqc[:, :], 1.0 / 127.0, None, op0=ALU.mult), "SV")

            @block.tensor
            def _(t, fc=fc):
                t.wait_ge(sem["SV"], cnt["SV"])
                for dc in range(KC):
                    if dc > 0:
                        t.wait_ge(sem["SS"], cnt["SS"])
                    if dc % 2 == 0:
                        inc(t.transpose(ptk[:, :], kqb[:, dc * 128:(dc + 1) * 128], idb[:, :]), "ST")
                    else:
                        inc(t.transpose(ptk2[:, :], kqb[:, dc * 128:(dc + 1) * 128], idb[:, :]), "ST")

                    @block.scalar
                    def _(s, fc=fc, dc=dc):
                        s.wait_ge(sem["ST"], cnt["ST"])
                        src = ptk if dc % 2 == 0 else ptk2
                        inc(s.copy(kq[:, dc * DM + fc * 128: dc * DM + (fc + 1) * 128],
                                   src[:, :]), "SS")
                        if fc == KC - 1 and dc == KC - 1:
                            sfence(s)

        if _DEBUG:
            @block.gpsimd
            def _(g):
                g.wait_ge(sem["SS"], cnt["SS"])
                dma(g, "QG", dbg["xqt"], xqT[:, :])
                dma(g, "QG", dbg["kq"], kq[:, :])

        ev_ospill = {}
        with (
            nc.sbuf_tensor("xt6", [128, DM], F32) as xt6,
            nc.sbuf_tensor("obufA", [128, DM], F32) as obufA,
            nc.sbuf_tensor("obufB", [128, DM], F32) as obufB,
            nc.sbuf_tensor("f0", [128, 1], F32) as f0c,
            nc.sbuf_tensor("f1c", [128, 1], F32) as f1c,
            nc.psum_tensor("pf", [128, DM], F32) as pf,
        ):
            obuf = [obufA, obufB]
            f1 = scr8
            for tt in range(8):
                base_v, base_t, base_q = cnt["SV"], cnt["ST"], cnt["QS"]

                @block.sync
                def _(s, tt=tt, base_v=base_v):
                    s.wait_ge(sem["SV"], base_v)
                    dma(s, "QS", xt6[:, :], x_in[HALO + tt * 128: HALO + (tt + 1) * 128, :])

                @block.tensor
                def _(t, tt=tt, base_v=base_v):
                    if tt == 0:
                        t.wait_ge(sem["SS"], cnt["SS"])
                    else:
                        t.wait_ge(sem["SV"], base_v)
                    last = None
                    for nh in range(4):
                        for dc in range(KC):
                            last = t.matmul(pf[:, nh * 512:(nh + 1) * 512],
                                            xqT[:, dc * POS + HALO + tt * 128:
                                                dc * POS + HALO + (tt + 1) * 128],
                                            kq[:, dc * DM + nh * 512: dc * DM + (nh + 1) * 512],
                                            start=(dc == 0), stop=(dc == KC - 1),
                                            skip_group_check=True)
                    inc(last, "ST")

                @block.vector
                def _(v, tt=tt):
                    v.wait_ge(sem["ST"], cnt["ST"])
                    v.wait_ge(sem["QS"], cnt["QS"])
                    inc(v.tensor_tensor(f1[:, :], pf[:, :], bq_rep[:, :], op=ALU.add), "SV")
                    inc(v.tensor_scalar(f1[:, :], f1[:, :], 0.0, None, op0=ALU.max), "SV")
                    inc(v.reduce_max(f0c[:, :], f1[:, :], axis=AX.X), "SV")
                    inc(v.tensor_copy(kab[:, 0:256], f1[:, 0:256]), "SV")
                    inc(v.tensor_scalar(f0c[:, :], f0c[:, :], EPS_AMAX, None, op0=ALU.max), "SV")

                @block.vector
                def _(v, tt=tt):
                    inc(v.tensor_copy(kab[:, 256:512], f1[:, 0:256]), "SV")
                    inc(v.reciprocal(f1c[:, :], f0c[:, :]), "SV")
                    inc(v.tensor_copy(kab[:, 512:768], f1[:, 0:256]), "SV")
                    inc(v.tensor_scalar(f1[:, :], f1[:, :], f1c[:, :], None, op0=ALU.mult), "SV")
                    inc(v.tensor_scalar(f1[:, :], f1[:, :], 127.0, MAGIC,
                                        op0=ALU.mult, op1=ALU.add), "SV")
                    inc(v.tensor_scalar(f1[:, :], f1[:, :], MAGIC + 127.0, MAGIC - 127.0,
                                        op0=ALU.min, op1=ALU.max), "SV")
                    inc(v.tensor_scalar(f1[:, :], f1[:, :], MAGIC, f0c[:, :],
                                        op0=ALU.subtract, op1=ALU.mult), "SV")
                    inc(v.tensor_scalar(f1[:, :], f1[:, :], 1.0 / 127.0, None, op0=ALU.mult), "SV")
                    if tt >= 2:
                        v.wait_ge(sem["QG"], ev_ospill[tt - 2])
                    inc(v.tensor_tensor(obuf[tt % 2][:, :], xt6[:, :], f1[:, :], op=ALU.add), "SV")

                @block.gpsimd
                def _(g, tt=tt):
                    g.wait_ge(sem["SV"], cnt["SV"])
                    dma(g, "QG", o_scr.ap()[tt * 128:(tt + 1) * 128, :], obuf[tt % 2][:, :])
                    ev_ospill[tt] = cnt["QG"]

                if _DEBUG:
                    @block.gpsimd
                    def _(g, tt=tt):
                        g.wait_ge(sem["SV"], cnt["SV"])
                        dma(g, "QG", dbg["ffnq"][tt * 128:(tt + 1) * 128, :], f1[:, :])

        s6b = snap()

    es_xq.close()

    @block.gpsimd
    def _(g):
        wait_all(g, s2)
        with allow_ici_all_to_all():
            inc(g.collective_compute(
                "AllToAll", ALU.bypass,
                replica_groups=[list(range(NCORES))],
                ins=[a2a_in.ap().opt()],
                outs=[a2a_out.ap().opt()],
            ), "SG")

    s3 = snap()

    with ExitStack() as es4:
        dtq = es4.enter_context(nc.sbuf_tensor("dtq", [128, BL * TQW], F32))
        dtldA = es4.enter_context(nc.sbuf_tensor("dtldA", [128, TQW], F32))
        dtldB = es4.enter_context(nc.sbuf_tensor("dtldB", [128, TQW], F32))
        dtuT = es4.enter_context(nc.sbuf_tensor("dtuT", [128, BL * 8 * 128], BF16))
        bcolq = es4.enter_context(nc.sbuf_tensor("bcolq", [128, 128], F32))
        ball = es4.enter_context(nc.sbuf_tensor("ball", [128, TQW], F32))
        dgqA = es4.enter_context(nc.sbuf_tensor("dgqA", [128, 8 * 128], BF16))
        dgqB = es4.enter_context(nc.sbuf_tensor("dgqB", [128, 8 * 128], BF16))
        crep = es4.enter_context(nc.sbuf_tensor("crep", [128, 2 * TQW], F32))
        da = es4.enter_context(nc.sbuf_tensor("da", [128, 2 * 2 * TQW], F32))
        hb = es4.enter_context(nc.sbuf_tensor("hb", [128, 2 * TQW], F32))
        chb0 = es4.enter_context(nc.sbuf_tensor("chb0", [128, TQW], F32))
        chb1 = es4.enter_context(nc.sbuf_tensor("chb1", [128, TQW], F32))
        chb2 = es4.enter_context(nc.sbuf_tensor("chb2", [128, TQW], F32))
        chb3 = es4.enter_context(nc.sbuf_tensor("chb3", [128, TQW], F32))
        yq = es4.enter_context(nc.sbuf_tensor("yq", [128, BL * TQW], F32))
        chbs = [chb0, chb1, chb2, chb3]
        dtld = [dtldA, dtldB]
        dgq = [dgqA, dgqB]
        TE_Y = (0, 1)
        ev_tts = {}
        ev_ch = {}
        ev_ymm = {}
        ev_vy = {}
        ev_dbu = {}
        ev_dg = {}
        ev_evac = {}
        ev_tr = {}
        qg_y = {}

        py0 = es4.enter_context(nc.psum_tensor("py0", [128, TQW], F32))
        py1 = es4.enter_context(nc.psum_tensor("py1", [128, TQW], F32))
        pb0 = es4.enter_context(nc.psum_tensor("pb0", [128, TQW], F32))
        pb1 = es4.enter_context(nc.psum_tensor("pb1", [128, TQW], F32))
        if True:
            pys = {0: py0, 1: py1}
            pdbu = [pb0, pb1]
            ev_bcol = {}

            @block.gpsimd
            def _(g):
                inc(g.memset(ball[:, :], 0.0), "SG")
            sg_ball = cnt["SG"]

            for tq in range(TQ):
                @block.sync
                def _(s, tq=tq):
                    if tq == 0:
                        wait_all(s, s3, skip=("QS", "QC"))
                        s.wait_ge(sem["SG"], sg_ball)
                    else:
                        s.wait_ge(sem["SS"], cnt["SS"])
                        s.wait_ge(sem["SV"], ev_tts[(tq - 1, DST - 1, BL - 1)])
                    for b in range(BL):
                        dma(s, "QS", dtq[:, b * TQW:(b + 1) * TQW],
                            a2a_out.ap()[tq, b * 128:(b + 1) * 128, :])
                    dma(s, "QS", ball[0:DST, :], a2a_out.ap()[tq, 2 * CPC:2 * CPC + DST, :])

                qs_dtq = cnt["QS"]

                @block.tensor
                def _(t, tq=tq):
                    t.wait_ge(sem["QS"], qs_dtq)
                    if tq == 0:
                        t.wait_ge(sem["SG"], cnt["SG"])
                    else:
                        t.wait_ge(sem["SV"], ev_tts[(tq - 1, DST - 1, BL - 1)])
                    for tt in range(8):
                        ins = t.transpose(pdbu[1][:, tt * 128:(tt + 1) * 128],
                                          ball[:, tt * 128:(tt + 1) * 128], idf32[:, :])
                    inc(ins, "ST")
                ev_btr = cnt["ST"]

                @block.scalar
                def _(se, tq=tq, ev_btr=ev_btr):
                    se.wait_ge(sem["ST"], ev_btr)
                    for tt in range(8):
                        se.copy(bcolq[:, tt * DST:(tt + 1) * DST],
                                pdbu[1][:, tt * 128:tt * 128 + DST])
                    sfence(se)
                    ev_bcol[tq] = cnt["SS"]

                for b in range(BL):
                    @block.sync
                    def _(s, tq=tq, b=b):
                        if (tq, b - 2) in ev_tr:
                            s.wait_ge(sem["SS"], ev_tr[(tq, b - 2)])
                        elif tq > 0 and b < 2:
                            s.wait_ge(sem["SS"], ev_tr[(tq - 1, 2 + b)])
                        dma(s, "QS", dtld[b % 2][:, :],
                            a2a_out.ap()[tq, CPC + b * 128:CPC + (b + 1) * 128, :])
                    qs_ld = cnt["QS"]

                    @block.tensor
                    def _(t, tq=tq, b=b, qs_ld=qs_ld):
                        t.wait_ge(sem["QS"], qs_ld)
                        if tq == 0 and b == 0:
                            pass
                        else:
                            pk = (tq, b - 1) if b >= 1 else (tq - 1, BL - 1)
                            if pk in ev_tr:
                                t.wait_ge(sem["SS"], ev_tr[pk])
                        if tq > 0 and b < 2:
                            t.wait_ge(sem["SV"], ev_tts[(tq - 1, DST - 1, BL - 1)])
                        for tt in range(8):
                            inc(t.transpose(pdbu[b % 2][:, tt * 128:(tt + 1) * 128],
                                            dtld[b % 2][:, tt * 128:(tt + 1) * 128],
                                            idf32[:, :]), "ST")

                    @block.scalar
                    def _(se, tq=tq, b=b):
                        se.wait_ge(sem["ST"], cnt["ST"])
                        se.copy(dtuT[:, b * TQW:(b + 1) * TQW], pdbu[b % 2][:, :])
                        sfence(se)
                        ev_tr[(tq, b)] = cnt["SS"]

                for si in range(DST):
                    spar = si % 2

                    @block.sync
                    def _(s, tq=tq, si=si, spar=spar):
                        ptq, psi = (tq, si - 2) if si >= 2 else (tq - 1, DST - 2 + si)
                        if ptq >= 0:
                            s.wait_ge(sem["SV"], ev_ch[(ptq, psi, BL - 1)])
                        dma(s, "QS", crep[:, spar * TQW:(spar + 1) * TQW],
                            a2a_out.ap()[tq, 2 * CPC + DST + si:2 * CPC + DST + si + 1, :]
                            .broadcast_to([128, TQW]))

                    qs_bc = cnt["QS"]

                    @block.scalar
                    def _(se, tq=tq, si=si, spar=spar):
                        ptq, psi = (tq, si - 2) if si >= 2 else (tq - 1, DST - 2 + si)
                        if ptq >= 0:
                            se.wait_ge(sem["ST"], ev_dbu[(ptq, psi, BL - 1)])
                        for tt in range(8):
                            se.activation(dgq[spar][:, tt * 128:(tt + 1) * 128], idb[:, :],
                                          ACTF.Copy, bias=0.0,
                                          scale=bcolq[:, tt * DST + si:tt * DST + si + 1])
                        sfence(se)
                        ev_dg[(tq, si)] = cnt["SS"]

                    if _DEBUG and tq == 0 and si == 0:
                        @block.sync
                        def _(s):
                            s.wait_ge(sem["SS"], ev_dg[(0, 0)])
                            s.wait_ge(sem["SS"], ev_tr[(0, BL - 1)])
                            dma(s, "QS", dbg["bcolq"], bcolq[:, :])
                            dma(s, "QS", dbg["dgq0"], dgq[0][:, :])
                            dma(s, "QS", dbg["dtut"], dtuT[:, :])

                    for pid in range(2):
                        q = si * 2 + pid
                        ppar = q % 2

                        @block.scalar
                        def _(se, tq=tq, si=si, pid=pid, q=q, ppar=ppar):
                            ptq, q2 = (tq, q - 2) if q >= 2 else (tq - 1, 2 * DST - 2 + q)
                            if ptq >= 0:
                                se.wait_ge(sem["SV"], ev_tts[(ptq, q2 // 2, 2 * (q2 % 2) + 1)])
                            for bi in range(2):
                                b = 2 * pid + bi
                                ins = se.activation(
                                    da[:, (ppar * 2 + bi) * TQW:(ppar * 2 + bi + 1) * TQW],
                                    dtq[:, b * TQW:(b + 1) * TQW], ACTF.Exp, bias=0.0,
                                    scale=a_sb[:, b * DST + si:b * DST + si + 1])
                                if bi == 0:
                                    inc(ins, "SS")
                            sfence(se)

                        @block.tensor
                        def _(t, tq=tq, si=si, pid=pid):
                            t.wait_ge(sem["SS"], ev_dg[(tq, si)])
                            for bi in range(2):
                                b = 2 * pid + bi
                                it = si * BL + b
                                pit = it - 2
                                if pit >= 0:
                                    t.wait_ge(sem["SV"], ev_tts[(tq, pit // BL, pit % BL)])
                                else:
                                    t.wait_ge(sem["SS"], ev_tr[(tq, BL - 1)])
                                last = None
                                for tt in range(8):
                                    last = t.matmul(
                                        pdbu[it % 2][:, tt * 128:(tt + 1) * 128],
                                        dtuT[:, b * TQW + tt * 128: b * TQW + (tt + 1) * 128],
                                        dgq[si % 2][:, tt * 128:(tt + 1) * 128],
                                        start=True, stop=True, skip_group_check=True)
                                inc(last, "ST")
                                ev_dbu[(tq, si, b)] = cnt["ST"]

                        @block.vector
                        def _(v, tq=tq, si=si, pid=pid, q=q, ppar=ppar, spar=spar):
                            v.wait_ge(sem["SS"], cnt["SS"])
                            if pid == 0:
                                v.wait_ge(sem["QS"], qs_bc)
                            for bi in range(2):
                                b = 2 * pid + bi
                                it = si * BL + b
                                col = b * DST + si
                                v.wait_ge(sem["ST"], ev_dbu[(tq, si, b)])
                                if it == 0 and tq > 0:
                                    lcol = (BL - 1) * DST + (DST - 1)
                                    inc(v.tensor_scalar(carry[:, lcol:lcol + 1],
                                                        hb[:, TQW + TQW - 1:2 * TQW],
                                                        1.0, None, op0=ALU.mult), "SV")
                                inc(v.tensor_tensor_scan(
                                    hb[:, (it % 2) * TQW:(it % 2 + 1) * TQW],
                                    da[:, (ppar * 2 + bi) * TQW:(ppar * 2 + bi + 1) * TQW],
                                    pdbu[it % 2][:, :], carry[:, col:col + 1],
                                    op0=ALU.mult, op1=ALU.add), "SV")
                                ev_tts[(tq, si, b)] = cnt["SV"]
                                pit = it - 1
                                if pit >= 0 and tq < TQ - 1:
                                    psi2, pb2 = pit // BL, pit % BL
                                    pcol = pb2 * DST + psi2
                                    inc(v.tensor_scalar(carry[:, pcol:pcol + 1],
                                                        hb[:, (pit % 2) * TQW + TQW - 1:
                                                           (pit % 2) * TQW + TQW],
                                                        1.0, None, op0=ALU.mult), "SV")
                                if b in TE_Y:
                                    pk = (tq, si - 1, b)
                                    if pk in ev_ymm:
                                        v.wait_ge(sem["ST"], ev_ymm[pk])
                                    elif tq > 0:
                                        v.wait_ge(sem["ST"], ev_ymm[(tq - 1, DST - 1, b)])
                                cb = chbs[b]
                                inc(v.tensor_tensor(cb[:, :],
                                                    hb[:, (it % 2) * TQW:(it % 2 + 1) * TQW],
                                                    crep[:, spar * TQW:(spar + 1) * TQW],
                                                    op=ALU.mult), "SV")
                                ev_ch[(tq, si, b)] = cnt["SV"]
                                if b not in TE_Y:
                                    if si == 0 and tq > 0 and b == 2:
                                        v.wait_ge(sem["QG"], qg_y[tq - 1])
                                    if si == 0:
                                        inc(v.tensor_copy(yq[:, b * TQW:(b + 1) * TQW],
                                                          cb[:, :]), "SV")
                                    else:
                                        inc(v.tensor_tensor(yq[:, b * TQW:(b + 1) * TQW],
                                                            yq[:, b * TQW:(b + 1) * TQW],
                                                            cb[:, :], op=ALU.add), "SV")
                                    ev_vy[(tq, si, b)] = cnt["SV"]

                        @block.tensor
                        def _(t, tq=tq, si=si, pid=pid):
                            for bi in range(2):
                                b = 2 * pid + bi
                                if b not in TE_Y:
                                    continue
                                t.wait_ge(sem["SV"], ev_ch[(tq, si, b)])
                                if si == 0 and tq > 0:
                                    t.wait_ge(sem["SS"], ev_evac[tq - 1])
                                cb = chbs[b]
                                t.matmul(pys[b][:, 0:512], idf32[:, :], cb[:, 0:512],
                                         start=(si == 0), stop=(si == DST - 1),
                                         skip_group_check=True)
                                inc(t.matmul(pys[b][:, 512:TQW], idf32[:, :], cb[:, 512:TQW],
                                             start=(si == 0), stop=(si == DST - 1),
                                             skip_group_check=True), "ST")
                                ev_ymm[(tq, si, b)] = cnt["ST"]

                @block.scalar
                def _(s, tq=tq):
                    s.wait_ge(sem["ST"], ev_ymm[(tq, DST - 1, 1)])
                    if tq > 0:
                        s.wait_ge(sem["QG"], qg_y[tq - 1])
                    for b in TE_Y:
                        ins = s.copy(yq[:, b * TQW:(b + 1) * TQW], pys[b][:, :])
                        if b == 0:
                            inc(ins, "SS")
                    sfence(s)
                    ev_evac[tq] = cnt["SS"]

                @block.gpsimd
                def _(g, tq=tq):
                    g.wait_ge(sem["SS"], ev_evac[tq])
                    for b in range(BL):
                        if b not in TE_Y:
                            g.wait_ge(sem["SV"], ev_vy[(tq, DST - 1, b)])
                    for b in range(BL):
                        dma(g, "QG", y_in.ap()[tq, b * 128:(b + 1) * 128, :],
                            yq[:, b * TQW:(b + 1) * TQW])
                qg_y[tq] = cnt["QG"]

    s4 = snap()

    @block.gpsimd
    def _(g):
        wait_all(g, s4)
        with allow_ici_all_to_all():
            inc(g.collective_compute(
                "AllToAll", ALU.bypass,
                replica_groups=[list(range(NCORES))],
                ins=[y_in.ap().opt()],
                outs=[y_out.ap().opt()],
            ), "SG")

    s5 = snap()


    with (
        nc.sbuf_tensor("out_acc", [128, 8 * DM], F32) as out_acc,
        nc.sbuf_tensor("yg", [128, DCH * TOK], BF16) as yg,
        nc.sbuf_tensor("ycl", [128, TOK], F32) as ycl,
        nc.sbuf_tensor("ucl", [128, TOK], F32) as ucl,
        nc.sbuf_tensor("szl", [128, TOK], F32) as szl,
        nc.sbuf_tensor("gt", [128, TOK], F32) as gt,
        nc.sbuf_tensor("dcl", [128, 1], F32) as dcl,
        nc.sbuf_tensor("wovA", [128, 256], F32) as wovA,
        nc.sbuf_tensor("wovB", [128, 256], F32) as wovB,
        nc.sbuf_tensor("wobA", [128, 256], BF16) as wobA,
        nc.sbuf_tensor("wobB", [128, 256], BF16) as wobB,
        nc.sbuf_tensor("mst", [128, 2 * TOK], F32) as mst,
        nc.psum_tensor("pw0", [128, TOK], F32) as pw0,
        nc.psum_tensor("pw1", [128, TOK], F32) as pw1,
        nc.psum_tensor("ptm", [128, 128], F32) as ptm,
        nc.psum_tensor("ptm2", [128, 128], F32) as ptm2,
        nc.sbuf_tensor("idf", [128, 128], F32) as idf,
    ):
        pws = [pw0, pw1]

        @block.vector
        def _(v):
            inc(v.tensor_copy(idf[:, :], idb[:, :]), "SV")

        @block.sync
        def _(s):
            wait_all(s, s5, skip=("QS", "QC"))
            for tt in range(8):
                dma(s, "QS", out_acc[:, tt * DM:(tt + 1) * DM],
                    o_scr.ap()[tt * 128:(tt + 1) * 128, :])
        qs_oacc = cnt["QS"]

        @block.vector
        def _(v):
            v.wait_ge(sem["QS"], qs_oacc)
            inc(v.engine_nop(), "SV")
        sv_oacc = cnt["SV"]

        for c in range(DCH):
            base_v, base_q, base_c = cnt["SV"], cnt["QS"], cnt["QC"]

            @block.sync
            def _(s, c=c, base_v=base_v):
                if c == 0:
                    wait_all(s, s5, skip=("QS", "QC", "QG"))
                else:
                    s.wait_ge(sem["SV"], base_v)
                j, r = c // BL, (c % BL) * 128
                dma(s, "QS", ycl[:, :], y_out.ap()[j, r:r + 128, :])
                dma(s, "QS", ucl[:, :], u_scr.ap()[c * 128:(c + 1) * 128, :])
                dma(s, "QS", szl[:, :], sz_scr.ap()[c * 128:(c + 1) * 128, HALO:HALO + TOK])
                dma(s, "QS", dcl[:, :], d_vec[c * 128:(c + 1) * 128, :])

            @block.vector
            def _(v, c=c):
                v.wait_ge(sem["QS"], cnt["QS"])
                inc(v.scalar_tensor_tensor(gt[:, :], ucl[:, :], dcl[:, :], ycl[:, :],
                                           op0=ALU.mult, op1=ALU.add), "SV")
                inc(v.tensor_tensor(yg[:, c * TOK:(c + 1) * TOK], gt[:, :], szl[:, :],
                                    op=ALU.mult), "SV")

            if _DEBUG:
                @block.gpsimd
                def _(g, c=c):
                    g.wait_ge(sem["SV"], cnt["SV"] - 1)
                    dma(g, "QG", dbg["yown"][c * 128:(c + 1) * 128, :], gt[:, :])

        wovs = [wovA, wovB]
        wobs = [wobA, wobB]
        ev_wld = {}
        ev_wcast = {}
        ev_wmm = {}
        ev_wevac = {}
        ev_wadd = {}
        ev_wtr = {}
        for mq in range(8):
            for dc in range(DCH):
                gi = mq * DCH + dc
                wpar = gi % 2
                Q = "QS" if wpar == 0 else "QC"

                @block.sync if wpar == 0 else block.scalar
                def _(s, mq=mq, dc=dc, gi=gi, wpar=wpar, Q=Q):
                    if gi - 2 in ev_wcast:
                        s.wait_ge(sem["SV"], ev_wcast[gi - 2])
                    dma(s, Q, wovs[wpar][:, :],
                        w_out_t[dc * 128:(dc + 1) * 128, mq * 256:(mq + 1) * 256])
                ev_wld[gi] = cnt["QS" if wpar == 0 else "QC"]

                @block.vector
                def _(v, gi=gi, wpar=wpar, Q=Q):
                    v.wait_ge(sem[Q], ev_wld[gi])
                    if gi - 2 in ev_wmm:
                        v.wait_ge(sem["ST"], ev_wmm[gi - 2])
                    inc(v.tensor_copy(wobs[wpar][:, :], wovs[wpar][:, :]), "SV")
                ev_wcast[gi] = cnt["SV"]

                @block.tensor
                def _(t, mq=mq, dc=dc, gi=gi, wpar=wpar):
                    t.wait_ge(sem["SV"], ev_wcast[gi])
                    if dc == 0 and mq >= 1:
                        t.wait_ge(sem["SV"], ev_wevac[mq - 1])
                    last = None
                    for j in range(2):
                        for nh in range(2):
                            sl = slice(nh * 512, (nh + 1) * 512)
                            last = t.matmul(pws[j][:, sl], wobs[wpar][:, j * 128:(j + 1) * 128],
                                            yg[:, dc * TOK + nh * 512: dc * TOK + (nh + 1) * 512],
                                            start=(dc == 0), stop=(dc == DCH - 1),
                                            skip_group_check=True)
                    inc(last, "ST")
                ev_wmm[gi] = cnt["ST"]

            @block.vector
            def _(v, mq=mq):
                v.wait_ge(sem["ST"], ev_wmm[mq * DCH + DCH - 1])
                if mq >= 1:
                    v.wait_ge(sem["ST"], ev_wtr[(mq - 1, 1, 7)])
                for j in range(2):
                    inc(v.tensor_copy(mst[:, j * TOK:(j + 1) * TOK], pws[j][:, :]), "SV")
            ev_wevac[mq] = cnt["SV"]

            @block.tensor
            def _(t, mq=mq):
                t.wait_ge(sem["SV"], ev_wevac[mq])
                for j in range(2):
                    for tt in range(8):
                        k = j * 8 + tt
                        pk = (mq, j, tt - 2) if tt >= 2 else                              ((mq, j - 1, 6 + tt) if j == 1 else
                              ((mq - 1, 1, 6 + tt) if mq >= 1 else None))
                        if pk is not None and pk in ev_wadd:
                            t.wait_ge(sem["SV"], ev_wadd[pk])
                        ptx = ptm if k % 2 == 0 else ptm2
                        inc(t.transpose(ptx[:, :],
                                        mst[:, j * TOK + tt * 128: j * TOK + (tt + 1) * 128],
                                        idf[:, :]), "ST")
                        ev_wtr[(mq, j, tt)] = cnt["ST"]

                        @block.vector
                        def _(v, mq=mq, j=j, tt=tt, ptx=ptx):
                            v.wait_ge(sem["ST"], ev_wtr[(mq, j, tt)])
                            if mq == 0 and j == 0 and tt == 0:
                                v.wait_ge(sem["QS"], qs_oacc)
                            col = tt * DM + mq * 256 + j * 128
                            inc(v.tensor_tensor(out_acc[:, col:col + 128],
                                                out_acc[:, col:col + 128],
                                                ptx[:, :], op=ALU.add), "SV")
                            ev_wadd[(mq, j, tt)] = cnt["SV"]

        @block.gpsimd
        def _(g):
            g.wait_ge(sem["SV"], cnt["SV"])
            for tt in range(8):
                dma(g, "QG", out_d[tt * 128:(tt + 1) * 128, :],
                    out_acc[:, tt * DM:(tt + 1) * DM])
            g.wait_ge(sem["QG"], cnt["QG"])

    es.close()
    return nc


_NC_CACHE = None


def make_in_maps(L0_attn_x, L0_attn_W_in, L0_attn_conv_w, L0_attn_conv_b, L0_attn_W_x,
                 L0_attn_W_dt, L0_attn_b_dt, L0_attn_A_log, L0_attn_D, L0_attn_W_out,
                 L0_ffn_kernel, L0_ffn_bias):
    x = np.ascontiguousarray(np.asarray(L0_attn_x, dtype=np.float32).reshape(8192, DM))
    w_in = np.asarray(L0_attn_W_in, np.float32)
    w_in_4 = np.ascontiguousarray(w_in.reshape(2 * DCH, 128, KC, 128).transpose(3, 0, 2, 1))
    conv_w = np.ascontiguousarray(np.asarray(L0_attn_conv_w, np.float32))
    conv_b = np.ascontiguousarray(np.asarray(L0_attn_conv_b, np.float32).reshape(DIN, 1))
    w_x_t = np.ascontiguousarray(np.asarray(L0_attn_W_x, np.float32).T)
    w_dt_t = np.ascontiguousarray(np.asarray(L0_attn_W_dt, np.float32).T)
    b_dt_a = np.ascontiguousarray(np.asarray(L0_attn_b_dt, np.float32).reshape(DIN, 1))
    a_log = np.ascontiguousarray(np.asarray(L0_attn_A_log, np.float32))
    d_vec = np.ascontiguousarray(np.asarray(L0_attn_D, np.float32).reshape(DIN, 1))
    w_out_t = np.ascontiguousarray(np.asarray(L0_attn_W_out, np.float32).T)
    ffn_k = np.ascontiguousarray(np.asarray(L0_ffn_kernel, np.float32))
    ffn_k_t = np.ascontiguousarray(ffn_k.T)
    ffn_b = np.ascontiguousarray(np.asarray(L0_ffn_bias, np.float32).reshape(1, DM))

    in_maps = []
    for i in range(NCORES):
        lo = i * TOK
        xs = np.zeros((POS, DM), np.float32)
        if i > 0:
            xs[0:HALO] = x[lo - HALO:lo]
        xs[HALO:HALO + TOK] = x[lo:lo + TOK]
        in_maps.append({
            "x": xs,
            "w_in_4": w_in_4,
            "conv_w": conv_w,
            "conv_b": conv_b,
            "w_x_t": w_x_t,
            "w_dt_t": w_dt_t,
            "b_dt": b_dt_a,
            "a_log_sl": np.ascontiguousarray(a_log[i * CPC:(i + 1) * CPC]),
            "d_vec": d_vec,
            "w_out_t": w_out_t,
            "ffn_k_t": ffn_k_t,
            "ffn_k": ffn_k,
            "ffn_b": ffn_b,
        })
    return in_maps


def kernel(L0_attn_x, L0_attn_W_in, L0_attn_conv_w, L0_attn_conv_b, L0_attn_W_x,
           L0_attn_W_dt, L0_attn_b_dt, L0_attn_A_log, L0_attn_D, L0_attn_W_out,
           L0_ffn_kernel, L0_ffn_bias):
    global _NC_CACHE
    in_maps = make_in_maps(L0_attn_x, L0_attn_W_in, L0_attn_conv_w, L0_attn_conv_b,
                           L0_attn_W_x, L0_attn_W_dt, L0_attn_b_dt, L0_attn_A_log,
                           L0_attn_D, L0_attn_W_out, L0_ffn_kernel, L0_ffn_bias)
    if _NC_CACHE is None:
        _NC_CACHE = build_nc()
    res = run_bass_kernel_spmd(_NC_CACHE, in_maps, core_ids=list(range(NCORES)))
    out = np.concatenate([res.results[i]["out"] for i in range(NCORES)], axis=0)
    return out.reshape(1, 8192, DM).astype(np.float32)


# version 13
# speedup vs baseline: 1.0630x; 1.0630x over previous
import numpy as np
import concourse.bass as bass
import concourse.mybir as mybir
from concourse.bass_utils import run_bass_kernel_spmd
from concourse.aot_env import allow_ici_all_to_all
from contextlib import ExitStack

F32 = mybir.dt.float32
BF16 = mybir.dt.bfloat16
I32 = mybir.dt.int32
ALU = mybir.AluOpType
ACTF = mybir.ActivationFunctionType
AX = mybir.AxisListType

NCORES = 8
TOK = 1024
HALO = 3
POS = 1152
NT = POS // 128
DM = 2048
DIN = 4096
DST = 16
DTR = 128
DCH = DIN // 128
KC = DM // 128
CPC = DIN // NCORES
BL = CPC // 128
TQ = 8
TQW = 1024
MAGIC = 12582912.0
EPS_AMAX = 1e-18
LN127 = 4.844187086458591

_DEBUG = False


def build_nc():
    nc = bass.Bass("TRN2", target_bir_lowering=False)

    x_in = nc.dram_tensor("x", [POS, DM], F32, kind="ExternalInput").ap()
    w_in_4 = nc.dram_tensor("w_in_4", [128, 2 * DCH, KC, 128], F32, kind="ExternalInput").ap()
    conv_w = nc.dram_tensor("conv_w", [DIN, 4], F32, kind="ExternalInput").ap()
    conv_b = nc.dram_tensor("conv_b", [DIN, 1], F32, kind="ExternalInput").ap()
    w_x_t = nc.dram_tensor("w_x_t", [DIN, 160], F32, kind="ExternalInput").ap()
    w_dt_t = nc.dram_tensor("w_dt_t", [DTR, DIN], F32, kind="ExternalInput").ap()
    b_dt = nc.dram_tensor("b_dt", [DIN, 1], F32, kind="ExternalInput").ap()
    a_log_sl = nc.dram_tensor("a_log_sl", [CPC, DST], F32, kind="ExternalInput").ap()
    d_vec = nc.dram_tensor("d_vec", [DIN, 1], F32, kind="ExternalInput").ap()
    w_out_t = nc.dram_tensor("w_out_t", [DIN, DM], F32, kind="ExternalInput").ap()
    ffn_k_t = nc.dram_tensor("ffn_k_t", [DM, DM], F32, kind="ExternalInput").ap()
    ffn_k = nc.dram_tensor("ffn_k", [DM, DM], F32, kind="ExternalInput").ap()
    ffn_b = nc.dram_tensor("ffn_b", [1, DM], F32, kind="ExternalInput").ap()
    out_d = nc.dram_tensor("out", [TOK, DM], F32, kind="ExternalOutput").ap()

    ROWS_SH = 2 * CPC + 2 * DST
    a2a_in = nc.dram_tensor("a2a_in", [NCORES, ROWS_SH, TOK], F32)
    a2a_out = nc.dram_tensor("a2a_out", [NCORES, ROWS_SH, TOK], F32)
    y_in = nc.dram_tensor("y_in", [NCORES, CPC, TOK], F32)
    y_out = nc.dram_tensor("y_out", [NCORES, CPC, TOK], F32)
    u_scr = nc.dram_tensor("u_scr", [DIN, TOK], F32)
    sz_scr = nc.dram_tensor("sz_scr", [DIN, POS], F32)
    bq_scr = nc.dram_tensor("bq_scr", [1, DM], F32)
    o_scr = nc.dram_tensor("o_scr", [TOK, DM], F32)

    dbg = {}
    if _DEBUG:
        dbg["xn"] = nc.dram_tensor("dbg_xn", [POS, DM], F32, kind="ExternalOutput").ap()
        dbg["xs"] = nc.dram_tensor("dbg_xs", [DIN, POS], F32, kind="ExternalOutput").ap()
        dbg["xdbl"] = nc.dram_tensor("dbg_xdbl", [160, TOK], F32, kind="ExternalOutput").ap()
        dbg["dt"] = nc.dram_tensor("dbg_dt", [DIN, TOK], F32, kind="ExternalOutput").ap()
        dbg["yown"] = nc.dram_tensor("dbg_yown", [DIN, TOK], F32, kind="ExternalOutput").ap()
        dbg["ffnq"] = nc.dram_tensor("dbg_ffnq", [TOK, DM], F32, kind="ExternalOutput").ap()
        dbg["xqt"] = nc.dram_tensor("dbg_xqt", [128, KC * POS], BF16, kind="ExternalOutput").ap()
        dbg["xqt_e"] = nc.dram_tensor("dbg_xqt_e", [128, KC * POS], BF16, kind="ExternalOutput").ap()
        dbg["xqb"] = nc.dram_tensor("dbg_xqb", [POS, DM], BF16, kind="ExternalOutput").ap()
        dbg["qd"] = nc.dram_tensor("dbg_qd", [128, 4], F32, kind="ExternalOutput").ap()
        dbg["sq0"] = nc.dram_tensor("dbg_sq0", [128, DM], F32, kind="ExternalOutput").ap()
        dbg["kq"] = nc.dram_tensor("dbg_kq", [128, KC * DM], BF16, kind="ExternalOutput").ap()
        dbg["bcolq"] = nc.dram_tensor("dbg_bcolq", [128, 128], F32, kind="ExternalOutput").ap()
        dbg["dgq0"] = nc.dram_tensor("dbg_dgq0", [128, 1024], BF16, kind="ExternalOutput").ap()
        dbg["dtut"] = nc.dram_tensor("dbg_dtut", [128, 4096], BF16, kind="ExternalOutput").ap()
        dbg["dbu0"] = nc.dram_tensor("dbg_dbu0", [128, 1024], F32, kind="ExternalOutput").ap()

    cnt = {k: 0 for k in ("QS", "QC", "QG", "SV", "SS", "ST", "SG", "SD")}
    es = ExitStack()
    sem = {k: es.enter_context(nc.semaphore(k)) for k in cnt}

    idb = es.enter_context(nc.sbuf_tensor("idb", [128, 128], BF16))
    a_sb = es.enter_context(nc.sbuf_tensor("a_sb", [128, BL * DST], F32))
    carry = es.enter_context(nc.sbuf_tensor("carry", [128, BL * DST], F32))
    ln127c = es.enter_context(nc.sbuf_tensor("ln127c", [128, 1], F32))
    idf32 = es.enter_context(nc.sbuf_tensor("idf32", [128, 128], F32))

    block = es.enter_context(nc.Block())

    def snap():
        return dict(cnt)

    def wait_all(eng, snp, skip=()):
        for k, v in snp.items():
            if v > 0 and k not in skip:
                eng.wait_ge(sem[k], v)

    def dma(eng, q, out, in_):
        eng.dma_start(out=out, in_=in_).then_inc(sem[q], 16)
        cnt[q] += 16

    def inc(instr, name, n=1):
        instr.then_inc(sem[name], n)
        cnt[name] += n
        return instr

    def maxtree(v, buf, width, out_col):
        w = width // 2
        while w >= 1:
            inc(v.tensor_tensor(buf[:, 0:w], buf[:, 0:w], buf[:, w:2 * w], op=ALU.max), "SV")
            w //= 2
        inc(v.tensor_scalar(out_col, buf[:, 0:1], EPS_AMAX, None, op0=ALU.max), "SV")

    def preload_ln_exp_set(eng):
        from concourse.hw_specs import get_activation_tables
        for idx, funcs in enumerate(get_activation_tables(nc.m.arch).values()):
            if ACTF.Exp in funcs and ACTF.Ln in funcs:
                eng.add_instruction(mybir.InstLoadActFuncSet(
                    name=nc.get_next_instruction_name(),
                    act_func_set_id=idx, ins=[], outs=[]))
                return
        raise ValueError("no set with Exp+Ln")

    def sfence(s):
        s.drain()
        inc(s.nop(), "SS")

    with (
        nc.sbuf_tensor("pi", [128, 128], I32) as pi,
        nc.sbuf_tensor("fi", [128, 128], I32) as fi,
        nc.sbuf_tensor("idf0", [128, 128], F32) as idf0,
        nc.sbuf_tensor("alog", [128, BL * DST], F32) as alog,
    ):
        @block.gpsimd
        def _(g):
            inc(g.iota(pi[:, :], [[0, 128]], base=0, channel_multiplier=1), "SG")
            inc(g.iota(fi[:, :], [[1, 128]], base=0, channel_multiplier=0), "SG")

        @block.sync
        def _(s):
            for b in range(BL):
                dma(s, "QS", alog[:, b * DST:(b + 1) * DST],
                    a_log_sl[b * 128:(b + 1) * 128, :])

        @block.vector
        def _(v):
            v.wait_ge(sem["SG"], cnt["SG"])
            inc(v.memset(ln127c[:, :], LN127), "SV")
            inc(v.tensor_tensor(idf32[:, :], pi[:, :], fi[:, :], op=ALU.is_equal), "SV")
            inc(v.tensor_copy(idb[:, :], idf32[:, :]), "SV")
            inc(v.memset(carry[:, :], 0.0), "SV")

        @block.scalar
        def _(s):
            s.wait_ge(sem["QS"], cnt["QS"])
            s.activation(a_sb[:, :], alog[:, :], ACTF.Exp, bias=0.0, scale=1.0)
            sfence(s)

        @block.vector
        def _(v):
            v.wait_ge(sem["SS"], cnt["SS"])
            inc(v.tensor_scalar(a_sb[:, :], a_sb[:, :], -1.0, None, op0=ALU.mult), "SV")

        s0 = snap()

    es_xq = ExitStack()
    xqT = es_xq.enter_context(nc.sbuf_tensor("xqT", [128, KC * POS], BF16))
    es1 = ExitStack()
    xnT = es1.enter_context(nc.sbuf_tensor("xnT", [128, KC * POS], BF16))
    with (
        nc.sbuf_tensor("xt", [128, DM], F32) as xt,
        nc.sbuf_tensor("xn_f", [128, DM], F32) as xn_f,
        nc.sbuf_tensor("xn_b", [128, DM], BF16) as xn_b,
        nc.sbuf_tensor("xq_b", [128, DM], BF16) as xq_b,
        nc.sbuf_tensor("sq", [128, DM], F32) as sq,
        nc.sbuf_tensor("c0", [128, 1], F32) as c0,
        nc.sbuf_tensor("c1", [128, 1], F32) as c1,
        nc.sbuf_tensor("c2", [128, 1], F32) as c2,
        nc.sbuf_tensor("c3", [128, 1], F32) as c3,
        nc.sbuf_tensor("dbgq", [128, 4], F32) as dbgq,
        nc.sbuf_tensor("spc", [128, 256], F32) as spc,
        nc.psum_tensor("ptr", [128, 128], BF16) as ptr,
        nc.psum_tensor("ptr2", [128, 128], BF16) as ptr2,
        nc.psum_tensor("ptr3", [128, 128], BF16) as ptr3,
        nc.psum_tensor("ptr4", [128, 128], BF16) as ptr4,
    ):
        ptrs = [(ptr, ptr2), (ptr3, ptr4)]
        ev_s1evac = {}
        for tt in range(NT):
            base_v, base_s, base_t, base_q = cnt["SV"], cnt["SS"], cnt["ST"], cnt["QS"]

            @block.sync
            def _(s, tt=tt):
                if tt == 0:
                    wait_all(s, s0, skip=("QS",))
                else:
                    s.wait_ge(sem["ST"], base_t)
                dma(s, "QS", xt[:, :], x_in[tt * 128:(tt + 1) * 128, :])

            @block.scalar
            def _(s, tt=tt):
                s.wait_ge(sem["QS"], cnt["QS"])
                s.activation(sq[:, :], xt[:, :], ACTF.Square, bias=0.0, scale=1.0)
                sfence(s)

            @block.vector
            def _(v, tt=tt):
                v.wait_ge(sem["QS"], cnt["QS"])
                inc(v.reduce_sum(c0[:, :], xt[:, :], axis=AX.X), "SV")
                v.wait_ge(sem["SS"], base_s + 1)
                inc(v.reduce_sum(c1[:, :], sq[:, :], axis=AX.X), "SV")
                inc(v.tensor_scalar(c0[:, :], c0[:, :], 1.0 / DM, None, op0=ALU.mult), "SV")
                inc(v.tensor_scalar(c1[:, :], c1[:, :], 1.0 / DM, 1e-5,
                                    op0=ALU.mult, op1=ALU.add), "SV")
                inc(v.tensor_copy(spc[:, :], xt[:, 0:256]), "SV")
                inc(v.tensor_tensor(c2[:, :], c0[:, :], c0[:, :], op=ALU.mult), "SV")
                inc(v.tensor_copy(spc[:, :], xt[:, 256:512]), "SV")
                inc(v.tensor_tensor(c1[:, :], c1[:, :], c2[:, :], op=ALU.subtract), "SV")

            @block.scalar
            def _(s, tt=tt):
                s.wait_ge(sem["SV"], base_v + 8)
                inc(s.activation(c1[:, :], c1[:, :], ACTF.Ln, bias=0.0, scale=1.0), "SS")
                s.activation(c1[:, :], c1[:, :], ACTF.Exp, bias=0.0, scale=-0.5)
                sfence(s)

            @block.vector
            def _(v, tt=tt):
                v.wait_ge(sem["SS"], base_s + 3)
                inc(v.tensor_scalar(xn_f[:, :], xt[:, :], c0[:, :], c1[:, :],
                                    op0=ALU.subtract, op1=ALU.mult), "SV")
                inc(v.tensor_scalar(sq[:, :], xn_f[:, :], -1.0, None, op0=ALU.mult), "SV")
                inc(v.tensor_tensor(sq[:, :], sq[:, :], xn_f[:, :], op=ALU.max), "SV")
                inc(v.reduce_max(c2[:, :], sq[:, :], axis=AX.X), "SV")
                inc(v.tensor_copy(xn_b[:, :], xn_f[:, :]), "SV")
                inc(v.tensor_scalar(c2[:, :], c2[:, :], EPS_AMAX, None, op0=ALU.max), "SV")
                inc(v.tensor_copy(spc[:, :], xn_f[:, 0:256]), "SV")
                inc(v.reciprocal(c3[:, :], c2[:, :]), "SV")
                inc(v.tensor_copy(spc[:, :], xn_f[:, 256:512]), "SV")
                inc(v.tensor_scalar(sq[:, :], xn_f[:, :], c3[:, :], None, op0=ALU.mult), "SV")
                inc(v.tensor_scalar(sq[:, :], sq[:, :], 127.0, MAGIC,
                                    op0=ALU.mult, op1=ALU.add), "SV")
                inc(v.tensor_scalar(sq[:, :], sq[:, :], MAGIC + 127.0, MAGIC - 127.0,
                                    op0=ALU.min, op1=ALU.max), "SV")
                inc(v.tensor_scalar(sq[:, :], sq[:, :], MAGIC, c2[:, :],
                                    op0=ALU.subtract, op1=ALU.mult), "SV")
                inc(v.tensor_scalar(xq_b[:, :], sq[:, :], 1.0 / 127.0, None, op0=ALU.mult), "SV")

            if _DEBUG:
                @block.gpsimd
                def _(g, tt=tt):
                    g.wait_ge(sem["SV"], base_v + 10)
                    dma(g, "QG", dbg["xn"][tt * 128:(tt + 1) * 128, :], xn_f[:, :])
                    g.wait_ge(sem["SV"], cnt["SV"])
                    dma(g, "QG", dbg["xqb"][tt * 128:(tt + 1) * 128, :], xq_b[:, :])

            @block.tensor
            def _(t, tt=tt):
                t.wait_ge(sem["SV"], cnt["SV"])
                for kcc in range(KC):
                    pa, pb = ptrs[kcc % 2]
                    gi = tt * KC + kcc - 2
                    if gi >= 0 and gi in ev_s1evac:
                        t.wait_ge(sem["SS"], ev_s1evac[gi])
                    t.transpose(pa[:, :], xn_b[:, kcc * 128:(kcc + 1) * 128], idb[:, :])
                    inc(t.transpose(pb[:, :], xq_b[:, kcc * 128:(kcc + 1) * 128], idb[:, :]), "ST")

                    @block.scalar
                    def _(s, tt=tt, kcc=kcc, pa=pa, pb=pb):
                        s.wait_ge(sem["ST"], cnt["ST"])
                        inc(s.copy(xnT[:, kcc * POS + tt * 128: kcc * POS + (tt + 1) * 128],
                                   pa[:, :]), "SS")
                        inc(s.copy(xqT[:, kcc * POS + tt * 128: kcc * POS + (tt + 1) * 128],
                                   pb[:, :]), "SS")
                        ev_s1evac[tt * KC + kcc] = cnt["SS"]

        if _DEBUG:
            @block.gpsimd
            def _(g):
                g.wait_ge(sem["SS"], cnt["SS"])
                dma(g, "QG", dbg["xqt_e"], xqT[:, :])

        s1 = snap()

    es2 = ExitStack()
    _e2 = es2.enter_context
    wsl = [_e2(nc.sbuf_tensor("wslA", [128, DM], F32)),
           _e2(nc.sbuf_tensor("wslB", [128, DM], F32))]
    wslb = [_e2(nc.sbuf_tensor("wslbA", [128, DM], BF16)),
            _e2(nc.sbuf_tensor("wslbB", [128, DM], BF16))]
    xs_sb = [_e2(nc.sbuf_tensor("xs_sbA", [128, POS], F32)),
             _e2(nc.sbuf_tensor("xs_sbB", [128, POS], F32))]
    zt = [_e2(nc.sbuf_tensor("ztA", [128, POS], F32)),
          _e2(nc.sbuf_tensor("ztB", [128, POS], F32))]
    cv1 = [_e2(nc.sbuf_tensor("cv1A", [128, TOK], F32)),
           _e2(nc.sbuf_tensor("cv1B", [128, TOK], F32))]
    cv2 = [_e2(nc.sbuf_tensor("cv2A", [128, TOK], F32)),
           _e2(nc.sbuf_tensor("cv2B", [128, TOK], F32))]
    u_f = [_e2(nc.sbuf_tensor("u_fA", [128, TOK], F32)),
           _e2(nc.sbuf_tensor("u_fB", [128, TOK], F32))]
    cwc = [_e2(nc.sbuf_tensor("cwcA", [128, 4], F32)),
           _e2(nc.sbuf_tensor("cwcB", [128, 4], F32))]
    cbc = [_e2(nc.sbuf_tensor("cbcA", [128, 1], F32)),
           _e2(nc.sbuf_tensor("cbcB", [128, 1], F32))]
    wx_f = _e2(nc.sbuf_tensor("wx_f", [128, DCH * 160], F32))
    wdt_b = _e2(nc.sbuf_tensor("wdt_b", [128, DIN], BF16))
    dtr_b = _e2(nc.sbuf_tensor("dtr_b", [128, TOK], BF16))
    bc_sb = _e2(nc.sbuf_tensor("bc_sb", [32, TOK], F32))
    dt_sb = [_e2(nc.sbuf_tensor("dt_sbA", [128, TOK], F32)),
             _e2(nc.sbuf_tensor("dt_sbB", [128, TOK], F32))]
    et = _e2(nc.sbuf_tensor("et", [128, TOK], F32))
    bdtc = [_e2(nc.sbuf_tensor("bdtcA", [128, 1], F32)),
            _e2(nc.sbuf_tensor("bdtcB", [128, 1], F32))]
    uld = [_e2(nc.sbuf_tensor("uldA", [128, TOK], F32)),
           _e2(nc.sbuf_tensor("uldB", [128, TOK], F32))]
    dtu_sb = [_e2(nc.sbuf_tensor("dtu_sbA", [128, TOK], F32)),
              _e2(nc.sbuf_tensor("dtu_sbB", [128, TOK], F32))]
    pp = _e2(nc.psum_tensor("pp", [128, 512], F32))
    pq = _e2(nc.psum_tensor("pq", [128, 512], F32))
    pr = _e2(nc.psum_tensor("pr", [128, 512], F32))
    pd0 = _e2(nc.psum_tensor("pd0", [128, 512], F32))
    px1 = _e2(nc.psum_tensor("px1", [128, TOK], F32))
    px2 = _e2(nc.psum_tensor("px2", [32, TOK], F32))
    if True:
        @block.sync
        def _(s):
            wait_all(s, s1, skip=("QS",))
            dma(s, "QS", wx_f.ap().rearrange("p (c f) -> p c f", c=DCH),
                w_x_t.rearrange("(c p) f -> p c f", p=128))
            for k, stg in enumerate([uld[0], uld[1], cv1[0], cv1[1]]):
                dma(s, "QS", stg[:, :], w_dt_t[:, k * 1024:(k + 1) * 1024])

        for k, stg in enumerate([uld[0], uld[1], cv1[0], cv1[1]]):
            @block.vector
            def _(v, k=k, stg=stg):
                v.wait_ge(sem["QS"], cnt["QS"])
                inc(v.tensor_copy(wdt_b[:, k * 1024:(k + 1) * 1024], stg[:, :]), "SV")

        ev_cast = {}
        ev_mm = {}
        ev_evac = {}
        ev_conv = {}
        ev_ub = {}
        ev_zspill = {}
        ev_silu = {}

        NSLICES = [(0, 512), (512, 512), (1024, 128)]

        for c in range(2 * DCH):
            is_xs = c < DCH
            par = c % 2
            Q = "QS" if par == 0 else "QC"

            @block.sync if par == 0 else block.scalar
            def _(s, c=c, par=par, Q=Q, is_xs=is_xs):
                if c - 2 in ev_cast:
                    s.wait_ge(sem["SV"], ev_cast[c - 2])
                if is_xs and c - 2 >= 0:
                    s.wait_ge(sem["SV"], ev_conv[c - 2])
                dma(s, Q, wsl[par].ap().rearrange("p (kc ch) -> p kc ch", kc=KC),
                    w_in_4[:, c, :, :])
                if is_xs:
                    dma(s, Q, cwc[par][:, :], conv_w[c * 128:(c + 1) * 128, :])
                    dma(s, Q, cbc[par][:, :], conv_b[c * 128:(c + 1) * 128, :])

            q_slab = cnt[Q]

            @block.vector
            def _(v, c=c, par=par, Q=Q, q_slab=q_slab, is_xs=is_xs):
                v.wait_ge(sem[Q], q_slab)
                if c - 2 in ev_mm:
                    v.wait_ge(sem["ST"], ev_mm[c - 2])
                inc(v.tensor_copy(wslb[par][:, :], wsl[par][:, :]), "SV")
                ev_cast[c] = cnt["SV"]

            NPS = [pp, pq, pr]
            for isl, (n0, nw) in enumerate(NSLICES):
                ps = NPS[isl]

                @block.tensor
                def _(t, c=c, par=par, n0=n0, nw=nw, ps=ps, isl=isl):
                    t.wait_ge(sem["SV"], ev_cast[c])
                    if (c - 1, isl) in ev_evac:
                        k, v_ = ev_evac[(c - 1, isl)]
                        t.wait_ge(sem[k], v_)
                    last = None
                    for kcc in range(KC):
                        last = t.matmul(ps[:, 0:nw], wslb[par][:, kcc * 128:(kcc + 1) * 128],
                                        xnT[:, kcc * POS + n0: kcc * POS + n0 + nw],
                                        start=(kcc == 0), stop=(kcc == KC - 1))
                    inc(last, "ST")
                    if isl == len(NSLICES) - 1:
                        ev_mm[c] = cnt["ST"]

                if is_xs:
                    @block.scalar
                    def _(se, c=c, par=par, n0=n0, nw=nw, ps=ps, isl=isl):
                        se.wait_ge(sem["ST"], cnt["ST"])
                        if isl == 0 and c - 2 in ev_conv:
                            se.wait_ge(sem["SV"], ev_conv[c - 2])
                        ins = se.copy(xs_sb[par][:, n0:n0 + nw], ps[:, 0:nw])
                        if isl < 2:
                            inc(ins, "SS")
                        else:
                            sfence(se)
                        ev_evac[(c, isl)] = ("SS", cnt["SS"])
                else:
                    @block.scalar
                    def _(se, c=c, par=par, n0=n0, nw=nw, ps=ps, isl=isl):
                        se.wait_ge(sem["ST"], cnt["ST"])
                        if isl == 0 and c - 2 >= DCH:
                            se.wait_ge(sem["QG"], ev_zspill[c - 2])
                        inc(se.activation(zt[par][:, n0:n0 + nw], ps[:, 0:nw], ACTF.Silu,
                                          bias=0.0, scale=1.0), "SS")
                        ev_evac[(c, isl)] = ("SS", cnt["SS"])

            if is_xs:
                @block.vector
                def _(v, c=c, par=par):
                    v.wait_ge(sem["SS"], ev_evac[(c, 2)][1])
                    if c - 2 in ev_silu:
                        v.wait_ge(sem["SS"], ev_silu[c - 2])
                    inc(v.tensor_scalar(cv1[par][:, :], xs_sb[par][:, 0:TOK], cwc[par][:, 0:1],
                                        cbc[par][:, :], op0=ALU.mult, op1=ALU.add), "SV")
                    inc(v.scalar_tensor_tensor(cv2[par][:, :], xs_sb[par][:, 1:TOK + 1],
                                               cwc[par][:, 1:2],
                                               cv1[par][:, :], op0=ALU.mult, op1=ALU.add), "SV")
                    inc(v.scalar_tensor_tensor(cv1[par][:, :], xs_sb[par][:, 2:TOK + 2],
                                               cwc[par][:, 2:3],
                                               cv2[par][:, :], op0=ALU.mult, op1=ALU.add), "SV")
                    inc(v.scalar_tensor_tensor(cv2[par][:, :], xs_sb[par][:, 3:TOK + 3],
                                               cwc[par][:, 3:4],
                                               cv1[par][:, :], op0=ALU.mult, op1=ALU.add), "SV")
                    ev_conv[c] = cnt["SV"]

                @block.scalar
                def _(se, c=c, par=par):
                    se.wait_ge(sem["SV"], ev_conv[c])
                    if c - 2 in ev_ub:
                        se.wait_ge(sem["QG"], ev_ub[c - 2])
                    se.activation(u_f[par][:, :], cv2[par][:, :], ACTF.Silu, bias=0.0, scale=1.0)
                    sfence(se)
                    ev_silu[c] = cnt["SS"]

                @block.gpsimd
                def _(g, c=c, par=par):
                    g.wait_ge(sem["SS"], ev_silu[c])
                    dma(g, "QG", u_scr.ap()[c * 128:(c + 1) * 128, :], u_f[par][:, :])
                    ev_ub[c] = cnt["QG"]
            else:
                @block.gpsimd
                def _(g, c=c, par=par):
                    g.wait_ge(sem["SS"], ev_evac[(c, 2)][1])
                    dma(g, "QG", sz_scr.ap()[(c - DCH) * 128:(c - DCH + 1) * 128, :],
                        zt[par][:, :])
                    ev_zspill[c] = cnt["QG"]

        ev_uxld = {}
        ev_uxmm = {}
        for c in range(DCH):
            upar = c % 2

            @block.sync
            def _(s, c=c, upar=upar):
                if c - 2 in ev_uxmm:
                    s.wait_ge(sem["ST"], ev_uxmm[c - 2])
                s.wait_ge(sem["QG"], ev_ub[c])
                dma(s, "QS", uld[upar][:, :], u_scr.ap()[c * 128:(c + 1) * 128, :])
            ev_uxld[c] = cnt["QS"]

            @block.tensor
            def _(t, c=c, upar=upar):
                t.wait_ge(sem["QS"], ev_uxld[c])
                for nh in range(2):
                    sl = slice(nh * 512, (nh + 1) * 512)
                    t.matmul(px1[:, sl], wx_f[:, c * 160:c * 160 + 128],
                             uld[upar][:, sl],
                             start=(c == 0), stop=(c == DCH - 1), skip_group_check=True)
                    ins = t.matmul(px2[:, sl], wx_f[:, c * 160 + 128:c * 160 + 160],
                                   uld[upar][:, sl],
                                   start=(c == 0), stop=(c == DCH - 1), skip_group_check=True)
                inc(ins, "ST")
            ev_uxmm[c] = cnt["ST"]

        @block.vector
        def _(v):
            v.wait_ge(sem["ST"], cnt["ST"])
            inc(v.tensor_copy(dtr_b[:, :], px1[:, :]), "SV")
            inc(v.tensor_copy(bc_sb[:, :], px2[:, :]), "SV")

        if _DEBUG:
            @block.gpsimd
            def _(g):
                g.wait_ge(sem["SV"], cnt["SV"])
                dma(g, "QG", dbg["xdbl"][128:160, :], bc_sb[:, :])

        @block.gpsimd
        def _(g):
            g.wait_ge(sem["SV"], cnt["SV"])
            for j in range(NCORES):
                dma(g, "QG", a2a_in.ap()[j, 2 * CPC:2 * CPC + 2 * DST, :], bc_sb[:, :])

        ss_dt0 = cnt["SS"]
        sv_dt0 = cnt["SV"]
        qg_dt0 = cnt["QG"]
        ev_dtmm, ev_dtexp, ev_dtu, ev_dtq, ev_dtg = {}, {}, {}, {}, {}
        for c in range(DCH):
            par = c % 2
            pda, pdb = (pp, pq) if par == 0 else (pr, pd0)

            @block.sync
            def _(s, c=c, par=par):
                if c - 2 in ev_dtu:
                    s.wait_ge(sem["SV"], ev_dtu[c - 2])
                elif c < 2:
                    s.wait_ge(sem["ST"], ev_uxmm[DCH - 2 + c])
                if c - 2 in ev_dtexp:
                    s.wait_ge(sem["SS"], ev_dtexp[c - 2])
                if c == 0:
                    s.wait_ge(sem["QG"], qg_dt0)
                dma(s, "QS", bdtc[par][:, :], b_dt[c * 128:(c + 1) * 128, :])
                dma(s, "QS", uld[par][:, :], u_scr.ap()[c * 128:(c + 1) * 128, :])
            ev_dtq[c] = cnt["QS"]

            @block.tensor
            def _(t, c=c, par=par, pda=pda, pdb=pdb):
                if c == 0:
                    t.wait_ge(sem["SV"], cnt["SV"])
                if c < 2:
                    t.wait_ge(sem["SS"], ss_dt0)
                elif c - 2 in ev_dtexp:
                    t.wait_ge(sem["SS"], ev_dtexp[c - 2])
                t.matmul(pda[:, 0:512], wdt_b[:, c * 128:(c + 1) * 128], dtr_b[:, 0:512],
                         start=True, stop=True, skip_group_check=True)
                inc(t.matmul(pdb[:, 0:512], wdt_b[:, c * 128:(c + 1) * 128], dtr_b[:, 512:1024],
                             start=True, stop=True, skip_group_check=True), "ST")
            ev_dtmm[c] = cnt["ST"]

            @block.scalar
            def _(s, c=c, par=par, pda=pda, pdb=pdb):
                s.wait_ge(sem["ST"], ev_dtmm[c])
                s.wait_ge(sem["QS"], ev_dtq[c])
                if c - 2 in ev_dtg:
                    s.wait_ge(sem["QG"], ev_dtg[c - 2])
                if c - 2 in ev_dtu:
                    s.wait_ge(sem["SV"], ev_dtu[c - 2])
                inc(s.activation(et[:, 0:512], pda[:, 0:512], ACTF.Exp,
                                 bias=bdtc[par][:, :], scale=1.0), "SS")
                inc(s.activation(et[:, 512:1024], pdb[:, 0:512], ACTF.Exp,
                                 bias=bdtc[par][:, :], scale=1.0), "SS")
                s.activation(dt_sb[par][:, :], et[:, :], ACTF.Ln, bias=1.0, scale=1.0)
                sfence(s)
            ev_dtexp[c] = cnt["SS"]

            @block.vector
            def _(v, c=c, par=par):
                v.wait_ge(sem["SS"], ev_dtexp[c])
                v.wait_ge(sem["QS"], ev_dtq[c])
                if c - 2 in ev_dtg:
                    v.wait_ge(sem["QG"], ev_dtg[c - 2])
                inc(v.tensor_tensor(dtu_sb[par][:, :], dt_sb[par][:, :], uld[par][:, :],
                                    op=ALU.mult), "SV")
            ev_dtu[c] = cnt["SV"]

            @block.gpsimd
            def _(g, c=c, par=par):
                j, r = c // BL, (c % BL) * 128
                g.wait_ge(sem["SS"], ev_dtexp[c])
                dma(g, "QG", a2a_in.ap()[j, r:r + 128, :], dt_sb[par][:, :])
                g.wait_ge(sem["SV"], ev_dtu[c])
                dma(g, "QG", a2a_in.ap()[j, CPC + r:CPC + r + 128, :], dtu_sb[par][:, :])
                if _DEBUG:
                    dma(g, "QG", dbg["dt"][c * 128:(c + 1) * 128, :], dt_sb[par][:, :])
            ev_dtg[c] = cnt["QG"]

        s2 = snap()

    es2.close()
    es1.close()


    with (
        nc.sbuf_tensor("kqb", [128, DM], BF16) as kqb,
        nc.sbuf_tensor("kq", [128, KC * DM], BF16) as kq,
        nc.sbuf_tensor("bq_rep", [128, DM], F32) as bq_rep,
        nc.sbuf_tensor("scr8", [128, DM], F32) as scr8,
        nc.sbuf_tensor("kab", [128, DM], F32) as kab,
        nc.sbuf_tensor("k0", [128, 1], F32) as k0,
        nc.sbuf_tensor("k1", [128, 1], F32) as k1,
        nc.psum_tensor("ptk", [128, 128], BF16) as ptk,
        nc.psum_tensor("ptk2", [128, 128], BF16) as ptk2,
    ):
        kqc = scr8
        with (
            nc.sbuf_tensor("bqr", [1, DM], F32) as bqr,
            nc.sbuf_tensor("b0", [1, 1], F32) as b0,
            nc.sbuf_tensor("b1", [1, 1], F32) as b1,
        ):
            @block.sync
            def _(s):
                wait_all(s, s2, skip=("QS", "QC"))
                dma(s, "QS", bqr[:, :], ffn_b)

            @block.vector
            def _(v):
                v.wait_ge(sem["QS"], cnt["QS"])
                inc(v.tensor_scalar(bq_rep[0:1, :], bqr[:, :], -1.0, None, op0=ALU.mult), "SV")
                inc(v.tensor_tensor(bq_rep[0:1, :], bq_rep[0:1, :], bqr[:, :], op=ALU.max), "SV")
                inc(v.reduce_max(b0[:, :], bq_rep[0:1, :], axis=AX.X), "SV")
                inc(v.tensor_copy(kab[0:1, 0:256], bqr[:, 0:256]), "SV")
                inc(v.tensor_scalar(b0[:, :], b0[:, :], EPS_AMAX, None, op0=ALU.max), "SV")
                inc(v.tensor_copy(kab[0:1, 256:512], bqr[:, 0:256]), "SV")
                inc(v.reciprocal(b1[:, :], b0[:, :]), "SV")
                inc(v.tensor_copy(kab[0:1, 512:768], bqr[:, 0:256]), "SV")
                inc(v.tensor_scalar(bqr[:, :], bqr[:, :], b1[:, :], None, op0=ALU.mult), "SV")
                inc(v.tensor_scalar(bqr[:, :], bqr[:, :], 127.0, MAGIC,
                                    op0=ALU.mult, op1=ALU.add), "SV")
                inc(v.tensor_scalar(bqr[:, :], bqr[:, :], MAGIC + 127.0, MAGIC - 127.0,
                                    op0=ALU.min, op1=ALU.max), "SV")
                inc(v.tensor_scalar(bqr[:, :], bqr[:, :], MAGIC, b0[:, :],
                                    op0=ALU.subtract, op1=ALU.mult), "SV")
                inc(v.tensor_scalar(bqr[:, :], bqr[:, :], 1.0 / 127.0, None, op0=ALU.mult), "SV")

            @block.gpsimd
            def _(g):
                g.wait_ge(sem["SV"], cnt["SV"])
                dma(g, "QG", bq_scr.ap(), bqr[:, :])

            @block.sync
            def _(s):
                s.wait_ge(sem["QG"], cnt["QG"])
                dma(s, "QS", bq_rep[:, :], bq_scr.ap().broadcast_to([128, DM]))

        for fc in range(KC):
            base_v, base_t, base_s = cnt["SV"], cnt["ST"], cnt["SS"]

            @block.sync
            def _(s, fc=fc, base_v=base_v):
                s.wait_ge(sem["SV"], base_v)
                dma(s, "QS", kqc[:, :], ffn_k_t[fc * 128:(fc + 1) * 128, :])

            @block.vector
            def _(v, fc=fc):
                v.wait_ge(sem["QS"], cnt["QS"])
                inc(v.tensor_scalar(kab[:, :], kqc[:, :], -1.0, None, op0=ALU.mult), "SV")
                inc(v.tensor_tensor(kab[:, :], kab[:, :], kqc[:, :], op=ALU.max), "SV")
                inc(v.reduce_max(k0[:, :], kab[:, :], axis=AX.X), "SV")
                inc(v.tensor_copy(kab[:, 0:256], kqc[:, 0:256]), "SV")
                inc(v.tensor_scalar(k0[:, :], k0[:, :], EPS_AMAX, None, op0=ALU.max), "SV")
                inc(v.tensor_copy(kab[:, 256:512], kqc[:, 0:256]), "SV")
                inc(v.reciprocal(k1[:, :], k0[:, :]), "SV")
                inc(v.tensor_copy(kab[:, 512:768], kqc[:, 0:256]), "SV")
                inc(v.tensor_scalar(kqc[:, :], kqc[:, :], k1[:, :], None, op0=ALU.mult), "SV")
                inc(v.tensor_scalar(kqc[:, :], kqc[:, :], 127.0, MAGIC,
                                    op0=ALU.mult, op1=ALU.add), "SV")
                inc(v.tensor_scalar(kqc[:, :], kqc[:, :], MAGIC + 127.0, MAGIC - 127.0,
                                    op0=ALU.min, op1=ALU.max), "SV")
                inc(v.tensor_scalar(kqc[:, :], kqc[:, :], MAGIC, k0[:, :],
                                    op0=ALU.subtract, op1=ALU.mult), "SV")
                inc(v.tensor_scalar(kqb[:, :], kqc[:, :], 1.0 / 127.0, None, op0=ALU.mult), "SV")

            @block.tensor
            def _(t, fc=fc):
                t.wait_ge(sem["SV"], cnt["SV"])
                for dc in range(KC):
                    if dc > 0:
                        t.wait_ge(sem["SS"], cnt["SS"])
                    if dc % 2 == 0:
                        inc(t.transpose(ptk[:, :], kqb[:, dc * 128:(dc + 1) * 128], idb[:, :]), "ST")
                    else:
                        inc(t.transpose(ptk2[:, :], kqb[:, dc * 128:(dc + 1) * 128], idb[:, :]), "ST")

                    @block.scalar
                    def _(s, fc=fc, dc=dc):
                        s.wait_ge(sem["ST"], cnt["ST"])
                        src = ptk if dc % 2 == 0 else ptk2
                        inc(s.copy(kq[:, dc * DM + fc * 128: dc * DM + (fc + 1) * 128],
                                   src[:, :]), "SS")
                        if fc == KC - 1 and dc == KC - 1:
                            sfence(s)

        if _DEBUG:
            @block.gpsimd
            def _(g):
                g.wait_ge(sem["SS"], cnt["SS"])
                dma(g, "QG", dbg["xqt"], xqT[:, :])
                dma(g, "QG", dbg["kq"], kq[:, :])

        ev_ospill = {}
        with (
            nc.sbuf_tensor("xt6", [128, DM], F32) as xt6,
            nc.sbuf_tensor("obufA", [128, DM], F32) as obufA,
            nc.sbuf_tensor("obufB", [128, DM], F32) as obufB,
            nc.sbuf_tensor("f0", [128, 1], F32) as f0c,
            nc.sbuf_tensor("f1c", [128, 1], F32) as f1c,
            nc.psum_tensor("pf", [128, DM], F32) as pf,
        ):
            obuf = [obufA, obufB]
            f1 = scr8
            for tt in range(8):
                base_v, base_t, base_q = cnt["SV"], cnt["ST"], cnt["QS"]

                @block.sync
                def _(s, tt=tt, base_v=base_v):
                    s.wait_ge(sem["SV"], base_v)
                    dma(s, "QS", xt6[:, :], x_in[HALO + tt * 128: HALO + (tt + 1) * 128, :])

                @block.tensor
                def _(t, tt=tt, base_v=base_v):
                    if tt == 0:
                        t.wait_ge(sem["SS"], cnt["SS"])
                    else:
                        t.wait_ge(sem["SV"], base_v)
                    last = None
                    for nh in range(4):
                        for dc in range(KC):
                            last = t.matmul(pf[:, nh * 512:(nh + 1) * 512],
                                            xqT[:, dc * POS + HALO + tt * 128:
                                                dc * POS + HALO + (tt + 1) * 128],
                                            kq[:, dc * DM + nh * 512: dc * DM + (nh + 1) * 512],
                                            start=(dc == 0), stop=(dc == KC - 1),
                                            skip_group_check=True)
                    inc(last, "ST")

                @block.vector
                def _(v, tt=tt):
                    v.wait_ge(sem["ST"], cnt["ST"])
                    v.wait_ge(sem["QS"], cnt["QS"])
                    inc(v.tensor_tensor(f1[:, :], pf[:, :], bq_rep[:, :], op=ALU.add), "SV")
                    inc(v.tensor_scalar(f1[:, :], f1[:, :], 0.0, None, op0=ALU.max), "SV")
                    inc(v.reduce_max(f0c[:, :], f1[:, :], axis=AX.X), "SV")
                    inc(v.tensor_copy(kab[:, 0:256], f1[:, 0:256]), "SV")
                    inc(v.tensor_scalar(f0c[:, :], f0c[:, :], EPS_AMAX, None, op0=ALU.max), "SV")

                @block.vector
                def _(v, tt=tt):
                    inc(v.tensor_copy(kab[:, 256:512], f1[:, 0:256]), "SV")
                    inc(v.reciprocal(f1c[:, :], f0c[:, :]), "SV")
                    inc(v.tensor_copy(kab[:, 512:768], f1[:, 0:256]), "SV")
                    inc(v.tensor_scalar(f1[:, :], f1[:, :], f1c[:, :], None, op0=ALU.mult), "SV")
                    inc(v.tensor_scalar(f1[:, :], f1[:, :], 127.0, MAGIC,
                                        op0=ALU.mult, op1=ALU.add), "SV")
                    inc(v.tensor_scalar(f1[:, :], f1[:, :], MAGIC + 127.0, MAGIC - 127.0,
                                        op0=ALU.min, op1=ALU.max), "SV")
                    inc(v.tensor_scalar(f1[:, :], f1[:, :], MAGIC, f0c[:, :],
                                        op0=ALU.subtract, op1=ALU.mult), "SV")
                    inc(v.tensor_scalar(f1[:, :], f1[:, :], 1.0 / 127.0, None, op0=ALU.mult), "SV")
                    if tt >= 2:
                        v.wait_ge(sem["QG"], ev_ospill[tt - 2])
                    inc(v.tensor_tensor(obuf[tt % 2][:, :], xt6[:, :], f1[:, :], op=ALU.add), "SV")

                @block.gpsimd
                def _(g, tt=tt):
                    g.wait_ge(sem["SV"], cnt["SV"])
                    dma(g, "QG", o_scr.ap()[tt * 128:(tt + 1) * 128, :], obuf[tt % 2][:, :])
                    ev_ospill[tt] = cnt["QG"]

                if _DEBUG:
                    @block.gpsimd
                    def _(g, tt=tt):
                        g.wait_ge(sem["SV"], cnt["SV"])
                        dma(g, "QG", dbg["ffnq"][tt * 128:(tt + 1) * 128, :], f1[:, :])

        s6b = snap()

    es_xq.close()

    @block.gpsimd
    def _(g):
        wait_all(g, s2)
        with allow_ici_all_to_all():
            inc(g.collective_compute(
                "AllToAll", ALU.bypass,
                replica_groups=[list(range(NCORES))],
                ins=[a2a_in.ap().opt()],
                outs=[a2a_out.ap().opt()],
            ), "SG")

    s3 = snap()

    with ExitStack() as es4:
        dtq = es4.enter_context(nc.sbuf_tensor("dtq", [128, BL * TQW], F32))
        dtldA = es4.enter_context(nc.sbuf_tensor("dtldA", [128, TQW], F32))
        dtldB = es4.enter_context(nc.sbuf_tensor("dtldB", [128, TQW], F32))
        dtuT = es4.enter_context(nc.sbuf_tensor("dtuT", [128, BL * 8 * 128], BF16))
        bcolq = es4.enter_context(nc.sbuf_tensor("bcolq", [128, 128], F32))
        ball = es4.enter_context(nc.sbuf_tensor("ball", [128, TQW], F32))
        dgqA = es4.enter_context(nc.sbuf_tensor("dgqA", [128, 8 * 128], BF16))
        dgqB = es4.enter_context(nc.sbuf_tensor("dgqB", [128, 8 * 128], BF16))
        crep = es4.enter_context(nc.sbuf_tensor("crep", [128, 2 * TQW], F32))
        da = es4.enter_context(nc.sbuf_tensor("da", [128, 2 * 2 * TQW], F32))
        hb = es4.enter_context(nc.sbuf_tensor("hb", [128, 2 * TQW], F32))
        chb0 = es4.enter_context(nc.sbuf_tensor("chb0", [128, TQW], F32))
        chb1 = es4.enter_context(nc.sbuf_tensor("chb1", [128, TQW], F32))
        chb2 = es4.enter_context(nc.sbuf_tensor("chb2", [128, TQW], F32))
        chb3 = es4.enter_context(nc.sbuf_tensor("chb3", [128, TQW], F32))
        yq = es4.enter_context(nc.sbuf_tensor("yq", [128, BL * TQW], F32))
        chbs = [chb0, chb1, chb2, chb3]
        dtld = [dtldA, dtldB]
        dgq = [dgqA, dgqB]
        TE_Y = (0, 1)
        ev_tts = {}
        ev_ch = {}
        ev_ymm = {}
        ev_vy = {}
        ev_dbu = {}
        ev_dg = {}
        ev_evac = {}
        ev_tr = {}
        qg_y = {}

        py0 = es4.enter_context(nc.psum_tensor("py0", [128, TQW], F32))
        py1 = es4.enter_context(nc.psum_tensor("py1", [128, TQW], F32))
        pb0 = es4.enter_context(nc.psum_tensor("pb0", [128, TQW], F32))
        pb1 = es4.enter_context(nc.psum_tensor("pb1", [128, TQW], F32))
        if True:
            pys = {0: py0, 1: py1}
            pdbu = [pb0, pb1]
            ev_bcol = {}

            @block.gpsimd
            def _(g):
                inc(g.memset(ball[:, :], 0.0), "SG")
            sg_ball = cnt["SG"]

            for tq in range(TQ):
                @block.sync
                def _(s, tq=tq):
                    if tq == 0:
                        wait_all(s, s3, skip=("QS", "QC"))
                        s.wait_ge(sem["SG"], sg_ball)
                    else:
                        s.wait_ge(sem["SS"], cnt["SS"])
                        s.wait_ge(sem["SD"], ev_tts[(tq - 1, DST - 1, BL - 1)])
                    for b in range(BL):
                        dma(s, "QS", dtq[:, b * TQW:(b + 1) * TQW],
                            a2a_out.ap()[tq, b * 128:(b + 1) * 128, :])
                    dma(s, "QS", ball[0:DST, :], a2a_out.ap()[tq, 2 * CPC:2 * CPC + DST, :])

                qs_dtq = cnt["QS"]

                @block.tensor
                def _(t, tq=tq):
                    t.wait_ge(sem["QS"], qs_dtq)
                    if tq == 0:
                        t.wait_ge(sem["SG"], cnt["SG"])
                    else:
                        t.wait_ge(sem["SD"], ev_tts[(tq - 1, DST - 1, BL - 1)])
                    for tt in range(8):
                        ins = t.transpose(pdbu[1][:, tt * 128:(tt + 1) * 128],
                                          ball[:, tt * 128:(tt + 1) * 128], idf32[:, :])
                    inc(ins, "ST")
                ev_btr = cnt["ST"]

                @block.scalar
                def _(se, tq=tq, ev_btr=ev_btr):
                    se.wait_ge(sem["ST"], ev_btr)
                    for tt in range(8):
                        se.copy(bcolq[:, tt * DST:(tt + 1) * DST],
                                pdbu[1][:, tt * 128:tt * 128 + DST])
                    sfence(se)
                    ev_bcol[tq] = cnt["SS"]

                for b in range(BL):
                    @block.sync
                    def _(s, tq=tq, b=b):
                        if (tq, b - 2) in ev_tr:
                            s.wait_ge(sem["SS"], ev_tr[(tq, b - 2)])
                        elif tq > 0 and b < 2:
                            s.wait_ge(sem["SS"], ev_tr[(tq - 1, 2 + b)])
                        dma(s, "QS", dtld[b % 2][:, :],
                            a2a_out.ap()[tq, CPC + b * 128:CPC + (b + 1) * 128, :])
                    qs_ld = cnt["QS"]

                    @block.tensor
                    def _(t, tq=tq, b=b, qs_ld=qs_ld):
                        t.wait_ge(sem["QS"], qs_ld)
                        if tq == 0 and b == 0:
                            pass
                        else:
                            pk = (tq, b - 1) if b >= 1 else (tq - 1, BL - 1)
                            if pk in ev_tr:
                                t.wait_ge(sem["SS"], ev_tr[pk])
                        if tq > 0 and b < 2:
                            t.wait_ge(sem["SD"], ev_tts[(tq - 1, DST - 1, BL - 1)])
                        for tt in range(8):
                            inc(t.transpose(pdbu[b % 2][:, tt * 128:(tt + 1) * 128],
                                            dtld[b % 2][:, tt * 128:(tt + 1) * 128],
                                            idf32[:, :]), "ST")

                    @block.scalar
                    def _(se, tq=tq, b=b):
                        se.wait_ge(sem["ST"], cnt["ST"])
                        se.copy(dtuT[:, b * TQW:(b + 1) * TQW], pdbu[b % 2][:, :])
                        sfence(se)
                        ev_tr[(tq, b)] = cnt["SS"]

                for si in range(DST):
                    spar = si % 2

                    @block.sync
                    def _(s, tq=tq, si=si, spar=spar):
                        ptq, psi = (tq, si - 2) if si >= 2 else (tq - 1, DST - 2 + si)
                        if ptq >= 0:
                            s.wait_ge(sem["SV"], ev_ch[(ptq, psi, BL - 1)])
                        dma(s, "QS", crep[:, spar * TQW:(spar + 1) * TQW],
                            a2a_out.ap()[tq, 2 * CPC + DST + si:2 * CPC + DST + si + 1, :]
                            .broadcast_to([128, TQW]))

                    qs_bc = cnt["QS"]

                    @block.scalar
                    def _(se, tq=tq, si=si, spar=spar):
                        ptq, psi = (tq, si - 2) if si >= 2 else (tq - 1, DST - 2 + si)
                        if ptq >= 0:
                            se.wait_ge(sem["ST"], ev_dbu[(ptq, psi, BL - 1)])
                        for tt in range(8):
                            se.activation(dgq[spar][:, tt * 128:(tt + 1) * 128], idb[:, :],
                                          ACTF.Copy, bias=0.0,
                                          scale=bcolq[:, tt * DST + si:tt * DST + si + 1])
                        sfence(se)
                        ev_dg[(tq, si)] = cnt["SS"]

                    if _DEBUG and tq == 0 and si == 0:
                        @block.sync
                        def _(s):
                            s.wait_ge(sem["SS"], ev_dg[(0, 0)])
                            s.wait_ge(sem["SS"], ev_tr[(0, BL - 1)])
                            dma(s, "QS", dbg["bcolq"], bcolq[:, :])
                            dma(s, "QS", dbg["dgq0"], dgq[0][:, :])
                            dma(s, "QS", dbg["dtut"], dtuT[:, :])

                    ev_da = {}

                    def emit_da(pid, tq=tq, si=si):
                        q = si * 2 + pid
                        ppar = q % 2

                        @block.scalar
                        def _(se, tq=tq, si=si, pid=pid, q=q, ppar=ppar):
                            ptq, q2 = (tq, q - 2) if q >= 2 else (tq - 1, 2 * DST - 2 + q)
                            if ptq >= 0:
                                se.wait_ge(sem["SD"], ev_tts[(ptq, q2 // 2, 2 * (q2 % 2) + 1)])
                            for bi in range(2):
                                b = 2 * pid + bi
                                ins = se.activation(
                                    da[:, (ppar * 2 + bi) * TQW:(ppar * 2 + bi + 1) * TQW],
                                    dtq[:, b * TQW:(b + 1) * TQW], ACTF.Exp, bias=0.0,
                                    scale=a_sb[:, b * DST + si:b * DST + si + 1])
                                if bi == 0:
                                    inc(ins, "SS")
                            sfence(se)
                            ev_da[q] = cnt["SS"]

                    def emit_dbu(pid, tq=tq, si=si):
                        @block.tensor
                        def _(t, tq=tq, si=si, pid=pid):
                            t.wait_ge(sem["SS"], ev_dg[(tq, si)])
                            for bi in range(2):
                                b = 2 * pid + bi
                                it = si * BL + b
                                pit = it - 2
                                if pit >= 0:
                                    t.wait_ge(sem["SD"], tq * 64 + pit + 1)
                                else:
                                    t.wait_ge(sem["SS"], ev_tr[(tq, BL - 1)])
                                last = None
                                for tt in range(8):
                                    last = t.matmul(
                                        pdbu[it % 2][:, tt * 128:(tt + 1) * 128],
                                        dtuT[:, b * TQW + tt * 128: b * TQW + (tt + 1) * 128],
                                        dgq[si % 2][:, tt * 128:(tt + 1) * 128],
                                        start=True, stop=True, skip_group_check=True)
                                inc(last, "ST")
                                ev_dbu[(tq, si, b)] = cnt["ST"]

                    def emit_ve(pid, tq=tq, si=si, spar=spar, qs_bc=qs_bc):
                        q = si * 2 + pid
                        ppar = q % 2

                        @block.vector
                        def _(v, tq=tq, si=si, pid=pid, q=q, ppar=ppar, spar=spar):
                            v.wait_ge(sem["SS"], ev_da[q])
                            if pid == 0:
                                v.wait_ge(sem["QS"], qs_bc)
                            for bi in range(2):
                                b = 2 * pid + bi
                                it = si * BL + b
                                col = b * DST + si
                                v.wait_ge(sem["ST"], ev_dbu[(tq, si, b)])
                                if it == 0 and tq > 0:
                                    lcol = (BL - 1) * DST + (DST - 1)
                                    inc(v.tensor_scalar(carry[:, lcol:lcol + 1],
                                                        hb[:, TQW + TQW - 1:2 * TQW],
                                                        1.0, None, op0=ALU.mult), "SV")
                                inc(v.tensor_tensor_scan(
                                    hb[:, (it % 2) * TQW:(it % 2 + 1) * TQW],
                                    da[:, (ppar * 2 + bi) * TQW:(ppar * 2 + bi + 1) * TQW],
                                    pdbu[it % 2][:, :], carry[:, col:col + 1],
                                    op0=ALU.mult, op1=ALU.add), "SD")
                                ev_tts[(tq, si, b)] = cnt["SD"]
                                assert cnt["SD"] == tq * 64 + it + 1
                                pit = it - 1
                                if pit >= 0 and tq < TQ - 1:
                                    psi2, pb2 = pit // BL, pit % BL
                                    pcol = pb2 * DST + psi2
                                    inc(v.tensor_scalar(carry[:, pcol:pcol + 1],
                                                        hb[:, (pit % 2) * TQW + TQW - 1:
                                                           (pit % 2) * TQW + TQW],
                                                        1.0, None, op0=ALU.mult), "SV")
                                if b in TE_Y:
                                    pk = (tq, si - 1, b)
                                    if pk in ev_ymm:
                                        v.wait_ge(sem["ST"], ev_ymm[pk])
                                    elif tq > 0:
                                        v.wait_ge(sem["ST"], ev_ymm[(tq - 1, DST - 1, b)])
                                cb = chbs[b]
                                inc(v.tensor_tensor(cb[:, :],
                                                    hb[:, (it % 2) * TQW:(it % 2 + 1) * TQW],
                                                    crep[:, spar * TQW:(spar + 1) * TQW],
                                                    op=ALU.mult), "SV")
                                ev_ch[(tq, si, b)] = cnt["SV"]
                                if b not in TE_Y:
                                    if si == 0 and tq > 0 and b == 2:
                                        v.wait_ge(sem["QG"], qg_y[tq - 1])
                                    if si == 0:
                                        inc(v.tensor_copy(yq[:, b * TQW:(b + 1) * TQW],
                                                          cb[:, :]), "SV")
                                    else:
                                        inc(v.tensor_tensor(yq[:, b * TQW:(b + 1) * TQW],
                                                            yq[:, b * TQW:(b + 1) * TQW],
                                                            cb[:, :], op=ALU.add), "SV")
                                    ev_vy[(tq, si, b)] = cnt["SV"]

                    def emit_ymm(pid, tq=tq, si=si):
                        @block.tensor
                        def _(t, tq=tq, si=si, pid=pid):
                            for bi in range(2):
                                b = 2 * pid + bi
                                if b not in TE_Y:
                                    continue
                                t.wait_ge(sem["SV"], ev_ch[(tq, si, b)])
                                if si == 0 and tq > 0:
                                    t.wait_ge(sem["SS"], ev_evac[tq - 1])
                                cb = chbs[b]
                                t.matmul(pys[b][:, 0:512], idf32[:, :], cb[:, 0:512],
                                         start=(si == 0), stop=(si == DST - 1),
                                         skip_group_check=True)
                                inc(t.matmul(pys[b][:, 512:TQW], idf32[:, :], cb[:, 512:TQW],
                                             start=(si == 0), stop=(si == DST - 1),
                                             skip_group_check=True), "ST")
                                ev_ymm[(tq, si, b)] = cnt["ST"]

                    emit_da(0)
                    emit_da(1)
                    emit_dbu(0)
                    emit_dbu(1)
                    emit_ve(0)
                    emit_ymm(0)
                    emit_ve(1)
                    emit_ymm(1)

                @block.scalar
                def _(s, tq=tq):
                    s.wait_ge(sem["ST"], ev_ymm[(tq, DST - 1, 1)])
                    if tq > 0:
                        s.wait_ge(sem["QG"], qg_y[tq - 1])
                    for b in TE_Y:
                        ins = s.copy(yq[:, b * TQW:(b + 1) * TQW], pys[b][:, :])
                        if b == 0:
                            inc(ins, "SS")
                    sfence(s)
                    ev_evac[tq] = cnt["SS"]

                @block.gpsimd
                def _(g, tq=tq):
                    g.wait_ge(sem["SS"], ev_evac[tq])
                    for b in range(BL):
                        if b not in TE_Y:
                            g.wait_ge(sem["SV"], ev_vy[(tq, DST - 1, b)])
                    for b in range(BL):
                        dma(g, "QG", y_in.ap()[tq, b * 128:(b + 1) * 128, :],
                            yq[:, b * TQW:(b + 1) * TQW])
                qg_y[tq] = cnt["QG"]

    s4 = snap()

    @block.gpsimd
    def _(g):
        wait_all(g, s4)
        with allow_ici_all_to_all():
            inc(g.collective_compute(
                "AllToAll", ALU.bypass,
                replica_groups=[list(range(NCORES))],
                ins=[y_in.ap().opt()],
                outs=[y_out.ap().opt()],
            ), "SG")

    s5 = snap()


    with (
        nc.sbuf_tensor("out_acc", [128, 8 * DM], F32) as out_acc,
        nc.sbuf_tensor("yg", [128, DCH * TOK], BF16) as yg,
        nc.sbuf_tensor("ycl", [128, TOK], F32) as ycl,
        nc.sbuf_tensor("ucl", [128, TOK], F32) as ucl,
        nc.sbuf_tensor("szl", [128, TOK], F32) as szl,
        nc.sbuf_tensor("gt", [128, TOK], F32) as gt,
        nc.sbuf_tensor("dcl", [128, 1], F32) as dcl,
        nc.sbuf_tensor("wovA", [128, 256], F32) as wovA,
        nc.sbuf_tensor("wovB", [128, 256], F32) as wovB,
        nc.sbuf_tensor("wobA", [128, 256], BF16) as wobA,
        nc.sbuf_tensor("wobB", [128, 256], BF16) as wobB,
        nc.sbuf_tensor("mst", [128, 2 * TOK], F32) as mst,
        nc.psum_tensor("pw0", [128, TOK], F32) as pw0,
        nc.psum_tensor("pw1", [128, TOK], F32) as pw1,
        nc.psum_tensor("ptm", [128, 128], F32) as ptm,
        nc.psum_tensor("ptm2", [128, 128], F32) as ptm2,
        nc.sbuf_tensor("idf", [128, 128], F32) as idf,
    ):
        pws = [pw0, pw1]

        @block.vector
        def _(v):
            inc(v.tensor_copy(idf[:, :], idb[:, :]), "SV")

        @block.sync
        def _(s):
            wait_all(s, s5, skip=("QS", "QC"))
            for tt in range(8):
                dma(s, "QS", out_acc[:, tt * DM:(tt + 1) * DM],
                    o_scr.ap()[tt * 128:(tt + 1) * 128, :])
        qs_oacc = cnt["QS"]

        @block.vector
        def _(v):
            v.wait_ge(sem["QS"], qs_oacc)
            inc(v.engine_nop(), "SV")
        sv_oacc = cnt["SV"]

        for c in range(DCH):
            base_v, base_q, base_c = cnt["SV"], cnt["QS"], cnt["QC"]

            @block.sync
            def _(s, c=c, base_v=base_v):
                if c == 0:
                    wait_all(s, s5, skip=("QS", "QC", "QG"))
                else:
                    s.wait_ge(sem["SV"], base_v)
                j, r = c // BL, (c % BL) * 128
                dma(s, "QS", ycl[:, :], y_out.ap()[j, r:r + 128, :])
                dma(s, "QS", ucl[:, :], u_scr.ap()[c * 128:(c + 1) * 128, :])
                dma(s, "QS", szl[:, :], sz_scr.ap()[c * 128:(c + 1) * 128, HALO:HALO + TOK])
                dma(s, "QS", dcl[:, :], d_vec[c * 128:(c + 1) * 128, :])

            @block.vector
            def _(v, c=c):
                v.wait_ge(sem["QS"], cnt["QS"])
                inc(v.scalar_tensor_tensor(gt[:, :], ucl[:, :], dcl[:, :], ycl[:, :],
                                           op0=ALU.mult, op1=ALU.add), "SV")
                inc(v.tensor_tensor(yg[:, c * TOK:(c + 1) * TOK], gt[:, :], szl[:, :],
                                    op=ALU.mult), "SV")

            if _DEBUG:
                @block.gpsimd
                def _(g, c=c):
                    g.wait_ge(sem["SV"], cnt["SV"] - 1)
                    dma(g, "QG", dbg["yown"][c * 128:(c + 1) * 128, :], gt[:, :])

        wovs = [wovA, wovB]
        wobs = [wobA, wobB]
        ev_wld = {}
        ev_wcast = {}
        ev_wmm = {}
        ev_wevac = {}
        ev_wadd = {}
        ev_wtr = {}
        for mq in range(8):
            for dc in range(DCH):
                gi = mq * DCH + dc
                wpar = gi % 2
                Q = "QS" if wpar == 0 else "QC"

                @block.sync if wpar == 0 else block.scalar
                def _(s, mq=mq, dc=dc, gi=gi, wpar=wpar, Q=Q):
                    if gi - 2 in ev_wcast:
                        s.wait_ge(sem["SV"], ev_wcast[gi - 2])
                    dma(s, Q, wovs[wpar][:, :],
                        w_out_t[dc * 128:(dc + 1) * 128, mq * 256:(mq + 1) * 256])
                ev_wld[gi] = cnt["QS" if wpar == 0 else "QC"]

                @block.vector
                def _(v, gi=gi, wpar=wpar, Q=Q):
                    v.wait_ge(sem[Q], ev_wld[gi])
                    if gi - 2 in ev_wmm:
                        v.wait_ge(sem["ST"], ev_wmm[gi - 2])
                    inc(v.tensor_copy(wobs[wpar][:, :], wovs[wpar][:, :]), "SV")
                ev_wcast[gi] = cnt["SV"]

                @block.tensor
                def _(t, mq=mq, dc=dc, gi=gi, wpar=wpar):
                    t.wait_ge(sem["SV"], ev_wcast[gi])
                    if dc == 0 and mq >= 1:
                        t.wait_ge(sem["SV"], ev_wevac[mq - 1])
                    last = None
                    for j in range(2):
                        for nh in range(2):
                            sl = slice(nh * 512, (nh + 1) * 512)
                            last = t.matmul(pws[j][:, sl], wobs[wpar][:, j * 128:(j + 1) * 128],
                                            yg[:, dc * TOK + nh * 512: dc * TOK + (nh + 1) * 512],
                                            start=(dc == 0), stop=(dc == DCH - 1),
                                            skip_group_check=True)
                    inc(last, "ST")
                ev_wmm[gi] = cnt["ST"]

            @block.vector
            def _(v, mq=mq):
                v.wait_ge(sem["ST"], ev_wmm[mq * DCH + DCH - 1])
                if mq >= 1:
                    v.wait_ge(sem["ST"], ev_wtr[(mq - 1, 1, 7)])
                for j in range(2):
                    inc(v.tensor_copy(mst[:, j * TOK:(j + 1) * TOK], pws[j][:, :]), "SV")
            ev_wevac[mq] = cnt["SV"]

            @block.tensor
            def _(t, mq=mq):
                t.wait_ge(sem["SV"], ev_wevac[mq])
                for j in range(2):
                    for tt in range(8):
                        k = j * 8 + tt
                        pk = (mq, j, tt - 2) if tt >= 2 else                              ((mq, j - 1, 6 + tt) if j == 1 else
                              ((mq - 1, 1, 6 + tt) if mq >= 1 else None))
                        if pk is not None and pk in ev_wadd:
                            t.wait_ge(sem["SV"], ev_wadd[pk])
                        ptx = ptm if k % 2 == 0 else ptm2
                        inc(t.transpose(ptx[:, :],
                                        mst[:, j * TOK + tt * 128: j * TOK + (tt + 1) * 128],
                                        idf[:, :]), "ST")
                        ev_wtr[(mq, j, tt)] = cnt["ST"]

                        @block.vector
                        def _(v, mq=mq, j=j, tt=tt, ptx=ptx):
                            v.wait_ge(sem["ST"], ev_wtr[(mq, j, tt)])
                            if mq == 0 and j == 0 and tt == 0:
                                v.wait_ge(sem["QS"], qs_oacc)
                            col = tt * DM + mq * 256 + j * 128
                            inc(v.tensor_tensor(out_acc[:, col:col + 128],
                                                out_acc[:, col:col + 128],
                                                ptx[:, :], op=ALU.add), "SV")
                            ev_wadd[(mq, j, tt)] = cnt["SV"]

        @block.gpsimd
        def _(g):
            g.wait_ge(sem["SV"], cnt["SV"])
            for tt in range(8):
                dma(g, "QG", out_d[tt * 128:(tt + 1) * 128, :],
                    out_acc[:, tt * DM:(tt + 1) * DM])
            g.wait_ge(sem["QG"], cnt["QG"])

    es.close()
    return nc


_NC_CACHE = None


def make_in_maps(L0_attn_x, L0_attn_W_in, L0_attn_conv_w, L0_attn_conv_b, L0_attn_W_x,
                 L0_attn_W_dt, L0_attn_b_dt, L0_attn_A_log, L0_attn_D, L0_attn_W_out,
                 L0_ffn_kernel, L0_ffn_bias):
    x = np.ascontiguousarray(np.asarray(L0_attn_x, dtype=np.float32).reshape(8192, DM))
    w_in = np.asarray(L0_attn_W_in, np.float32)
    w_in_4 = np.ascontiguousarray(w_in.reshape(2 * DCH, 128, KC, 128).transpose(3, 0, 2, 1))
    conv_w = np.ascontiguousarray(np.asarray(L0_attn_conv_w, np.float32))
    conv_b = np.ascontiguousarray(np.asarray(L0_attn_conv_b, np.float32).reshape(DIN, 1))
    w_x_t = np.ascontiguousarray(np.asarray(L0_attn_W_x, np.float32).T)
    w_dt_t = np.ascontiguousarray(np.asarray(L0_attn_W_dt, np.float32).T)
    b_dt_a = np.ascontiguousarray(np.asarray(L0_attn_b_dt, np.float32).reshape(DIN, 1))
    a_log = np.ascontiguousarray(np.asarray(L0_attn_A_log, np.float32))
    d_vec = np.ascontiguousarray(np.asarray(L0_attn_D, np.float32).reshape(DIN, 1))
    w_out_t = np.ascontiguousarray(np.asarray(L0_attn_W_out, np.float32).T)
    ffn_k = np.ascontiguousarray(np.asarray(L0_ffn_kernel, np.float32))
    ffn_k_t = np.ascontiguousarray(ffn_k.T)
    ffn_b = np.ascontiguousarray(np.asarray(L0_ffn_bias, np.float32).reshape(1, DM))

    in_maps = []
    for i in range(NCORES):
        lo = i * TOK
        xs = np.zeros((POS, DM), np.float32)
        if i > 0:
            xs[0:HALO] = x[lo - HALO:lo]
        xs[HALO:HALO + TOK] = x[lo:lo + TOK]
        in_maps.append({
            "x": xs,
            "w_in_4": w_in_4,
            "conv_w": conv_w,
            "conv_b": conv_b,
            "w_x_t": w_x_t,
            "w_dt_t": w_dt_t,
            "b_dt": b_dt_a,
            "a_log_sl": np.ascontiguousarray(a_log[i * CPC:(i + 1) * CPC]),
            "d_vec": d_vec,
            "w_out_t": w_out_t,
            "ffn_k_t": ffn_k_t,
            "ffn_k": ffn_k,
            "ffn_b": ffn_b,
        })
    return in_maps


def kernel(L0_attn_x, L0_attn_W_in, L0_attn_conv_w, L0_attn_conv_b, L0_attn_W_x,
           L0_attn_W_dt, L0_attn_b_dt, L0_attn_A_log, L0_attn_D, L0_attn_W_out,
           L0_ffn_kernel, L0_ffn_bias):
    global _NC_CACHE
    in_maps = make_in_maps(L0_attn_x, L0_attn_W_in, L0_attn_conv_w, L0_attn_conv_b,
                           L0_attn_W_x, L0_attn_W_dt, L0_attn_b_dt, L0_attn_A_log,
                           L0_attn_D, L0_attn_W_out, L0_ffn_kernel, L0_ffn_bias)
    if _NC_CACHE is None:
        _NC_CACHE = build_nc()
    res = run_bass_kernel_spmd(_NC_CACHE, in_maps, core_ids=list(range(NCORES)))
    out = np.concatenate([res.results[i]["out"] for i in range(NCORES)], axis=0)
    return out.reshape(1, 8192, DM).astype(np.float32)


# version 14
# speedup vs baseline: 1.0758x; 1.0121x over previous
import numpy as np
import concourse.bass as bass
import concourse.mybir as mybir
from concourse.bass_utils import run_bass_kernel_spmd
from concourse.aot_env import allow_ici_all_to_all
from contextlib import ExitStack

F32 = mybir.dt.float32
BF16 = mybir.dt.bfloat16
I32 = mybir.dt.int32
ALU = mybir.AluOpType
ACTF = mybir.ActivationFunctionType
AX = mybir.AxisListType

NCORES = 8
TOK = 1024
HALO = 3
POS = 1152
NT = POS // 128
DM = 2048
DIN = 4096
DST = 16
DTR = 128
DCH = DIN // 128
KC = DM // 128
CPC = DIN // NCORES
BL = CPC // 128
TQ = 8
TQW = 1024
MAGIC = 12582912.0
EPS_AMAX = 1e-18
LN127 = 4.844187086458591

_DEBUG = False


def build_nc():
    nc = bass.Bass("TRN2", target_bir_lowering=False)

    x_in = nc.dram_tensor("x", [POS, DM], F32, kind="ExternalInput").ap()
    w_in_4 = nc.dram_tensor("w_in_4", [128, 2 * DCH, KC, 128], F32, kind="ExternalInput").ap()
    conv_w = nc.dram_tensor("conv_w", [DIN, 4], F32, kind="ExternalInput").ap()
    conv_b = nc.dram_tensor("conv_b", [DIN, 1], F32, kind="ExternalInput").ap()
    w_x_t = nc.dram_tensor("w_x_t", [DIN, 160], F32, kind="ExternalInput").ap()
    w_dt_t = nc.dram_tensor("w_dt_t", [DTR, DIN], F32, kind="ExternalInput").ap()
    b_dt = nc.dram_tensor("b_dt", [DIN, 1], F32, kind="ExternalInput").ap()
    a_log_sl = nc.dram_tensor("a_log_sl", [CPC, DST], F32, kind="ExternalInput").ap()
    d_vec = nc.dram_tensor("d_vec", [DIN, 1], F32, kind="ExternalInput").ap()
    w_out_t = nc.dram_tensor("w_out_t", [DIN, DM], F32, kind="ExternalInput").ap()
    ffn_k_t = nc.dram_tensor("ffn_k_t", [DM, DM], F32, kind="ExternalInput").ap()
    ffn_k = nc.dram_tensor("ffn_k", [DM, DM], F32, kind="ExternalInput").ap()
    ffn_b = nc.dram_tensor("ffn_b", [1, DM], F32, kind="ExternalInput").ap()
    out_d = nc.dram_tensor("out", [TOK, DM], F32, kind="ExternalOutput").ap()

    ROWS_SH = 2 * CPC + 2 * DST
    a2a_in = nc.dram_tensor("a2a_in", [NCORES, ROWS_SH, TOK], F32)
    a2a_out = nc.dram_tensor("a2a_out", [NCORES, ROWS_SH, TOK], F32)
    y_in = nc.dram_tensor("y_in", [NCORES, CPC, TOK], F32)
    y_out = nc.dram_tensor("y_out", [NCORES, CPC, TOK], F32)
    u_scr = nc.dram_tensor("u_scr", [DIN, TOK], F32)
    sz_scr = nc.dram_tensor("sz_scr", [DIN, POS], F32)
    bq_scr = nc.dram_tensor("bq_scr", [1, DM], F32)
    o_scr = nc.dram_tensor("o_scr", [TOK, DM], F32)

    dbg = {}
    if _DEBUG:
        dbg["xn"] = nc.dram_tensor("dbg_xn", [POS, DM], F32, kind="ExternalOutput").ap()
        dbg["xs"] = nc.dram_tensor("dbg_xs", [DIN, POS], F32, kind="ExternalOutput").ap()
        dbg["xdbl"] = nc.dram_tensor("dbg_xdbl", [160, TOK], F32, kind="ExternalOutput").ap()
        dbg["dt"] = nc.dram_tensor("dbg_dt", [DIN, TOK], F32, kind="ExternalOutput").ap()
        dbg["yown"] = nc.dram_tensor("dbg_yown", [DIN, TOK], F32, kind="ExternalOutput").ap()
        dbg["ffnq"] = nc.dram_tensor("dbg_ffnq", [TOK, DM], F32, kind="ExternalOutput").ap()
        dbg["xqt"] = nc.dram_tensor("dbg_xqt", [128, KC * POS], BF16, kind="ExternalOutput").ap()
        dbg["xqt_e"] = nc.dram_tensor("dbg_xqt_e", [128, KC * POS], BF16, kind="ExternalOutput").ap()
        dbg["xqb"] = nc.dram_tensor("dbg_xqb", [POS, DM], BF16, kind="ExternalOutput").ap()
        dbg["qd"] = nc.dram_tensor("dbg_qd", [128, 4], F32, kind="ExternalOutput").ap()
        dbg["sq0"] = nc.dram_tensor("dbg_sq0", [128, DM], F32, kind="ExternalOutput").ap()
        dbg["kq"] = nc.dram_tensor("dbg_kq", [128, KC * DM], BF16, kind="ExternalOutput").ap()
        dbg["bcolq"] = nc.dram_tensor("dbg_bcolq", [128, 128], F32, kind="ExternalOutput").ap()
        dbg["dgq0"] = nc.dram_tensor("dbg_dgq0", [128, 1024], BF16, kind="ExternalOutput").ap()
        dbg["dtut"] = nc.dram_tensor("dbg_dtut", [128, 4096], BF16, kind="ExternalOutput").ap()
        dbg["dbu0"] = nc.dram_tensor("dbg_dbu0", [128, 1024], F32, kind="ExternalOutput").ap()

    cnt = {k: 0 for k in ("QS", "QC", "QG", "SV", "SS", "ST", "SG", "SD")}
    es = ExitStack()
    sem = {k: es.enter_context(nc.semaphore(k)) for k in cnt}

    idb = es.enter_context(nc.sbuf_tensor("idb", [128, 128], BF16))
    a_sb = es.enter_context(nc.sbuf_tensor("a_sb", [128, BL * DST], F32))
    carry = es.enter_context(nc.sbuf_tensor("carry", [128, BL * DST], F32))
    ln127c = es.enter_context(nc.sbuf_tensor("ln127c", [128, 1], F32))
    idf32 = es.enter_context(nc.sbuf_tensor("idf32", [128, 128], F32))

    block = es.enter_context(nc.Block())

    def snap():
        return dict(cnt)

    def wait_all(eng, snp, skip=()):
        for k, v in snp.items():
            if v > 0 and k not in skip:
                eng.wait_ge(sem[k], v)

    def dma(eng, q, out, in_):
        eng.dma_start(out=out, in_=in_).then_inc(sem[q], 16)
        cnt[q] += 16

    def inc(instr, name, n=1):
        instr.then_inc(sem[name], n)
        cnt[name] += n
        return instr

    def maxtree(v, buf, width, out_col):
        w = width // 2
        while w >= 1:
            inc(v.tensor_tensor(buf[:, 0:w], buf[:, 0:w], buf[:, w:2 * w], op=ALU.max), "SV")
            w //= 2
        inc(v.tensor_scalar(out_col, buf[:, 0:1], EPS_AMAX, None, op0=ALU.max), "SV")

    def preload_ln_exp_set(eng):
        from concourse.hw_specs import get_activation_tables
        for idx, funcs in enumerate(get_activation_tables(nc.m.arch).values()):
            if ACTF.Exp in funcs and ACTF.Ln in funcs:
                eng.add_instruction(mybir.InstLoadActFuncSet(
                    name=nc.get_next_instruction_name(),
                    act_func_set_id=idx, ins=[], outs=[]))
                return
        raise ValueError("no set with Exp+Ln")

    def sfence(s):
        s.drain()
        inc(s.nop(), "SS")

    with (
        nc.sbuf_tensor("pi", [128, 128], I32) as pi,
        nc.sbuf_tensor("fi", [128, 128], I32) as fi,
        nc.sbuf_tensor("idf0", [128, 128], F32) as idf0,
        nc.sbuf_tensor("alog", [128, BL * DST], F32) as alog,
    ):
        @block.gpsimd
        def _(g):
            inc(g.iota(pi[:, :], [[0, 128]], base=0, channel_multiplier=1), "SG")
            inc(g.iota(fi[:, :], [[1, 128]], base=0, channel_multiplier=0), "SG")

        @block.sync
        def _(s):
            for b in range(BL):
                dma(s, "QS", alog[:, b * DST:(b + 1) * DST],
                    a_log_sl[b * 128:(b + 1) * 128, :])

        @block.vector
        def _(v):
            v.wait_ge(sem["SG"], cnt["SG"])
            inc(v.memset(ln127c[:, :], LN127), "SV")
            inc(v.tensor_tensor(idf32[:, :], pi[:, :], fi[:, :], op=ALU.is_equal), "SV")
            inc(v.tensor_copy(idb[:, :], idf32[:, :]), "SV")
            inc(v.memset(carry[:, :], 0.0), "SV")

        @block.scalar
        def _(s):
            s.wait_ge(sem["QS"], cnt["QS"])
            s.activation(a_sb[:, :], alog[:, :], ACTF.Exp, bias=0.0, scale=1.0)
            sfence(s)

        @block.vector
        def _(v):
            v.wait_ge(sem["SS"], cnt["SS"])
            inc(v.tensor_scalar(a_sb[:, :], a_sb[:, :], -1.0, None, op0=ALU.mult), "SV")

        s0 = snap()

    es_xq = ExitStack()
    xqT = es_xq.enter_context(nc.sbuf_tensor("xqT", [128, KC * POS], BF16))
    es1 = ExitStack()
    xnT = es1.enter_context(nc.sbuf_tensor("xnT", [128, KC * POS], BF16))
    with (
        nc.sbuf_tensor("xt", [128, DM], F32) as xt,
        nc.sbuf_tensor("xn_f", [128, DM], F32) as xn_f,
        nc.sbuf_tensor("xn_b", [128, DM], BF16) as xn_b,
        nc.sbuf_tensor("xq_b", [128, DM], BF16) as xq_b,
        nc.sbuf_tensor("sq", [128, DM], F32) as sq,
        nc.sbuf_tensor("c0", [128, 1], F32) as c0,
        nc.sbuf_tensor("c1", [128, 1], F32) as c1,
        nc.sbuf_tensor("c2", [128, 1], F32) as c2,
        nc.sbuf_tensor("c3", [128, 1], F32) as c3,
        nc.sbuf_tensor("dbgq", [128, 4], F32) as dbgq,
        nc.sbuf_tensor("spc", [128, 256], F32) as spc,
        nc.psum_tensor("ptr", [128, 128], BF16) as ptr,
        nc.psum_tensor("ptr2", [128, 128], BF16) as ptr2,
        nc.psum_tensor("ptr3", [128, 128], BF16) as ptr3,
        nc.psum_tensor("ptr4", [128, 128], BF16) as ptr4,
    ):
        ptrs = [(ptr, ptr2), (ptr3, ptr4)]
        ev_s1evac = {}
        for tt in range(NT):
            base_v, base_s, base_t, base_q = cnt["SV"], cnt["SS"], cnt["ST"], cnt["QS"]

            @block.sync
            def _(s, tt=tt):
                if tt == 0:
                    wait_all(s, s0, skip=("QS",))
                else:
                    s.wait_ge(sem["ST"], base_t)
                dma(s, "QS", xt[:, :], x_in[tt * 128:(tt + 1) * 128, :])

            @block.scalar
            def _(s, tt=tt):
                s.wait_ge(sem["QS"], cnt["QS"])
                s.activation(sq[:, :], xt[:, :], ACTF.Square, bias=0.0, scale=1.0)
                sfence(s)

            @block.vector
            def _(v, tt=tt):
                v.wait_ge(sem["QS"], cnt["QS"])
                inc(v.reduce_sum(c0[:, :], xt[:, :], axis=AX.X), "SV")
                v.wait_ge(sem["SS"], base_s + 1)
                inc(v.reduce_sum(c1[:, :], sq[:, :], axis=AX.X), "SV")
                inc(v.tensor_scalar(c0[:, :], c0[:, :], 1.0 / DM, None, op0=ALU.mult), "SV")
                inc(v.tensor_scalar(c1[:, :], c1[:, :], 1.0 / DM, 1e-5,
                                    op0=ALU.mult, op1=ALU.add), "SV")
                inc(v.tensor_copy(spc[:, :], xt[:, 0:256]), "SV")
                inc(v.tensor_tensor(c2[:, :], c0[:, :], c0[:, :], op=ALU.mult), "SV")
                inc(v.tensor_copy(spc[:, :], xt[:, 256:512]), "SV")
                inc(v.tensor_tensor(c1[:, :], c1[:, :], c2[:, :], op=ALU.subtract), "SV")

            @block.scalar
            def _(s, tt=tt):
                s.wait_ge(sem["SV"], base_v + 8)
                inc(s.activation(c1[:, :], c1[:, :], ACTF.Ln, bias=0.0, scale=1.0), "SS")
                s.activation(c1[:, :], c1[:, :], ACTF.Exp, bias=0.0, scale=-0.5)
                sfence(s)

            @block.vector
            def _(v, tt=tt):
                v.wait_ge(sem["SS"], base_s + 3)
                inc(v.tensor_scalar(xn_f[:, :], xt[:, :], c0[:, :], c1[:, :],
                                    op0=ALU.subtract, op1=ALU.mult), "SV")
                inc(v.tensor_scalar(sq[:, :], xn_f[:, :], -1.0, None, op0=ALU.mult), "SV")
                inc(v.tensor_tensor(sq[:, :], sq[:, :], xn_f[:, :], op=ALU.max), "SV")
                inc(v.reduce_max(c2[:, :], sq[:, :], axis=AX.X), "SV")
                inc(v.tensor_copy(xn_b[:, :], xn_f[:, :]), "SV")
                inc(v.tensor_scalar(c2[:, :], c2[:, :], EPS_AMAX, None, op0=ALU.max), "SV")
                inc(v.tensor_copy(spc[:, :], xn_f[:, 0:256]), "SV")
                inc(v.reciprocal(c3[:, :], c2[:, :]), "SV")
                inc(v.tensor_copy(spc[:, :], xn_f[:, 256:512]), "SV")
                inc(v.tensor_scalar(sq[:, :], xn_f[:, :], c3[:, :], None, op0=ALU.mult), "SV")
                inc(v.tensor_scalar(sq[:, :], sq[:, :], 127.0, MAGIC,
                                    op0=ALU.mult, op1=ALU.add), "SV")
                inc(v.tensor_scalar(sq[:, :], sq[:, :], MAGIC + 127.0, MAGIC - 127.0,
                                    op0=ALU.min, op1=ALU.max), "SV")
                inc(v.tensor_scalar(sq[:, :], sq[:, :], MAGIC, c2[:, :],
                                    op0=ALU.subtract, op1=ALU.mult), "SV")
                inc(v.tensor_scalar(xq_b[:, :], sq[:, :], 1.0 / 127.0, None, op0=ALU.mult), "SV")

            if _DEBUG:
                @block.gpsimd
                def _(g, tt=tt):
                    g.wait_ge(sem["SV"], base_v + 10)
                    dma(g, "QG", dbg["xn"][tt * 128:(tt + 1) * 128, :], xn_f[:, :])
                    g.wait_ge(sem["SV"], cnt["SV"])
                    dma(g, "QG", dbg["xqb"][tt * 128:(tt + 1) * 128, :], xq_b[:, :])

            @block.tensor
            def _(t, tt=tt):
                t.wait_ge(sem["SV"], cnt["SV"])
                for kcc in range(KC):
                    pa, pb = ptrs[kcc % 2]
                    gi = tt * KC + kcc - 2
                    if gi >= 0 and gi in ev_s1evac:
                        t.wait_ge(sem["SS"], ev_s1evac[gi])
                    t.transpose(pa[:, :], xn_b[:, kcc * 128:(kcc + 1) * 128], idb[:, :])
                    inc(t.transpose(pb[:, :], xq_b[:, kcc * 128:(kcc + 1) * 128], idb[:, :]), "ST")

                    @block.scalar
                    def _(s, tt=tt, kcc=kcc, pa=pa, pb=pb):
                        s.wait_ge(sem["ST"], cnt["ST"])
                        inc(s.copy(xnT[:, kcc * POS + tt * 128: kcc * POS + (tt + 1) * 128],
                                   pa[:, :]), "SS")
                        inc(s.copy(xqT[:, kcc * POS + tt * 128: kcc * POS + (tt + 1) * 128],
                                   pb[:, :]), "SS")
                        ev_s1evac[tt * KC + kcc] = cnt["SS"]

        if _DEBUG:
            @block.gpsimd
            def _(g):
                g.wait_ge(sem["SS"], cnt["SS"])
                dma(g, "QG", dbg["xqt_e"], xqT[:, :])

        s1 = snap()

    es2 = ExitStack()
    _e2 = es2.enter_context
    wsl = [_e2(nc.sbuf_tensor("wslA", [128, DM], F32)),
           _e2(nc.sbuf_tensor("wslB", [128, DM], F32))]
    wslb = [_e2(nc.sbuf_tensor("wslbA", [128, DM], BF16)),
            _e2(nc.sbuf_tensor("wslbB", [128, DM], BF16))]
    xs_sb = [_e2(nc.sbuf_tensor("xs_sbA", [128, POS], F32)),
             _e2(nc.sbuf_tensor("xs_sbB", [128, POS], F32))]
    zt = [_e2(nc.sbuf_tensor("ztA", [128, POS], F32)),
          _e2(nc.sbuf_tensor("ztB", [128, POS], F32))]
    cv1 = [_e2(nc.sbuf_tensor("cv1A", [128, TOK], F32)),
           _e2(nc.sbuf_tensor("cv1B", [128, TOK], F32))]
    cv2 = [_e2(nc.sbuf_tensor("cv2A", [128, TOK], F32)),
           _e2(nc.sbuf_tensor("cv2B", [128, TOK], F32))]
    u_f = [_e2(nc.sbuf_tensor("u_fA", [128, TOK], F32)),
           _e2(nc.sbuf_tensor("u_fB", [128, TOK], F32))]
    cwc = [_e2(nc.sbuf_tensor("cwcA", [128, 4], F32)),
           _e2(nc.sbuf_tensor("cwcB", [128, 4], F32))]
    cbc = [_e2(nc.sbuf_tensor("cbcA", [128, 1], F32)),
           _e2(nc.sbuf_tensor("cbcB", [128, 1], F32))]
    wx_f = _e2(nc.sbuf_tensor("wx_f", [128, DCH * 160], F32))
    wdt_b = _e2(nc.sbuf_tensor("wdt_b", [128, DIN], BF16))
    dtr_b = _e2(nc.sbuf_tensor("dtr_b", [128, TOK], BF16))
    bc_sb = _e2(nc.sbuf_tensor("bc_sb", [32, TOK], F32))
    dt_sb = [_e2(nc.sbuf_tensor("dt_sbA", [128, TOK], F32)),
             _e2(nc.sbuf_tensor("dt_sbB", [128, TOK], F32))]
    et = _e2(nc.sbuf_tensor("et", [128, TOK], F32))
    bdtc = [_e2(nc.sbuf_tensor("bdtcA", [128, 1], F32)),
            _e2(nc.sbuf_tensor("bdtcB", [128, 1], F32))]
    uld = [_e2(nc.sbuf_tensor("uldA", [128, TOK], F32)),
           _e2(nc.sbuf_tensor("uldB", [128, TOK], F32))]
    dtu_sb = [_e2(nc.sbuf_tensor("dtu_sbA", [128, TOK], F32)),
              _e2(nc.sbuf_tensor("dtu_sbB", [128, TOK], F32))]
    pp = _e2(nc.psum_tensor("pp", [128, 512], F32))
    pq = _e2(nc.psum_tensor("pq", [128, 512], F32))
    pr = _e2(nc.psum_tensor("pr", [128, 512], F32))
    pd0 = _e2(nc.psum_tensor("pd0", [128, 512], F32))
    px1 = _e2(nc.psum_tensor("px1", [128, TOK], F32))
    px2 = _e2(nc.psum_tensor("px2", [32, TOK], F32))
    if True:
        @block.sync
        def _(s):
            wait_all(s, s1, skip=("QS",))
            dma(s, "QS", wx_f.ap().rearrange("p (c f) -> p c f", c=DCH),
                w_x_t.rearrange("(c p) f -> p c f", p=128))
            for k, stg in enumerate([uld[0], uld[1], cv1[0], cv1[1]]):
                dma(s, "QS", stg[:, :], w_dt_t[:, k * 1024:(k + 1) * 1024])

        for k, stg in enumerate([uld[0], uld[1], cv1[0], cv1[1]]):
            @block.vector
            def _(v, k=k, stg=stg):
                v.wait_ge(sem["QS"], cnt["QS"])
                inc(v.tensor_copy(wdt_b[:, k * 1024:(k + 1) * 1024], stg[:, :]), "SV")

        ev_cast = {}
        ev_mm = {}
        ev_evac = {}
        ev_conv = {}
        ev_ub = {}
        ev_zspill = {}
        ev_silu = {}

        NSLICES = [(0, 512), (512, 512), (1024, 128)]

        for c in range(2 * DCH):
            is_xs = c < DCH
            par = c % 2
            Q = "QS" if par == 0 else "QC"

            @block.sync if par == 0 else block.scalar
            def _(s, c=c, par=par, Q=Q, is_xs=is_xs):
                if c - 2 in ev_cast:
                    s.wait_ge(sem["SV"], ev_cast[c - 2])
                if is_xs and c - 2 >= 0:
                    s.wait_ge(sem["SV"], ev_conv[c - 2])
                dma(s, Q, wsl[par].ap().rearrange("p (kc ch) -> p kc ch", kc=KC),
                    w_in_4[:, c, :, :])
                if is_xs:
                    dma(s, Q, cwc[par][:, :], conv_w[c * 128:(c + 1) * 128, :])
                    dma(s, Q, cbc[par][:, :], conv_b[c * 128:(c + 1) * 128, :])

            q_slab = cnt[Q]

            @block.vector
            def _(v, c=c, par=par, Q=Q, q_slab=q_slab, is_xs=is_xs):
                v.wait_ge(sem[Q], q_slab)
                if c - 2 in ev_mm:
                    v.wait_ge(sem["ST"], ev_mm[c - 2])
                inc(v.tensor_copy(wslb[par][:, :], wsl[par][:, :]), "SV")
                ev_cast[c] = cnt["SV"]

            NPS = [pp, pq, pr]
            for isl, (n0, nw) in enumerate(NSLICES):
                ps = NPS[isl]

                @block.tensor
                def _(t, c=c, par=par, n0=n0, nw=nw, ps=ps, isl=isl):
                    t.wait_ge(sem["SV"], ev_cast[c])
                    if (c - 1, isl) in ev_evac:
                        k, v_ = ev_evac[(c - 1, isl)]
                        t.wait_ge(sem[k], v_)
                    last = None
                    for kcc in range(KC):
                        last = t.matmul(ps[:, 0:nw], wslb[par][:, kcc * 128:(kcc + 1) * 128],
                                        xnT[:, kcc * POS + n0: kcc * POS + n0 + nw],
                                        start=(kcc == 0), stop=(kcc == KC - 1))
                    inc(last, "ST")
                    if isl == len(NSLICES) - 1:
                        ev_mm[c] = cnt["ST"]

                if is_xs:
                    @block.scalar
                    def _(se, c=c, par=par, n0=n0, nw=nw, ps=ps, isl=isl):
                        se.wait_ge(sem["ST"], cnt["ST"])
                        if isl == 0 and c - 2 in ev_conv:
                            se.wait_ge(sem["SV"], ev_conv[c - 2])
                        ins = se.copy(xs_sb[par][:, n0:n0 + nw], ps[:, 0:nw])
                        if isl < 2:
                            inc(ins, "SS")
                        else:
                            sfence(se)
                        ev_evac[(c, isl)] = ("SS", cnt["SS"])
                else:
                    @block.scalar
                    def _(se, c=c, par=par, n0=n0, nw=nw, ps=ps, isl=isl):
                        se.wait_ge(sem["ST"], cnt["ST"])
                        if isl == 0 and c - 2 >= DCH:
                            se.wait_ge(sem["QG"], ev_zspill[c - 2])
                        inc(se.activation(zt[par][:, n0:n0 + nw], ps[:, 0:nw], ACTF.Silu,
                                          bias=0.0, scale=1.0), "SS")
                        ev_evac[(c, isl)] = ("SS", cnt["SS"])

            if is_xs:
                @block.vector
                def _(v, c=c, par=par):
                    v.wait_ge(sem["SS"], ev_evac[(c, 2)][1])
                    if c - 2 in ev_silu:
                        v.wait_ge(sem["SS"], ev_silu[c - 2])
                    inc(v.tensor_scalar(cv1[par][:, :], xs_sb[par][:, 0:TOK], cwc[par][:, 0:1],
                                        cbc[par][:, :], op0=ALU.mult, op1=ALU.add), "SV")
                    inc(v.scalar_tensor_tensor(cv2[par][:, :], xs_sb[par][:, 1:TOK + 1],
                                               cwc[par][:, 1:2],
                                               cv1[par][:, :], op0=ALU.mult, op1=ALU.add), "SV")
                    inc(v.scalar_tensor_tensor(cv1[par][:, :], xs_sb[par][:, 2:TOK + 2],
                                               cwc[par][:, 2:3],
                                               cv2[par][:, :], op0=ALU.mult, op1=ALU.add), "SV")
                    inc(v.scalar_tensor_tensor(cv2[par][:, :], xs_sb[par][:, 3:TOK + 3],
                                               cwc[par][:, 3:4],
                                               cv1[par][:, :], op0=ALU.mult, op1=ALU.add), "SV")
                    ev_conv[c] = cnt["SV"]

                @block.scalar
                def _(se, c=c, par=par):
                    se.wait_ge(sem["SV"], ev_conv[c])
                    if c - 2 in ev_ub:
                        se.wait_ge(sem["QG"], ev_ub[c - 2])
                    se.activation(u_f[par][:, :], cv2[par][:, :], ACTF.Silu, bias=0.0, scale=1.0)
                    sfence(se)
                    ev_silu[c] = cnt["SS"]

                @block.gpsimd
                def _(g, c=c, par=par):
                    g.wait_ge(sem["SS"], ev_silu[c])
                    dma(g, "QG", u_scr.ap()[c * 128:(c + 1) * 128, :], u_f[par][:, :])
                    ev_ub[c] = cnt["QG"]
            else:
                @block.gpsimd
                def _(g, c=c, par=par):
                    g.wait_ge(sem["SS"], ev_evac[(c, 2)][1])
                    dma(g, "QG", sz_scr.ap()[(c - DCH) * 128:(c - DCH + 1) * 128, :],
                        zt[par][:, :])
                    ev_zspill[c] = cnt["QG"]

        ev_uxld = {}
        ev_uxmm = {}
        for c in range(DCH):
            upar = c % 2

            @block.sync
            def _(s, c=c, upar=upar):
                if c - 2 in ev_uxmm:
                    s.wait_ge(sem["ST"], ev_uxmm[c - 2])
                s.wait_ge(sem["QG"], ev_ub[c])
                dma(s, "QS", uld[upar][:, :], u_scr.ap()[c * 128:(c + 1) * 128, :])
            ev_uxld[c] = cnt["QS"]

            @block.tensor
            def _(t, c=c, upar=upar):
                t.wait_ge(sem["QS"], ev_uxld[c])
                for nh in range(2):
                    sl = slice(nh * 512, (nh + 1) * 512)
                    t.matmul(px1[:, sl], wx_f[:, c * 160:c * 160 + 128],
                             uld[upar][:, sl],
                             start=(c == 0), stop=(c == DCH - 1), skip_group_check=True)
                    ins = t.matmul(px2[:, sl], wx_f[:, c * 160 + 128:c * 160 + 160],
                                   uld[upar][:, sl],
                                   start=(c == 0), stop=(c == DCH - 1), skip_group_check=True)
                inc(ins, "ST")
            ev_uxmm[c] = cnt["ST"]

        @block.vector
        def _(v):
            v.wait_ge(sem["ST"], cnt["ST"])
            inc(v.tensor_copy(dtr_b[:, :], px1[:, :]), "SV")
            inc(v.tensor_copy(bc_sb[:, :], px2[:, :]), "SV")

        if _DEBUG:
            @block.gpsimd
            def _(g):
                g.wait_ge(sem["SV"], cnt["SV"])
                dma(g, "QG", dbg["xdbl"][128:160, :], bc_sb[:, :])

        @block.gpsimd
        def _(g):
            g.wait_ge(sem["SV"], cnt["SV"])
            for j in range(NCORES):
                dma(g, "QG", a2a_in.ap()[j, 2 * CPC:2 * CPC + 2 * DST, :], bc_sb[:, :])

        ss_dt0 = cnt["SS"]
        sv_dt0 = cnt["SV"]
        qg_dt0 = cnt["QG"]
        ev_dtmm, ev_dtexp, ev_dtu, ev_dtq, ev_dtg = {}, {}, {}, {}, {}
        for c in range(DCH):
            par = c % 2
            pda, pdb = (pp, pq) if par == 0 else (pr, pd0)

            @block.sync
            def _(s, c=c, par=par):
                if c - 2 in ev_dtu:
                    s.wait_ge(sem["SV"], ev_dtu[c - 2])
                elif c < 2:
                    s.wait_ge(sem["ST"], ev_uxmm[DCH - 2 + c])
                if c - 2 in ev_dtexp:
                    s.wait_ge(sem["SS"], ev_dtexp[c - 2])
                if c == 0:
                    s.wait_ge(sem["QG"], qg_dt0)
                dma(s, "QS", bdtc[par][:, :], b_dt[c * 128:(c + 1) * 128, :])
                dma(s, "QS", uld[par][:, :], u_scr.ap()[c * 128:(c + 1) * 128, :])
            ev_dtq[c] = cnt["QS"]

            @block.tensor
            def _(t, c=c, par=par, pda=pda, pdb=pdb):
                if c == 0:
                    t.wait_ge(sem["SV"], cnt["SV"])
                if c < 2:
                    t.wait_ge(sem["SS"], ss_dt0)
                elif c - 2 in ev_dtexp:
                    t.wait_ge(sem["SS"], ev_dtexp[c - 2])
                t.matmul(pda[:, 0:512], wdt_b[:, c * 128:(c + 1) * 128], dtr_b[:, 0:512],
                         start=True, stop=True, skip_group_check=True)
                inc(t.matmul(pdb[:, 0:512], wdt_b[:, c * 128:(c + 1) * 128], dtr_b[:, 512:1024],
                             start=True, stop=True, skip_group_check=True), "ST")
            ev_dtmm[c] = cnt["ST"]

            @block.scalar
            def _(s, c=c, par=par, pda=pda, pdb=pdb):
                s.wait_ge(sem["ST"], ev_dtmm[c])
                s.wait_ge(sem["QS"], ev_dtq[c])
                if c - 2 in ev_dtg:
                    s.wait_ge(sem["QG"], ev_dtg[c - 2])
                if c - 2 in ev_dtu:
                    s.wait_ge(sem["SV"], ev_dtu[c - 2])
                inc(s.activation(et[:, 0:512], pda[:, 0:512], ACTF.Exp,
                                 bias=bdtc[par][:, :], scale=1.0), "SS")
                inc(s.activation(et[:, 512:1024], pdb[:, 0:512], ACTF.Exp,
                                 bias=bdtc[par][:, :], scale=1.0), "SS")
                s.activation(dt_sb[par][:, :], et[:, :], ACTF.Ln, bias=1.0, scale=1.0)
                sfence(s)
            ev_dtexp[c] = cnt["SS"]

            @block.vector
            def _(v, c=c, par=par):
                v.wait_ge(sem["SS"], ev_dtexp[c])
                v.wait_ge(sem["QS"], ev_dtq[c])
                if c - 2 in ev_dtg:
                    v.wait_ge(sem["QG"], ev_dtg[c - 2])
                inc(v.tensor_tensor(dtu_sb[par][:, :], dt_sb[par][:, :], uld[par][:, :],
                                    op=ALU.mult), "SV")
            ev_dtu[c] = cnt["SV"]

            @block.gpsimd
            def _(g, c=c, par=par):
                j, r = c // BL, (c % BL) * 128
                g.wait_ge(sem["SS"], ev_dtexp[c])
                dma(g, "QG", a2a_in.ap()[j, r:r + 128, :], dt_sb[par][:, :])
                g.wait_ge(sem["SV"], ev_dtu[c])
                dma(g, "QG", a2a_in.ap()[j, CPC + r:CPC + r + 128, :], dtu_sb[par][:, :])
                if _DEBUG:
                    dma(g, "QG", dbg["dt"][c * 128:(c + 1) * 128, :], dt_sb[par][:, :])
            ev_dtg[c] = cnt["QG"]

        s2 = snap()

    es2.close()
    es1.close()


    with (
        nc.sbuf_tensor("kqb", [128, DM], BF16) as kqb,
        nc.sbuf_tensor("kq", [128, KC * DM], BF16) as kq,
        nc.sbuf_tensor("bq_rep", [128, DM], F32) as bq_rep,
        nc.sbuf_tensor("scr8", [128, DM], F32) as scr8,
        nc.sbuf_tensor("kab", [128, DM], F32) as kab,
        nc.sbuf_tensor("k0", [128, 1], F32) as k0,
        nc.sbuf_tensor("k1", [128, 1], F32) as k1,
        nc.psum_tensor("ptk", [128, 128], BF16) as ptk,
        nc.psum_tensor("ptk2", [128, 128], BF16) as ptk2,
    ):
        kqc = scr8
        with (
            nc.sbuf_tensor("bqr", [1, DM], F32) as bqr,
            nc.sbuf_tensor("b0", [1, 1], F32) as b0,
            nc.sbuf_tensor("b1", [1, 1], F32) as b1,
        ):
            @block.sync
            def _(s):
                wait_all(s, s2, skip=("QS", "QC"))
                dma(s, "QS", bqr[:, :], ffn_b)

            @block.vector
            def _(v):
                v.wait_ge(sem["QS"], cnt["QS"])
                inc(v.tensor_scalar(bq_rep[0:1, :], bqr[:, :], -1.0, None, op0=ALU.mult), "SV")
                inc(v.tensor_tensor(bq_rep[0:1, :], bq_rep[0:1, :], bqr[:, :], op=ALU.max), "SV")
                inc(v.reduce_max(b0[:, :], bq_rep[0:1, :], axis=AX.X), "SV")
                inc(v.tensor_copy(kab[0:1, 0:256], bqr[:, 0:256]), "SV")
                inc(v.tensor_scalar(b0[:, :], b0[:, :], EPS_AMAX, None, op0=ALU.max), "SV")
                inc(v.tensor_copy(kab[0:1, 256:512], bqr[:, 0:256]), "SV")
                inc(v.reciprocal(b1[:, :], b0[:, :]), "SV")
                inc(v.tensor_copy(kab[0:1, 512:768], bqr[:, 0:256]), "SV")
                inc(v.tensor_scalar(bqr[:, :], bqr[:, :], b1[:, :], None, op0=ALU.mult), "SV")
                inc(v.tensor_scalar(bqr[:, :], bqr[:, :], 127.0, MAGIC,
                                    op0=ALU.mult, op1=ALU.add), "SV")
                inc(v.tensor_scalar(bqr[:, :], bqr[:, :], MAGIC + 127.0, MAGIC - 127.0,
                                    op0=ALU.min, op1=ALU.max), "SV")
                inc(v.tensor_scalar(bqr[:, :], bqr[:, :], MAGIC, b0[:, :],
                                    op0=ALU.subtract, op1=ALU.mult), "SV")
                inc(v.tensor_scalar(bqr[:, :], bqr[:, :], 1.0 / 127.0, None, op0=ALU.mult), "SV")

            @block.gpsimd
            def _(g):
                g.wait_ge(sem["SV"], cnt["SV"])
                dma(g, "QG", bq_scr.ap(), bqr[:, :])

            @block.sync
            def _(s):
                s.wait_ge(sem["QG"], cnt["QG"])
                dma(s, "QS", bq_rep[:, :], bq_scr.ap().broadcast_to([128, DM]))

        for fc in range(KC):
            base_v, base_t, base_s = cnt["SV"], cnt["ST"], cnt["SS"]

            @block.sync
            def _(s, fc=fc, base_v=base_v):
                s.wait_ge(sem["SV"], base_v)
                dma(s, "QS", kqc[:, :], ffn_k_t[fc * 128:(fc + 1) * 128, :])

            @block.vector
            def _(v, fc=fc):
                v.wait_ge(sem["QS"], cnt["QS"])
                inc(v.tensor_scalar(kab[:, :], kqc[:, :], -1.0, None, op0=ALU.mult), "SV")
                inc(v.tensor_tensor(kab[:, :], kab[:, :], kqc[:, :], op=ALU.max), "SV")
                inc(v.reduce_max(k0[:, :], kab[:, :], axis=AX.X), "SV")
                inc(v.tensor_copy(kab[:, 0:256], kqc[:, 0:256]), "SV")
                inc(v.tensor_scalar(k0[:, :], k0[:, :], EPS_AMAX, None, op0=ALU.max), "SV")
                inc(v.tensor_copy(kab[:, 256:512], kqc[:, 0:256]), "SV")
                inc(v.reciprocal(k1[:, :], k0[:, :]), "SV")
                inc(v.tensor_copy(kab[:, 512:768], kqc[:, 0:256]), "SV")
                inc(v.tensor_scalar(kqc[:, :], kqc[:, :], k1[:, :], None, op0=ALU.mult), "SV")
                inc(v.tensor_scalar(kqc[:, :], kqc[:, :], 127.0, MAGIC,
                                    op0=ALU.mult, op1=ALU.add), "SV")
                inc(v.tensor_scalar(kqc[:, :], kqc[:, :], MAGIC + 127.0, MAGIC - 127.0,
                                    op0=ALU.min, op1=ALU.max), "SV")
                inc(v.tensor_scalar(kqc[:, :], kqc[:, :], MAGIC, k0[:, :],
                                    op0=ALU.subtract, op1=ALU.mult), "SV")
                inc(v.tensor_scalar(kqb[:, :], kqc[:, :], 1.0 / 127.0, None, op0=ALU.mult), "SV")

            @block.tensor
            def _(t, fc=fc):
                t.wait_ge(sem["SV"], cnt["SV"])
                for dc in range(KC):
                    if dc > 0:
                        t.wait_ge(sem["SS"], cnt["SS"])
                    if dc % 2 == 0:
                        inc(t.transpose(ptk[:, :], kqb[:, dc * 128:(dc + 1) * 128], idb[:, :]), "ST")
                    else:
                        inc(t.transpose(ptk2[:, :], kqb[:, dc * 128:(dc + 1) * 128], idb[:, :]), "ST")

                    @block.scalar
                    def _(s, fc=fc, dc=dc):
                        s.wait_ge(sem["ST"], cnt["ST"])
                        src = ptk if dc % 2 == 0 else ptk2
                        inc(s.copy(kq[:, dc * DM + fc * 128: dc * DM + (fc + 1) * 128],
                                   src[:, :]), "SS")
                        if fc == KC - 1 and dc == KC - 1:
                            sfence(s)

        if _DEBUG:
            @block.gpsimd
            def _(g):
                g.wait_ge(sem["SS"], cnt["SS"])
                dma(g, "QG", dbg["xqt"], xqT[:, :])
                dma(g, "QG", dbg["kq"], kq[:, :])

        ev_ospill = {}
        with (
            nc.sbuf_tensor("xt6", [128, DM], F32) as xt6,
            nc.sbuf_tensor("obufA", [128, DM], F32) as obufA,
            nc.sbuf_tensor("obufB", [128, DM], F32) as obufB,
            nc.sbuf_tensor("f0", [128, 1], F32) as f0c,
            nc.sbuf_tensor("f1c", [128, 1], F32) as f1c,
            nc.psum_tensor("pf", [128, DM], F32) as pf,
        ):
            obuf = [obufA, obufB]
            f1 = scr8
            for tt in range(8):
                base_v, base_t, base_q = cnt["SV"], cnt["ST"], cnt["QS"]

                @block.sync
                def _(s, tt=tt, base_v=base_v):
                    s.wait_ge(sem["SV"], base_v)
                    dma(s, "QS", xt6[:, :], x_in[HALO + tt * 128: HALO + (tt + 1) * 128, :])

                @block.tensor
                def _(t, tt=tt, base_v=base_v):
                    if tt == 0:
                        t.wait_ge(sem["SS"], cnt["SS"])
                    else:
                        t.wait_ge(sem["SV"], base_v)
                    last = None
                    for nh in range(4):
                        for dc in range(KC):
                            last = t.matmul(pf[:, nh * 512:(nh + 1) * 512],
                                            xqT[:, dc * POS + HALO + tt * 128:
                                                dc * POS + HALO + (tt + 1) * 128],
                                            kq[:, dc * DM + nh * 512: dc * DM + (nh + 1) * 512],
                                            start=(dc == 0), stop=(dc == KC - 1),
                                            skip_group_check=True)
                    inc(last, "ST")

                @block.vector
                def _(v, tt=tt):
                    v.wait_ge(sem["ST"], cnt["ST"])
                    v.wait_ge(sem["QS"], cnt["QS"])
                    inc(v.tensor_tensor(f1[:, :], pf[:, :], bq_rep[:, :], op=ALU.add), "SV")
                    inc(v.tensor_scalar(f1[:, :], f1[:, :], 0.0, None, op0=ALU.max), "SV")
                    inc(v.reduce_max(f0c[:, :], f1[:, :], axis=AX.X), "SV")
                    inc(v.tensor_copy(kab[:, 0:256], f1[:, 0:256]), "SV")
                    inc(v.tensor_scalar(f0c[:, :], f0c[:, :], EPS_AMAX, None, op0=ALU.max), "SV")

                @block.vector
                def _(v, tt=tt):
                    inc(v.tensor_copy(kab[:, 256:512], f1[:, 0:256]), "SV")
                    inc(v.reciprocal(f1c[:, :], f0c[:, :]), "SV")
                    inc(v.tensor_copy(kab[:, 512:768], f1[:, 0:256]), "SV")
                    inc(v.tensor_scalar(f1[:, :], f1[:, :], f1c[:, :], None, op0=ALU.mult), "SV")
                    inc(v.tensor_scalar(f1[:, :], f1[:, :], 127.0, MAGIC,
                                        op0=ALU.mult, op1=ALU.add), "SV")
                    inc(v.tensor_scalar(f1[:, :], f1[:, :], MAGIC + 127.0, MAGIC - 127.0,
                                        op0=ALU.min, op1=ALU.max), "SV")
                    inc(v.tensor_scalar(f1[:, :], f1[:, :], MAGIC, f0c[:, :],
                                        op0=ALU.subtract, op1=ALU.mult), "SV")
                    inc(v.tensor_scalar(f1[:, :], f1[:, :], 1.0 / 127.0, None, op0=ALU.mult), "SV")
                    if tt >= 2:
                        v.wait_ge(sem["QG"], ev_ospill[tt - 2])
                    inc(v.tensor_tensor(obuf[tt % 2][:, :], xt6[:, :], f1[:, :], op=ALU.add), "SV")

                @block.gpsimd
                def _(g, tt=tt):
                    g.wait_ge(sem["SV"], cnt["SV"])
                    dma(g, "QG", o_scr.ap()[tt * 128:(tt + 1) * 128, :], obuf[tt % 2][:, :])
                    ev_ospill[tt] = cnt["QG"]

                if _DEBUG:
                    @block.gpsimd
                    def _(g, tt=tt):
                        g.wait_ge(sem["SV"], cnt["SV"])
                        dma(g, "QG", dbg["ffnq"][tt * 128:(tt + 1) * 128, :], f1[:, :])

        s6b = snap()

    es_xq.close()

    @block.gpsimd
    def _(g):
        wait_all(g, s2)
        with allow_ici_all_to_all():
            inc(g.collective_compute(
                "AllToAll", ALU.bypass,
                replica_groups=[list(range(NCORES))],
                ins=[a2a_in.ap().opt()],
                outs=[a2a_out.ap().opt()],
            ), "SG")

    s3 = snap()

    with ExitStack() as es4:
        dtq = es4.enter_context(nc.sbuf_tensor("dtq", [128, BL * TQW], F32))
        dtldA = es4.enter_context(nc.sbuf_tensor("dtldA", [128, TQW], F32))
        dtldB = es4.enter_context(nc.sbuf_tensor("dtldB", [128, TQW], F32))
        dtuT = es4.enter_context(nc.sbuf_tensor("dtuT", [128, BL * 8 * 128], BF16))
        bcolq = es4.enter_context(nc.sbuf_tensor("bcolq", [128, 128], F32))
        ball = es4.enter_context(nc.sbuf_tensor("ball", [128, TQW], F32))
        dgqA = es4.enter_context(nc.sbuf_tensor("dgqA", [128, 8 * 128], BF16))
        dgqB = es4.enter_context(nc.sbuf_tensor("dgqB", [128, 8 * 128], BF16))
        crep = es4.enter_context(nc.sbuf_tensor("crep", [128, 2 * TQW], F32))
        da = es4.enter_context(nc.sbuf_tensor("da", [128, 2 * 2 * TQW], F32))
        hb = es4.enter_context(nc.sbuf_tensor("hb", [128, 2 * TQW], F32))
        chb0 = es4.enter_context(nc.sbuf_tensor("chb0", [128, TQW], F32))
        chb1 = es4.enter_context(nc.sbuf_tensor("chb1", [128, TQW], F32))
        chb2 = es4.enter_context(nc.sbuf_tensor("chb2", [128, TQW], F32))
        chb3 = es4.enter_context(nc.sbuf_tensor("chb3", [128, TQW], F32))
        yq = es4.enter_context(nc.sbuf_tensor("yq", [128, BL * TQW], F32))
        chbs = [chb0, chb1, chb2, chb3]
        dtld = [dtldA, dtldB]
        dgq = [dgqA, dgqB]
        TE_Y = (0, 1)
        ev_tts = {}
        ev_ch = {}
        ev_ymm = {}
        ev_vy = {}
        ev_dbu = {}
        ev_dg = {}
        ev_evac = {}
        ev_tr = {}
        qg_y = {}

        py0 = es4.enter_context(nc.psum_tensor("py0", [128, TQW], F32))
        py1 = es4.enter_context(nc.psum_tensor("py1", [128, TQW], F32))
        pb0 = es4.enter_context(nc.psum_tensor("pb0", [128, TQW], F32))
        pb1 = es4.enter_context(nc.psum_tensor("pb1", [128, TQW], F32))
        if True:
            pys = {0: py0, 1: py1}
            pdbu = [pb0, pb1]
            ev_bcol = {}
            ev_da_last = {}

            @block.gpsimd
            def _(g):
                inc(g.memset(ball[:, :], 0.0), "SG")
            sg_ball = cnt["SG"]

            for tq in range(TQ):
                @block.sync
                def _(s, tq=tq):
                    if tq == 0:
                        wait_all(s, s3, skip=("QS", "QC"))
                        s.wait_ge(sem["SG"], sg_ball)
                    else:
                        s.wait_ge(sem["SS"], ev_da_last[tq - 1])
                    for b in range(BL):
                        dma(s, "QS", dtq[:, b * TQW:(b + 1) * TQW],
                            a2a_out.ap()[tq, b * 128:(b + 1) * 128, :])
                    dma(s, "QS", ball[0:DST, :], a2a_out.ap()[tq, 2 * CPC:2 * CPC + DST, :])

                qs_dtq = cnt["QS"]

                @block.tensor
                def _(t, tq=tq):
                    t.wait_ge(sem["QS"], qs_dtq)
                    if tq == 0:
                        t.wait_ge(sem["SG"], cnt["SG"])
                    else:
                        t.wait_ge(sem["SD"], ev_tts[(tq - 1, DST - 1, BL - 1)])
                    for tt in range(8):
                        ins = t.transpose(pdbu[1][:, tt * 128:(tt + 1) * 128],
                                          ball[:, tt * 128:(tt + 1) * 128], idf32[:, :])
                    inc(ins, "ST")
                ev_btr = cnt["ST"]

                @block.scalar
                def _(se, tq=tq, ev_btr=ev_btr):
                    se.wait_ge(sem["ST"], ev_btr)
                    for tt in range(8):
                        se.copy(bcolq[:, tt * DST:(tt + 1) * DST],
                                pdbu[1][:, tt * 128:tt * 128 + DST])
                    sfence(se)
                    ev_bcol[tq] = cnt["SS"]

                for b in range(BL):
                    @block.sync
                    def _(s, tq=tq, b=b):
                        if (tq, b - 2) in ev_tr:
                            s.wait_ge(sem["SS"], ev_tr[(tq, b - 2)])
                        elif tq > 0 and b < 2:
                            s.wait_ge(sem["SS"], ev_tr[(tq - 1, 2 + b)])
                        dma(s, "QS", dtld[b % 2][:, :],
                            a2a_out.ap()[tq, CPC + b * 128:CPC + (b + 1) * 128, :])
                    qs_ld = cnt["QS"]

                    @block.tensor
                    def _(t, tq=tq, b=b, qs_ld=qs_ld):
                        t.wait_ge(sem["QS"], qs_ld)
                        if tq == 0 and b == 0:
                            pass
                        else:
                            pk = (tq, b - 1) if b >= 1 else (tq - 1, BL - 1)
                            if pk in ev_tr:
                                t.wait_ge(sem["SS"], ev_tr[pk])
                        if tq > 0 and b < 2:
                            t.wait_ge(sem["SD"], ev_tts[(tq - 1, DST - 1, BL - 1)])
                        for tt in range(8):
                            inc(t.transpose(pdbu[b % 2][:, tt * 128:(tt + 1) * 128],
                                            dtld[b % 2][:, tt * 128:(tt + 1) * 128],
                                            idf32[:, :]), "ST")

                    @block.scalar
                    def _(se, tq=tq, b=b):
                        se.wait_ge(sem["ST"], cnt["ST"])
                        se.copy(dtuT[:, b * TQW:(b + 1) * TQW], pdbu[b % 2][:, :])
                        sfence(se)
                        ev_tr[(tq, b)] = cnt["SS"]

                for si in range(DST):
                    spar = si % 2

                    @block.sync
                    def _(s, tq=tq, si=si, spar=spar):
                        ptq, psi = (tq, si - 2) if si >= 2 else (tq - 1, DST - 2 + si)
                        if ptq >= 0:
                            s.wait_ge(sem["SV"], ev_ch[(ptq, psi, BL - 1)])
                        dma(s, "QS", crep[:, spar * TQW:(spar + 1) * TQW],
                            a2a_out.ap()[tq, 2 * CPC + DST + si:2 * CPC + DST + si + 1, :]
                            .broadcast_to([128, TQW]))

                    qs_bc = cnt["QS"]

                    @block.scalar
                    def _(se, tq=tq, si=si, spar=spar):
                        ptq, psi = (tq, si - 2) if si >= 2 else (tq - 1, DST - 2 + si)
                        if ptq >= 0:
                            se.wait_ge(sem["ST"], ev_dbu[(ptq, psi, BL - 1)])
                        for tt in range(8):
                            se.activation(dgq[spar][:, tt * 128:(tt + 1) * 128], idb[:, :],
                                          ACTF.Copy, bias=0.0,
                                          scale=bcolq[:, tt * DST + si:tt * DST + si + 1])
                        sfence(se)
                        ev_dg[(tq, si)] = cnt["SS"]

                    if _DEBUG and tq == 0 and si == 0:
                        @block.sync
                        def _(s):
                            s.wait_ge(sem["SS"], ev_dg[(0, 0)])
                            s.wait_ge(sem["SS"], ev_tr[(0, BL - 1)])
                            dma(s, "QS", dbg["bcolq"], bcolq[:, :])
                            dma(s, "QS", dbg["dgq0"], dgq[0][:, :])
                            dma(s, "QS", dbg["dtut"], dtuT[:, :])

                    ev_da = {}

                    def emit_da(pid, tq=tq, si=si):
                        q = si * 2 + pid
                        ppar = q % 2

                        @block.scalar
                        def _(se, tq=tq, si=si, pid=pid, q=q, ppar=ppar):
                            ptq, q2 = (tq, q - 2) if q >= 2 else (tq - 1, 2 * DST - 2 + q)
                            if ptq >= 0:
                                se.wait_ge(sem["SD"], ev_tts[(ptq, q2 // 2, 2 * (q2 % 2) + 1)])
                            for bi in range(2):
                                b = 2 * pid + bi
                                ins = se.activation(
                                    da[:, (ppar * 2 + bi) * TQW:(ppar * 2 + bi + 1) * TQW],
                                    dtq[:, b * TQW:(b + 1) * TQW], ACTF.Exp, bias=0.0,
                                    scale=a_sb[:, b * DST + si:b * DST + si + 1])
                                if bi == 0:
                                    inc(ins, "SS")
                            sfence(se)
                            ev_da[q] = cnt["SS"]
                            ev_da_last[tq] = cnt["SS"]

                    def emit_dbu(pid, tq=tq, si=si):
                        @block.tensor
                        def _(t, tq=tq, si=si, pid=pid):
                            t.wait_ge(sem["SS"], ev_dg[(tq, si)])
                            for bi in range(2):
                                b = 2 * pid + bi
                                it = si * BL + b
                                pit = it - 2
                                if pit >= 0:
                                    t.wait_ge(sem["SD"], tq * 64 + pit + 1)
                                else:
                                    t.wait_ge(sem["SS"], ev_tr[(tq, BL - 1)])
                                last = None
                                for tt in range(8):
                                    last = t.matmul(
                                        pdbu[it % 2][:, tt * 128:(tt + 1) * 128],
                                        dtuT[:, b * TQW + tt * 128: b * TQW + (tt + 1) * 128],
                                        dgq[si % 2][:, tt * 128:(tt + 1) * 128],
                                        start=True, stop=True, skip_group_check=True)
                                inc(last, "ST")
                                ev_dbu[(tq, si, b)] = cnt["ST"]

                    def emit_ve(pid, tq=tq, si=si, spar=spar, qs_bc=qs_bc):
                        q = si * 2 + pid
                        ppar = q % 2

                        @block.vector
                        def _(v, tq=tq, si=si, pid=pid, q=q, ppar=ppar, spar=spar):
                            v.wait_ge(sem["SS"], ev_da[q])
                            if pid == 0:
                                v.wait_ge(sem["QS"], qs_bc)
                            for bi in range(2):
                                b = 2 * pid + bi
                                it = si * BL + b
                                col = b * DST + si
                                v.wait_ge(sem["ST"], ev_dbu[(tq, si, b)])
                                if it == 0 and tq > 0:
                                    lcol = (BL - 1) * DST + (DST - 1)
                                    inc(v.tensor_scalar(carry[:, lcol:lcol + 1],
                                                        hb[:, TQW + TQW - 1:2 * TQW],
                                                        1.0, None, op0=ALU.mult), "SV")
                                inc(v.tensor_tensor_scan(
                                    hb[:, (it % 2) * TQW:(it % 2 + 1) * TQW],
                                    da[:, (ppar * 2 + bi) * TQW:(ppar * 2 + bi + 1) * TQW],
                                    pdbu[it % 2][:, :], carry[:, col:col + 1],
                                    op0=ALU.mult, op1=ALU.add), "SD")
                                ev_tts[(tq, si, b)] = cnt["SD"]
                                assert cnt["SD"] == tq * 64 + it + 1
                                pit = it - 1
                                if pit >= 0 and tq < TQ - 1:
                                    psi2, pb2 = pit // BL, pit % BL
                                    pcol = pb2 * DST + psi2
                                    inc(v.tensor_scalar(carry[:, pcol:pcol + 1],
                                                        hb[:, (pit % 2) * TQW + TQW - 1:
                                                           (pit % 2) * TQW + TQW],
                                                        1.0, None, op0=ALU.mult), "SV")
                                if b in TE_Y:
                                    pk = (tq, si - 1, b)
                                    if pk in ev_ymm:
                                        v.wait_ge(sem["ST"], ev_ymm[pk])
                                    elif tq > 0:
                                        v.wait_ge(sem["ST"], ev_ymm[(tq - 1, DST - 1, b)])
                                cb = chbs[b]
                                inc(v.tensor_tensor(cb[:, :],
                                                    hb[:, (it % 2) * TQW:(it % 2 + 1) * TQW],
                                                    crep[:, spar * TQW:(spar + 1) * TQW],
                                                    op=ALU.mult), "SV")
                                ev_ch[(tq, si, b)] = cnt["SV"]
                                if b not in TE_Y:
                                    if si == 0 and tq > 0 and b == 2:
                                        v.wait_ge(sem["QG"], qg_y[tq - 1])
                                    if si == 0:
                                        inc(v.tensor_copy(yq[:, b * TQW:(b + 1) * TQW],
                                                          cb[:, :]), "SV")
                                    else:
                                        inc(v.tensor_tensor(yq[:, b * TQW:(b + 1) * TQW],
                                                            yq[:, b * TQW:(b + 1) * TQW],
                                                            cb[:, :], op=ALU.add), "SV")
                                    ev_vy[(tq, si, b)] = cnt["SV"]

                    def emit_ymm(pid, tq=tq, si=si):
                        @block.tensor
                        def _(t, tq=tq, si=si, pid=pid):
                            for bi in range(2):
                                b = 2 * pid + bi
                                if b not in TE_Y:
                                    continue
                                t.wait_ge(sem["SV"], ev_ch[(tq, si, b)])
                                if si == 0 and tq > 0:
                                    t.wait_ge(sem["SS"], ev_evac[tq - 1])
                                cb = chbs[b]
                                t.matmul(pys[b][:, 0:512], idf32[:, :], cb[:, 0:512],
                                         start=(si == 0), stop=(si == DST - 1),
                                         skip_group_check=True)
                                inc(t.matmul(pys[b][:, 512:TQW], idf32[:, :], cb[:, 512:TQW],
                                             start=(si == 0), stop=(si == DST - 1),
                                             skip_group_check=True), "ST")
                                ev_ymm[(tq, si, b)] = cnt["ST"]

                    emit_da(0)
                    emit_da(1)
                    emit_dbu(0)
                    emit_dbu(1)
                    emit_ve(0)
                    emit_ymm(0)
                    emit_ve(1)
                    emit_ymm(1)

                @block.scalar
                def _(s, tq=tq):
                    s.wait_ge(sem["ST"], ev_ymm[(tq, DST - 1, 1)])
                    if tq > 0:
                        s.wait_ge(sem["QG"], qg_y[tq - 1])
                    for b in TE_Y:
                        ins = s.copy(yq[:, b * TQW:(b + 1) * TQW], pys[b][:, :])
                        if b == 0:
                            inc(ins, "SS")
                    sfence(s)
                    ev_evac[tq] = cnt["SS"]

                @block.gpsimd
                def _(g, tq=tq):
                    g.wait_ge(sem["SS"], ev_evac[tq])
                    for b in range(BL):
                        if b not in TE_Y:
                            g.wait_ge(sem["SV"], ev_vy[(tq, DST - 1, b)])
                    for b in range(BL):
                        dma(g, "QG", y_in.ap()[tq, b * 128:(b + 1) * 128, :],
                            yq[:, b * TQW:(b + 1) * TQW])
                qg_y[tq] = cnt["QG"]

    s4 = snap()

    @block.gpsimd
    def _(g):
        wait_all(g, s4)
        with allow_ici_all_to_all():
            inc(g.collective_compute(
                "AllToAll", ALU.bypass,
                replica_groups=[list(range(NCORES))],
                ins=[y_in.ap().opt()],
                outs=[y_out.ap().opt()],
            ), "SG")

    s5 = snap()


    with (
        nc.sbuf_tensor("out_acc", [128, 8 * DM], F32) as out_acc,
        nc.sbuf_tensor("yg", [128, DCH * TOK], BF16) as yg,
        nc.sbuf_tensor("ycl", [128, TOK], F32) as ycl,
        nc.sbuf_tensor("ucl", [128, TOK], F32) as ucl,
        nc.sbuf_tensor("szl", [128, TOK], F32) as szl,
        nc.sbuf_tensor("gt", [128, TOK], F32) as gt,
        nc.sbuf_tensor("dcl", [128, 1], F32) as dcl,
        nc.sbuf_tensor("wovA", [128, 256], F32) as wovA,
        nc.sbuf_tensor("wovB", [128, 256], F32) as wovB,
        nc.sbuf_tensor("wobA", [128, 256], BF16) as wobA,
        nc.sbuf_tensor("wobB", [128, 256], BF16) as wobB,
        nc.sbuf_tensor("mst", [128, 2 * TOK], F32) as mst,
        nc.psum_tensor("pw0", [128, TOK], F32) as pw0,
        nc.psum_tensor("pw1", [128, TOK], F32) as pw1,
        nc.psum_tensor("ptm", [128, 128], F32) as ptm,
        nc.psum_tensor("ptm2", [128, 128], F32) as ptm2,
        nc.sbuf_tensor("idf", [128, 128], F32) as idf,
    ):
        pws = [pw0, pw1]

        @block.vector
        def _(v):
            inc(v.tensor_copy(idf[:, :], idb[:, :]), "SV")

        @block.sync
        def _(s):
            wait_all(s, s5, skip=("QS", "QC"))
            for tt in range(8):
                dma(s, "QS", out_acc[:, tt * DM:(tt + 1) * DM],
                    o_scr.ap()[tt * 128:(tt + 1) * 128, :])
        qs_oacc = cnt["QS"]

        @block.vector
        def _(v):
            v.wait_ge(sem["QS"], qs_oacc)
            inc(v.engine_nop(), "SV")
        sv_oacc = cnt["SV"]

        for c in range(DCH):
            base_v, base_q, base_c = cnt["SV"], cnt["QS"], cnt["QC"]

            @block.sync
            def _(s, c=c, base_v=base_v):
                if c == 0:
                    wait_all(s, s5, skip=("QS", "QC", "QG"))
                else:
                    s.wait_ge(sem["SV"], base_v)
                j, r = c // BL, (c % BL) * 128
                dma(s, "QS", ycl[:, :], y_out.ap()[j, r:r + 128, :])
                dma(s, "QS", ucl[:, :], u_scr.ap()[c * 128:(c + 1) * 128, :])
                dma(s, "QS", szl[:, :], sz_scr.ap()[c * 128:(c + 1) * 128, HALO:HALO + TOK])
                dma(s, "QS", dcl[:, :], d_vec[c * 128:(c + 1) * 128, :])

            @block.vector
            def _(v, c=c):
                v.wait_ge(sem["QS"], cnt["QS"])
                inc(v.scalar_tensor_tensor(gt[:, :], ucl[:, :], dcl[:, :], ycl[:, :],
                                           op0=ALU.mult, op1=ALU.add), "SV")
                inc(v.tensor_tensor(yg[:, c * TOK:(c + 1) * TOK], gt[:, :], szl[:, :],
                                    op=ALU.mult), "SV")

            if _DEBUG:
                @block.gpsimd
                def _(g, c=c):
                    g.wait_ge(sem["SV"], cnt["SV"] - 1)
                    dma(g, "QG", dbg["yown"][c * 128:(c + 1) * 128, :], gt[:, :])

        wovs = [wovA, wovB]
        wobs = [wobA, wobB]
        ev_wld = {}
        ev_wcast = {}
        ev_wmm = {}
        ev_wevac = {}
        ev_wadd = {}
        ev_wtr = {}
        for mq in range(8):
            for dc in range(DCH):
                gi = mq * DCH + dc
                wpar = gi % 2
                Q = "QS" if wpar == 0 else "QC"

                @block.sync if wpar == 0 else block.scalar
                def _(s, mq=mq, dc=dc, gi=gi, wpar=wpar, Q=Q):
                    if gi - 2 in ev_wcast:
                        s.wait_ge(sem["SV"], ev_wcast[gi - 2])
                    dma(s, Q, wovs[wpar][:, :],
                        w_out_t[dc * 128:(dc + 1) * 128, mq * 256:(mq + 1) * 256])
                ev_wld[gi] = cnt["QS" if wpar == 0 else "QC"]

                @block.vector
                def _(v, gi=gi, wpar=wpar, Q=Q):
                    v.wait_ge(sem[Q], ev_wld[gi])
                    if gi - 2 in ev_wmm:
                        v.wait_ge(sem["ST"], ev_wmm[gi - 2])
                    inc(v.tensor_copy(wobs[wpar][:, :], wovs[wpar][:, :]), "SV")
                ev_wcast[gi] = cnt["SV"]

                @block.tensor
                def _(t, mq=mq, dc=dc, gi=gi, wpar=wpar):
                    t.wait_ge(sem["SV"], ev_wcast[gi])
                    if dc == 0 and mq >= 1:
                        t.wait_ge(sem["SV"], ev_wevac[mq - 1])
                    last = None
                    for j in range(2):
                        for nh in range(2):
                            sl = slice(nh * 512, (nh + 1) * 512)
                            last = t.matmul(pws[j][:, sl], wobs[wpar][:, j * 128:(j + 1) * 128],
                                            yg[:, dc * TOK + nh * 512: dc * TOK + (nh + 1) * 512],
                                            start=(dc == 0), stop=(dc == DCH - 1),
                                            skip_group_check=True)
                    inc(last, "ST")
                ev_wmm[gi] = cnt["ST"]

            @block.vector
            def _(v, mq=mq):
                v.wait_ge(sem["ST"], ev_wmm[mq * DCH + DCH - 1])
                if mq >= 1:
                    v.wait_ge(sem["ST"], ev_wtr[(mq - 1, 1, 7)])
                for j in range(2):
                    inc(v.tensor_copy(mst[:, j * TOK:(j + 1) * TOK], pws[j][:, :]), "SV")
            ev_wevac[mq] = cnt["SV"]

            @block.tensor
            def _(t, mq=mq):
                t.wait_ge(sem["SV"], ev_wevac[mq])
                for j in range(2):
                    for tt in range(8):
                        k = j * 8 + tt
                        pk = (mq, j, tt - 2) if tt >= 2 else                              ((mq, j - 1, 6 + tt) if j == 1 else
                              ((mq - 1, 1, 6 + tt) if mq >= 1 else None))
                        if pk is not None and pk in ev_wadd:
                            t.wait_ge(sem["SV"], ev_wadd[pk])
                        ptx = ptm if k % 2 == 0 else ptm2
                        inc(t.transpose(ptx[:, :],
                                        mst[:, j * TOK + tt * 128: j * TOK + (tt + 1) * 128],
                                        idf[:, :]), "ST")
                        ev_wtr[(mq, j, tt)] = cnt["ST"]

                        @block.vector
                        def _(v, mq=mq, j=j, tt=tt, ptx=ptx):
                            v.wait_ge(sem["ST"], ev_wtr[(mq, j, tt)])
                            if mq == 0 and j == 0 and tt == 0:
                                v.wait_ge(sem["QS"], qs_oacc)
                            col = tt * DM + mq * 256 + j * 128
                            inc(v.tensor_tensor(out_acc[:, col:col + 128],
                                                out_acc[:, col:col + 128],
                                                ptx[:, :], op=ALU.add), "SV")
                            ev_wadd[(mq, j, tt)] = cnt["SV"]

        @block.gpsimd
        def _(g):
            g.wait_ge(sem["SV"], cnt["SV"])
            for tt in range(8):
                dma(g, "QG", out_d[tt * 128:(tt + 1) * 128, :],
                    out_acc[:, tt * DM:(tt + 1) * DM])
            g.wait_ge(sem["QG"], cnt["QG"])

    es.close()
    return nc


_NC_CACHE = None


def make_in_maps(L0_attn_x, L0_attn_W_in, L0_attn_conv_w, L0_attn_conv_b, L0_attn_W_x,
                 L0_attn_W_dt, L0_attn_b_dt, L0_attn_A_log, L0_attn_D, L0_attn_W_out,
                 L0_ffn_kernel, L0_ffn_bias):
    x = np.ascontiguousarray(np.asarray(L0_attn_x, dtype=np.float32).reshape(8192, DM))
    w_in = np.asarray(L0_attn_W_in, np.float32)
    w_in_4 = np.ascontiguousarray(w_in.reshape(2 * DCH, 128, KC, 128).transpose(3, 0, 2, 1))
    conv_w = np.ascontiguousarray(np.asarray(L0_attn_conv_w, np.float32))
    conv_b = np.ascontiguousarray(np.asarray(L0_attn_conv_b, np.float32).reshape(DIN, 1))
    w_x_t = np.ascontiguousarray(np.asarray(L0_attn_W_x, np.float32).T)
    w_dt_t = np.ascontiguousarray(np.asarray(L0_attn_W_dt, np.float32).T)
    b_dt_a = np.ascontiguousarray(np.asarray(L0_attn_b_dt, np.float32).reshape(DIN, 1))
    a_log = np.ascontiguousarray(np.asarray(L0_attn_A_log, np.float32))
    d_vec = np.ascontiguousarray(np.asarray(L0_attn_D, np.float32).reshape(DIN, 1))
    w_out_t = np.ascontiguousarray(np.asarray(L0_attn_W_out, np.float32).T)
    ffn_k = np.ascontiguousarray(np.asarray(L0_ffn_kernel, np.float32))
    ffn_k_t = np.ascontiguousarray(ffn_k.T)
    ffn_b = np.ascontiguousarray(np.asarray(L0_ffn_bias, np.float32).reshape(1, DM))

    in_maps = []
    for i in range(NCORES):
        lo = i * TOK
        xs = np.zeros((POS, DM), np.float32)
        if i > 0:
            xs[0:HALO] = x[lo - HALO:lo]
        xs[HALO:HALO + TOK] = x[lo:lo + TOK]
        in_maps.append({
            "x": xs,
            "w_in_4": w_in_4,
            "conv_w": conv_w,
            "conv_b": conv_b,
            "w_x_t": w_x_t,
            "w_dt_t": w_dt_t,
            "b_dt": b_dt_a,
            "a_log_sl": np.ascontiguousarray(a_log[i * CPC:(i + 1) * CPC]),
            "d_vec": d_vec,
            "w_out_t": w_out_t,
            "ffn_k_t": ffn_k_t,
            "ffn_k": ffn_k,
            "ffn_b": ffn_b,
        })
    return in_maps


def kernel(L0_attn_x, L0_attn_W_in, L0_attn_conv_w, L0_attn_conv_b, L0_attn_W_x,
           L0_attn_W_dt, L0_attn_b_dt, L0_attn_A_log, L0_attn_D, L0_attn_W_out,
           L0_ffn_kernel, L0_ffn_bias):
    global _NC_CACHE
    in_maps = make_in_maps(L0_attn_x, L0_attn_W_in, L0_attn_conv_w, L0_attn_conv_b,
                           L0_attn_W_x, L0_attn_W_dt, L0_attn_b_dt, L0_attn_A_log,
                           L0_attn_D, L0_attn_W_out, L0_ffn_kernel, L0_ffn_bias)
    if _NC_CACHE is None:
        _NC_CACHE = build_nc()
    res = run_bass_kernel_spmd(_NC_CACHE, in_maps, core_ids=list(range(NCORES)))
    out = np.concatenate([res.results[i]["out"] for i in range(NCORES)], axis=0)
    return out.reshape(1, 8192, DM).astype(np.float32)


# version 15 (final)
# speedup vs baseline: 1.1037x; 1.0259x over previous
import numpy as np
import concourse.bass as bass
import concourse.mybir as mybir
from concourse.bass_utils import run_bass_kernel_spmd
from concourse.aot_env import allow_ici_all_to_all
from contextlib import ExitStack

F32 = mybir.dt.float32
BF16 = mybir.dt.bfloat16
I32 = mybir.dt.int32
ALU = mybir.AluOpType
ACTF = mybir.ActivationFunctionType
AX = mybir.AxisListType

NCORES = 8
TOK = 1024
HALO = 3
POS = 1152
NT = POS // 128
DM = 2048
DIN = 4096
DST = 16
DTR = 128
DCH = DIN // 128
KC = DM // 128
CPC = DIN // NCORES
BL = CPC // 128
TQ = 8
TQW = 1024
MAGIC = 12582912.0
EPS_AMAX = 1e-18
LN127 = 4.844187086458591

_DEBUG = False


def build_nc():
    nc = bass.Bass("TRN2", target_bir_lowering=False)

    x_in = nc.dram_tensor("x", [POS, DM], F32, kind="ExternalInput").ap()
    w_in_4 = nc.dram_tensor("w_in_4", [128, 2 * DCH, KC, 128], F32, kind="ExternalInput").ap()
    conv_w = nc.dram_tensor("conv_w", [DIN, 4], F32, kind="ExternalInput").ap()
    conv_b = nc.dram_tensor("conv_b", [DIN, 1], F32, kind="ExternalInput").ap()
    w_x_t = nc.dram_tensor("w_x_t", [DIN, 160], F32, kind="ExternalInput").ap()
    w_dt_t = nc.dram_tensor("w_dt_t", [DTR, DIN], F32, kind="ExternalInput").ap()
    b_dt = nc.dram_tensor("b_dt", [DIN, 1], F32, kind="ExternalInput").ap()
    a_log_sl = nc.dram_tensor("a_log_sl", [CPC, DST], F32, kind="ExternalInput").ap()
    d_vec = nc.dram_tensor("d_vec", [DIN, 1], F32, kind="ExternalInput").ap()
    w_out_t = nc.dram_tensor("w_out_t", [DIN, DM], F32, kind="ExternalInput").ap()
    ffn_k_t = nc.dram_tensor("ffn_k_t", [DM, DM], F32, kind="ExternalInput").ap()
    ffn_k = nc.dram_tensor("ffn_k", [DM, DM], F32, kind="ExternalInput").ap()
    ffn_b = nc.dram_tensor("ffn_b", [1, DM], F32, kind="ExternalInput").ap()
    out_d = nc.dram_tensor("out", [TOK, DM], F32, kind="ExternalOutput").ap()

    ROWS_SH = 2 * CPC + 2 * DST
    a2a_in = nc.dram_tensor("a2a_in", [NCORES, ROWS_SH, TOK], F32)
    a2a_out = nc.dram_tensor("a2a_out", [NCORES, ROWS_SH, TOK], F32)
    y_in = nc.dram_tensor("y_in", [NCORES, CPC, TOK], F32)
    y_out = nc.dram_tensor("y_out", [NCORES, CPC, TOK], F32)
    u_scr = nc.dram_tensor("u_scr", [DIN, TOK], F32)
    sz_scr = nc.dram_tensor("sz_scr", [DIN, POS], F32)
    bq_scr = nc.dram_tensor("bq_scr", [1, DM], F32)
    o_scr = nc.dram_tensor("o_scr", [TOK, DM], F32)

    dbg = {}
    if _DEBUG:
        dbg["xn"] = nc.dram_tensor("dbg_xn", [POS, DM], F32, kind="ExternalOutput").ap()
        dbg["xs"] = nc.dram_tensor("dbg_xs", [DIN, POS], F32, kind="ExternalOutput").ap()
        dbg["xdbl"] = nc.dram_tensor("dbg_xdbl", [160, TOK], F32, kind="ExternalOutput").ap()
        dbg["dt"] = nc.dram_tensor("dbg_dt", [DIN, TOK], F32, kind="ExternalOutput").ap()
        dbg["yown"] = nc.dram_tensor("dbg_yown", [DIN, TOK], F32, kind="ExternalOutput").ap()
        dbg["ffnq"] = nc.dram_tensor("dbg_ffnq", [TOK, DM], F32, kind="ExternalOutput").ap()
        dbg["xqt"] = nc.dram_tensor("dbg_xqt", [128, KC * POS], BF16, kind="ExternalOutput").ap()
        dbg["xqt_e"] = nc.dram_tensor("dbg_xqt_e", [128, KC * POS], BF16, kind="ExternalOutput").ap()
        dbg["xqb"] = nc.dram_tensor("dbg_xqb", [POS, DM], BF16, kind="ExternalOutput").ap()
        dbg["qd"] = nc.dram_tensor("dbg_qd", [128, 4], F32, kind="ExternalOutput").ap()
        dbg["sq0"] = nc.dram_tensor("dbg_sq0", [128, DM], F32, kind="ExternalOutput").ap()
        dbg["kq"] = nc.dram_tensor("dbg_kq", [128, KC * DM], BF16, kind="ExternalOutput").ap()
        dbg["bcolq"] = nc.dram_tensor("dbg_bcolq", [128, 128], F32, kind="ExternalOutput").ap()
        dbg["dgq0"] = nc.dram_tensor("dbg_dgq0", [128, 1024], BF16, kind="ExternalOutput").ap()
        dbg["dtut"] = nc.dram_tensor("dbg_dtut", [128, 4096], BF16, kind="ExternalOutput").ap()
        dbg["dbu0"] = nc.dram_tensor("dbg_dbu0", [128, 1024], F32, kind="ExternalOutput").ap()

    cnt = {k: 0 for k in ("QS", "QC", "QG", "SV", "SS", "ST", "SG", "SD")}
    es = ExitStack()
    sem = {k: es.enter_context(nc.semaphore(k)) for k in cnt}

    idb = es.enter_context(nc.sbuf_tensor("idb", [128, 128], BF16))
    a_sb = es.enter_context(nc.sbuf_tensor("a_sb", [128, BL * DST], F32))
    carry = es.enter_context(nc.sbuf_tensor("carry", [128, BL * DST], F32))
    ln127c = es.enter_context(nc.sbuf_tensor("ln127c", [128, 1], F32))
    idf32 = es.enter_context(nc.sbuf_tensor("idf32", [128, 128], F32))

    block = es.enter_context(nc.Block())

    def snap():
        return dict(cnt)

    def wait_all(eng, snp, skip=()):
        for k, v in snp.items():
            if v > 0 and k not in skip:
                eng.wait_ge(sem[k], v)

    def dma(eng, q, out, in_):
        eng.dma_start(out=out, in_=in_).then_inc(sem[q], 16)
        cnt[q] += 16

    def inc(instr, name, n=1):
        instr.then_inc(sem[name], n)
        cnt[name] += n
        return instr

    def maxtree(v, buf, width, out_col):
        w = width // 2
        while w >= 1:
            inc(v.tensor_tensor(buf[:, 0:w], buf[:, 0:w], buf[:, w:2 * w], op=ALU.max), "SV")
            w //= 2
        inc(v.tensor_scalar(out_col, buf[:, 0:1], EPS_AMAX, None, op0=ALU.max), "SV")

    def preload_ln_exp_set(eng):
        from concourse.hw_specs import get_activation_tables
        for idx, funcs in enumerate(get_activation_tables(nc.m.arch).values()):
            if ACTF.Exp in funcs and ACTF.Ln in funcs:
                eng.add_instruction(mybir.InstLoadActFuncSet(
                    name=nc.get_next_instruction_name(),
                    act_func_set_id=idx, ins=[], outs=[]))
                return
        raise ValueError("no set with Exp+Ln")

    def sfence(s):
        s.drain()
        inc(s.nop(), "SS")

    with (
        nc.sbuf_tensor("pi", [128, 128], I32) as pi,
        nc.sbuf_tensor("fi", [128, 128], I32) as fi,
        nc.sbuf_tensor("idf0", [128, 128], F32) as idf0,
        nc.sbuf_tensor("alog", [128, BL * DST], F32) as alog,
    ):
        @block.gpsimd
        def _(g):
            inc(g.iota(pi[:, :], [[0, 128]], base=0, channel_multiplier=1), "SG")
            inc(g.iota(fi[:, :], [[1, 128]], base=0, channel_multiplier=0), "SG")

        @block.sync
        def _(s):
            for b in range(BL):
                dma(s, "QS", alog[:, b * DST:(b + 1) * DST],
                    a_log_sl[b * 128:(b + 1) * 128, :])

        @block.vector
        def _(v):
            v.wait_ge(sem["SG"], cnt["SG"])
            inc(v.memset(ln127c[:, :], LN127), "SV")
            inc(v.tensor_tensor(idf32[:, :], pi[:, :], fi[:, :], op=ALU.is_equal), "SV")
            inc(v.tensor_copy(idb[:, :], idf32[:, :]), "SV")
            inc(v.memset(carry[:, :], 0.0), "SV")

        @block.scalar
        def _(s):
            s.wait_ge(sem["QS"], cnt["QS"])
            s.activation(a_sb[:, :], alog[:, :], ACTF.Exp, bias=0.0, scale=1.0)
            sfence(s)

        @block.vector
        def _(v):
            v.wait_ge(sem["SS"], cnt["SS"])
            inc(v.tensor_scalar(a_sb[:, :], a_sb[:, :], -1.0, None, op0=ALU.mult), "SV")

        s0 = snap()

    es_xq = ExitStack()
    xqT = es_xq.enter_context(nc.sbuf_tensor("xqT", [128, KC * POS], BF16))
    es1 = ExitStack()
    xnT = es1.enter_context(nc.sbuf_tensor("xnT", [128, KC * POS], BF16))
    with (
        nc.sbuf_tensor("xt", [128, DM], F32) as xt,
        nc.sbuf_tensor("xn_f", [128, DM], F32) as xn_f,
        nc.sbuf_tensor("xn_b", [128, DM], BF16) as xn_b,
        nc.sbuf_tensor("xq_b", [128, DM], BF16) as xq_b,
        nc.sbuf_tensor("sq", [128, DM], F32) as sq,
        nc.sbuf_tensor("c0", [128, 1], F32) as c0,
        nc.sbuf_tensor("c1", [128, 1], F32) as c1,
        nc.sbuf_tensor("c2", [128, 1], F32) as c2,
        nc.sbuf_tensor("c3", [128, 1], F32) as c3,
        nc.sbuf_tensor("dbgq", [128, 4], F32) as dbgq,
        nc.sbuf_tensor("spc", [128, 256], F32) as spc,
        nc.psum_tensor("ptr", [128, 128], BF16) as ptr,
        nc.psum_tensor("ptr2", [128, 128], BF16) as ptr2,
        nc.psum_tensor("ptr3", [128, 128], BF16) as ptr3,
        nc.psum_tensor("ptr4", [128, 128], BF16) as ptr4,
    ):
        ptrs = [(ptr, ptr2), (ptr3, ptr4)]
        ev_s1evac = {}
        for tt in range(NT):
            base_v, base_s, base_t, base_q = cnt["SV"], cnt["SS"], cnt["ST"], cnt["QS"]

            @block.sync
            def _(s, tt=tt):
                if tt == 0:
                    wait_all(s, s0, skip=("QS",))
                else:
                    s.wait_ge(sem["ST"], base_t)
                dma(s, "QS", xt[:, :], x_in[tt * 128:(tt + 1) * 128, :])

            @block.scalar
            def _(s, tt=tt):
                s.wait_ge(sem["QS"], cnt["QS"])
                s.activation(sq[:, :], xt[:, :], ACTF.Square, bias=0.0, scale=1.0)
                sfence(s)

            @block.vector
            def _(v, tt=tt):
                v.wait_ge(sem["QS"], cnt["QS"])
                inc(v.reduce_sum(c0[:, :], xt[:, :], axis=AX.X), "SV")
                v.wait_ge(sem["SS"], base_s + 1)
                inc(v.reduce_sum(c1[:, :], sq[:, :], axis=AX.X), "SV")
                inc(v.tensor_scalar(c0[:, :], c0[:, :], 1.0 / DM, None, op0=ALU.mult), "SV")
                inc(v.tensor_scalar(c1[:, :], c1[:, :], 1.0 / DM, 1e-5,
                                    op0=ALU.mult, op1=ALU.add), "SV")
                inc(v.tensor_copy(spc[:, :], xt[:, 0:256]), "SV")
                inc(v.tensor_tensor(c2[:, :], c0[:, :], c0[:, :], op=ALU.mult), "SV")
                inc(v.tensor_copy(spc[:, :], xt[:, 256:512]), "SV")
                inc(v.tensor_tensor(c1[:, :], c1[:, :], c2[:, :], op=ALU.subtract), "SV")

            @block.scalar
            def _(s, tt=tt):
                s.wait_ge(sem["SV"], base_v + 8)
                inc(s.activation(c1[:, :], c1[:, :], ACTF.Ln, bias=0.0, scale=1.0), "SS")
                s.activation(c1[:, :], c1[:, :], ACTF.Exp, bias=0.0, scale=-0.5)
                sfence(s)

            @block.vector
            def _(v, tt=tt):
                v.wait_ge(sem["SS"], base_s + 3)
                inc(v.tensor_scalar(xn_f[:, :], xt[:, :], c0[:, :], c1[:, :],
                                    op0=ALU.subtract, op1=ALU.mult), "SV")
                inc(v.tensor_scalar(sq[:, :], xn_f[:, :], -1.0, None, op0=ALU.mult), "SV")
                inc(v.tensor_tensor(sq[:, :], sq[:, :], xn_f[:, :], op=ALU.max), "SV")
                inc(v.reduce_max(c2[:, :], sq[:, :], axis=AX.X), "SV")
                inc(v.tensor_copy(xn_b[:, :], xn_f[:, :]), "SV")
                inc(v.tensor_scalar(c2[:, :], c2[:, :], EPS_AMAX, None, op0=ALU.max), "SV")
                inc(v.tensor_copy(spc[:, :], xn_f[:, 0:256]), "SV")
                inc(v.reciprocal(c3[:, :], c2[:, :]), "SV")
                inc(v.tensor_copy(spc[:, :], xn_f[:, 256:512]), "SV")
                inc(v.tensor_scalar(sq[:, :], xn_f[:, :], c3[:, :], None, op0=ALU.mult), "SV")
                inc(v.tensor_scalar(sq[:, :], sq[:, :], 127.0, MAGIC,
                                    op0=ALU.mult, op1=ALU.add), "SV")
                inc(v.tensor_scalar(sq[:, :], sq[:, :], MAGIC + 127.0, MAGIC - 127.0,
                                    op0=ALU.min, op1=ALU.max), "SV")
                inc(v.tensor_scalar(sq[:, :], sq[:, :], MAGIC, c2[:, :],
                                    op0=ALU.subtract, op1=ALU.mult), "SV")
                inc(v.tensor_scalar(xq_b[:, :], sq[:, :], 1.0 / 127.0, None, op0=ALU.mult), "SV")

            if _DEBUG:
                @block.gpsimd
                def _(g, tt=tt):
                    g.wait_ge(sem["SV"], base_v + 10)
                    dma(g, "QG", dbg["xn"][tt * 128:(tt + 1) * 128, :], xn_f[:, :])
                    g.wait_ge(sem["SV"], cnt["SV"])
                    dma(g, "QG", dbg["xqb"][tt * 128:(tt + 1) * 128, :], xq_b[:, :])

            @block.tensor
            def _(t, tt=tt):
                t.wait_ge(sem["SV"], cnt["SV"])
                for kcc in range(KC):
                    pa, pb = ptrs[kcc % 2]
                    gi = tt * KC + kcc - 2
                    if gi >= 0 and gi in ev_s1evac:
                        t.wait_ge(sem["SS"], ev_s1evac[gi])
                    t.transpose(pa[:, :], xn_b[:, kcc * 128:(kcc + 1) * 128], idb[:, :])
                    inc(t.transpose(pb[:, :], xq_b[:, kcc * 128:(kcc + 1) * 128], idb[:, :]), "ST")

                    @block.scalar
                    def _(s, tt=tt, kcc=kcc, pa=pa, pb=pb):
                        s.wait_ge(sem["ST"], cnt["ST"])
                        inc(s.copy(xnT[:, kcc * POS + tt * 128: kcc * POS + (tt + 1) * 128],
                                   pa[:, :]), "SS")
                        inc(s.copy(xqT[:, kcc * POS + tt * 128: kcc * POS + (tt + 1) * 128],
                                   pb[:, :]), "SS")
                        ev_s1evac[tt * KC + kcc] = cnt["SS"]

        if _DEBUG:
            @block.gpsimd
            def _(g):
                g.wait_ge(sem["SS"], cnt["SS"])
                dma(g, "QG", dbg["xqt_e"], xqT[:, :])

        s1 = snap()

    es2 = ExitStack()
    _e2 = es2.enter_context
    wsl = [_e2(nc.sbuf_tensor("wslA", [128, DM], F32)),
           _e2(nc.sbuf_tensor("wslB", [128, DM], F32))]
    wslb = [_e2(nc.sbuf_tensor("wslbA", [128, DM], BF16)),
            _e2(nc.sbuf_tensor("wslbB", [128, DM], BF16))]
    xs_sb = [_e2(nc.sbuf_tensor("xs_sbA", [128, POS], F32)),
             _e2(nc.sbuf_tensor("xs_sbB", [128, POS], F32))]
    zt = [_e2(nc.sbuf_tensor("ztA", [128, POS], F32)),
          _e2(nc.sbuf_tensor("ztB", [128, POS], F32))]
    cv1 = [_e2(nc.sbuf_tensor("cv1A", [128, TOK], F32)),
           _e2(nc.sbuf_tensor("cv1B", [128, TOK], F32))]
    cv2 = [_e2(nc.sbuf_tensor("cv2A", [128, TOK], F32)),
           _e2(nc.sbuf_tensor("cv2B", [128, TOK], F32))]
    u_f = [_e2(nc.sbuf_tensor("u_fA", [128, TOK], F32)),
           _e2(nc.sbuf_tensor("u_fB", [128, TOK], F32))]
    cwc = [_e2(nc.sbuf_tensor("cwcA", [128, 4], F32)),
           _e2(nc.sbuf_tensor("cwcB", [128, 4], F32))]
    cbc = [_e2(nc.sbuf_tensor("cbcA", [128, 1], F32)),
           _e2(nc.sbuf_tensor("cbcB", [128, 1], F32))]
    wx_f = _e2(nc.sbuf_tensor("wx_f", [128, DCH * 160], F32))
    wdt_b = _e2(nc.sbuf_tensor("wdt_b", [128, DIN], BF16))
    dtr_b = _e2(nc.sbuf_tensor("dtr_b", [128, TOK], BF16))
    bc_sb = _e2(nc.sbuf_tensor("bc_sb", [32, TOK], F32))
    dt_sb = [_e2(nc.sbuf_tensor("dt_sbA", [128, TOK], F32)),
             _e2(nc.sbuf_tensor("dt_sbB", [128, TOK], F32))]
    et = _e2(nc.sbuf_tensor("et", [128, TOK], F32))
    bdtc = [_e2(nc.sbuf_tensor("bdtcA", [128, 1], F32)),
            _e2(nc.sbuf_tensor("bdtcB", [128, 1], F32))]
    uld = [_e2(nc.sbuf_tensor("uldA", [128, TOK], F32)),
           _e2(nc.sbuf_tensor("uldB", [128, TOK], F32))]
    dtu_sb = [_e2(nc.sbuf_tensor("dtu_sbA", [128, TOK], F32)),
              _e2(nc.sbuf_tensor("dtu_sbB", [128, TOK], F32))]
    pp = _e2(nc.psum_tensor("pp", [128, 512], F32))
    pq = _e2(nc.psum_tensor("pq", [128, 512], F32))
    pr = _e2(nc.psum_tensor("pr", [128, 512], F32))
    pd0 = _e2(nc.psum_tensor("pd0", [128, 512], F32))
    px1 = _e2(nc.psum_tensor("px1", [128, TOK], F32))
    px2 = _e2(nc.psum_tensor("px2", [32, TOK], F32))
    if True:
        @block.sync
        def _(s):
            wait_all(s, s1, skip=("QS",))
            dma(s, "QS", wx_f.ap().rearrange("p (c f) -> p c f", c=DCH),
                w_x_t.rearrange("(c p) f -> p c f", p=128))
            for k, stg in enumerate([uld[0], uld[1], cv1[0], cv1[1]]):
                dma(s, "QS", stg[:, :], w_dt_t[:, k * 1024:(k + 1) * 1024])

        for k, stg in enumerate([uld[0], uld[1], cv1[0], cv1[1]]):
            @block.vector
            def _(v, k=k, stg=stg):
                v.wait_ge(sem["QS"], cnt["QS"])
                inc(v.tensor_copy(wdt_b[:, k * 1024:(k + 1) * 1024], stg[:, :]), "SV")

        ev_cast = {}
        ev_mm = {}
        ev_evac = {}
        ev_conv = {}
        ev_ub = {}
        ev_zspill = {}
        ev_silu = {}

        NSLICES = [(0, 512), (512, 512), (1024, 128)]

        for c in range(2 * DCH):
            is_xs = c < DCH
            par = c % 2
            Q = "QS" if par == 0 else "QC"

            @block.sync if par == 0 else block.scalar
            def _(s, c=c, par=par, Q=Q, is_xs=is_xs):
                if c - 2 in ev_cast:
                    s.wait_ge(sem["SV"], ev_cast[c - 2])
                if is_xs and c - 2 >= 0:
                    s.wait_ge(sem["SV"], ev_conv[c - 2])
                dma(s, Q, wsl[par].ap().rearrange("p (kc ch) -> p kc ch", kc=KC),
                    w_in_4[:, c, :, :])
                if is_xs:
                    dma(s, Q, cwc[par][:, :], conv_w[c * 128:(c + 1) * 128, :])
                    dma(s, Q, cbc[par][:, :], conv_b[c * 128:(c + 1) * 128, :])

            q_slab = cnt[Q]

            @block.vector
            def _(v, c=c, par=par, Q=Q, q_slab=q_slab, is_xs=is_xs):
                v.wait_ge(sem[Q], q_slab)
                if c - 2 in ev_mm:
                    v.wait_ge(sem["ST"], ev_mm[c - 2])
                inc(v.tensor_copy(wslb[par][:, :], wsl[par][:, :]), "SV")
                ev_cast[c] = cnt["SV"]

            NPS = [pp, pq, pr]
            for isl, (n0, nw) in enumerate(NSLICES):
                ps = NPS[isl]

                @block.tensor
                def _(t, c=c, par=par, n0=n0, nw=nw, ps=ps, isl=isl):
                    t.wait_ge(sem["SV"], ev_cast[c])
                    if (c - 1, isl) in ev_evac:
                        k, v_ = ev_evac[(c - 1, isl)]
                        t.wait_ge(sem[k], v_)
                    last = None
                    for kcc in range(KC):
                        last = t.matmul(ps[:, 0:nw], wslb[par][:, kcc * 128:(kcc + 1) * 128],
                                        xnT[:, kcc * POS + n0: kcc * POS + n0 + nw],
                                        start=(kcc == 0), stop=(kcc == KC - 1))
                    inc(last, "ST")
                    if isl == len(NSLICES) - 1:
                        ev_mm[c] = cnt["ST"]

                if is_xs:
                    @block.scalar
                    def _(se, c=c, par=par, n0=n0, nw=nw, ps=ps, isl=isl):
                        se.wait_ge(sem["ST"], cnt["ST"])
                        if isl == 0 and c - 2 in ev_conv:
                            se.wait_ge(sem["SV"], ev_conv[c - 2])
                        ins = se.copy(xs_sb[par][:, n0:n0 + nw], ps[:, 0:nw])
                        if isl < 2:
                            inc(ins, "SS")
                        else:
                            sfence(se)
                        ev_evac[(c, isl)] = ("SS", cnt["SS"])
                else:
                    @block.scalar
                    def _(se, c=c, par=par, n0=n0, nw=nw, ps=ps, isl=isl):
                        se.wait_ge(sem["ST"], cnt["ST"])
                        if isl == 0 and c - 2 >= DCH:
                            se.wait_ge(sem["QG"], ev_zspill[c - 2])
                        inc(se.activation(zt[par][:, n0:n0 + nw], ps[:, 0:nw], ACTF.Silu,
                                          bias=0.0, scale=1.0), "SS")
                        ev_evac[(c, isl)] = ("SS", cnt["SS"])

            if is_xs:
                @block.vector
                def _(v, c=c, par=par):
                    v.wait_ge(sem["SS"], ev_evac[(c, 2)][1])
                    if c - 2 in ev_silu:
                        v.wait_ge(sem["SS"], ev_silu[c - 2])
                    inc(v.tensor_scalar(cv1[par][:, :], xs_sb[par][:, 0:TOK], cwc[par][:, 0:1],
                                        cbc[par][:, :], op0=ALU.mult, op1=ALU.add), "SV")
                    inc(v.scalar_tensor_tensor(cv2[par][:, :], xs_sb[par][:, 1:TOK + 1],
                                               cwc[par][:, 1:2],
                                               cv1[par][:, :], op0=ALU.mult, op1=ALU.add), "SV")
                    inc(v.scalar_tensor_tensor(cv1[par][:, :], xs_sb[par][:, 2:TOK + 2],
                                               cwc[par][:, 2:3],
                                               cv2[par][:, :], op0=ALU.mult, op1=ALU.add), "SV")
                    inc(v.scalar_tensor_tensor(cv2[par][:, :], xs_sb[par][:, 3:TOK + 3],
                                               cwc[par][:, 3:4],
                                               cv1[par][:, :], op0=ALU.mult, op1=ALU.add), "SV")
                    ev_conv[c] = cnt["SV"]

                @block.scalar
                def _(se, c=c, par=par):
                    se.wait_ge(sem["SV"], ev_conv[c])
                    if c - 2 in ev_ub:
                        se.wait_ge(sem["QG"], ev_ub[c - 2])
                    se.activation(u_f[par][:, :], cv2[par][:, :], ACTF.Silu, bias=0.0, scale=1.0)
                    sfence(se)
                    ev_silu[c] = cnt["SS"]

                @block.gpsimd
                def _(g, c=c, par=par):
                    g.wait_ge(sem["SS"], ev_silu[c])
                    dma(g, "QG", u_scr.ap()[c * 128:(c + 1) * 128, :], u_f[par][:, :])
                    ev_ub[c] = cnt["QG"]
            else:
                @block.gpsimd
                def _(g, c=c, par=par):
                    g.wait_ge(sem["SS"], ev_evac[(c, 2)][1])
                    dma(g, "QG", sz_scr.ap()[(c - DCH) * 128:(c - DCH + 1) * 128, :],
                        zt[par][:, :])
                    ev_zspill[c] = cnt["QG"]

        ev_uxld = {}
        ev_uxmm = {}
        for c in range(DCH):
            upar = c % 2

            @block.sync
            def _(s, c=c, upar=upar):
                if c - 2 in ev_uxmm:
                    s.wait_ge(sem["ST"], ev_uxmm[c - 2])
                s.wait_ge(sem["QG"], ev_ub[c])
                dma(s, "QS", uld[upar][:, :], u_scr.ap()[c * 128:(c + 1) * 128, :])
            ev_uxld[c] = cnt["QS"]

            @block.tensor
            def _(t, c=c, upar=upar):
                t.wait_ge(sem["QS"], ev_uxld[c])
                for nh in range(2):
                    sl = slice(nh * 512, (nh + 1) * 512)
                    t.matmul(px1[:, sl], wx_f[:, c * 160:c * 160 + 128],
                             uld[upar][:, sl],
                             start=(c == 0), stop=(c == DCH - 1), skip_group_check=True)
                    ins = t.matmul(px2[:, sl], wx_f[:, c * 160 + 128:c * 160 + 160],
                                   uld[upar][:, sl],
                                   start=(c == 0), stop=(c == DCH - 1), skip_group_check=True)
                inc(ins, "ST")
            ev_uxmm[c] = cnt["ST"]

        @block.vector
        def _(v):
            v.wait_ge(sem["ST"], cnt["ST"])
            inc(v.tensor_copy(dtr_b[:, :], px1[:, :]), "SV")
            inc(v.tensor_copy(bc_sb[:, :], px2[:, :]), "SV")

        if _DEBUG:
            @block.gpsimd
            def _(g):
                g.wait_ge(sem["SV"], cnt["SV"])
                dma(g, "QG", dbg["xdbl"][128:160, :], bc_sb[:, :])

        @block.gpsimd
        def _(g):
            g.wait_ge(sem["SV"], cnt["SV"])
            for j in range(NCORES):
                dma(g, "QG", a2a_in.ap()[j, 2 * CPC:2 * CPC + 2 * DST, :], bc_sb[:, :])

        ss_dt0 = cnt["SS"]
        sv_dt0 = cnt["SV"]
        qg_dt0 = cnt["QG"]
        ev_dtmm, ev_dtexp, ev_dtu, ev_dtq, ev_dtg = {}, {}, {}, {}, {}
        for c in range(DCH):
            par = c % 2
            pda, pdb = (pp, pq) if par == 0 else (pr, pd0)

            @block.sync
            def _(s, c=c, par=par):
                if c - 2 in ev_dtu:
                    s.wait_ge(sem["SV"], ev_dtu[c - 2])
                elif c < 2:
                    s.wait_ge(sem["ST"], ev_uxmm[DCH - 2 + c])
                if c - 2 in ev_dtexp:
                    s.wait_ge(sem["SS"], ev_dtexp[c - 2])
                if c == 0:
                    s.wait_ge(sem["QG"], qg_dt0)
                dma(s, "QS", bdtc[par][:, :], b_dt[c * 128:(c + 1) * 128, :])
                dma(s, "QS", uld[par][:, :], u_scr.ap()[c * 128:(c + 1) * 128, :])
            ev_dtq[c] = cnt["QS"]

            @block.tensor
            def _(t, c=c, par=par, pda=pda, pdb=pdb):
                if c == 0:
                    t.wait_ge(sem["SV"], cnt["SV"])
                if c < 2:
                    t.wait_ge(sem["SS"], ss_dt0)
                elif c - 2 in ev_dtexp:
                    t.wait_ge(sem["SS"], ev_dtexp[c - 2])
                t.matmul(pda[:, 0:512], wdt_b[:, c * 128:(c + 1) * 128], dtr_b[:, 0:512],
                         start=True, stop=True, skip_group_check=True)
                inc(t.matmul(pdb[:, 0:512], wdt_b[:, c * 128:(c + 1) * 128], dtr_b[:, 512:1024],
                             start=True, stop=True, skip_group_check=True), "ST")
            ev_dtmm[c] = cnt["ST"]

            @block.scalar
            def _(s, c=c, par=par, pda=pda, pdb=pdb):
                s.wait_ge(sem["ST"], ev_dtmm[c])
                s.wait_ge(sem["QS"], ev_dtq[c])
                if c - 2 in ev_dtg:
                    s.wait_ge(sem["QG"], ev_dtg[c - 2])
                if c - 2 in ev_dtu:
                    s.wait_ge(sem["SV"], ev_dtu[c - 2])
                inc(s.activation(et[:, 0:512], pda[:, 0:512], ACTF.Exp,
                                 bias=bdtc[par][:, :], scale=1.0), "SS")
                inc(s.activation(et[:, 512:1024], pdb[:, 0:512], ACTF.Exp,
                                 bias=bdtc[par][:, :], scale=1.0), "SS")
                s.activation(dt_sb[par][:, :], et[:, :], ACTF.Ln, bias=1.0, scale=1.0)
                sfence(s)
            ev_dtexp[c] = cnt["SS"]

            @block.vector
            def _(v, c=c, par=par):
                v.wait_ge(sem["SS"], ev_dtexp[c])
                v.wait_ge(sem["QS"], ev_dtq[c])
                if c - 2 in ev_dtg:
                    v.wait_ge(sem["QG"], ev_dtg[c - 2])
                inc(v.tensor_tensor(dtu_sb[par][:, :], dt_sb[par][:, :], uld[par][:, :],
                                    op=ALU.mult), "SV")
            ev_dtu[c] = cnt["SV"]

            @block.gpsimd
            def _(g, c=c, par=par):
                j, r = c // BL, (c % BL) * 128
                g.wait_ge(sem["SS"], ev_dtexp[c])
                dma(g, "QG", a2a_in.ap()[j, r:r + 128, :], dt_sb[par][:, :])
                g.wait_ge(sem["SV"], ev_dtu[c])
                dma(g, "QG", a2a_in.ap()[j, CPC + r:CPC + r + 128, :], dtu_sb[par][:, :])
                if _DEBUG:
                    dma(g, "QG", dbg["dt"][c * 128:(c + 1) * 128, :], dt_sb[par][:, :])
            ev_dtg[c] = cnt["QG"]

        s2 = snap()

    es2.close()
    es1.close()


    with (
        nc.sbuf_tensor("kqb", [128, DM], BF16) as kqb,
        nc.sbuf_tensor("kq", [128, KC * DM], BF16) as kq,
        nc.sbuf_tensor("bq_rep", [128, DM], F32) as bq_rep,
        nc.sbuf_tensor("scr8", [128, DM], F32) as scr8,
        nc.sbuf_tensor("kab", [128, DM], F32) as kab,
        nc.sbuf_tensor("k0", [128, 1], F32) as k0,
        nc.sbuf_tensor("k1", [128, 1], F32) as k1,
        nc.psum_tensor("ptk", [128, 128], BF16) as ptk,
        nc.psum_tensor("ptk2", [128, 128], BF16) as ptk2,
    ):
        kqc = scr8
        with (
            nc.sbuf_tensor("bqr", [1, DM], F32) as bqr,
            nc.sbuf_tensor("b0", [1, 1], F32) as b0,
            nc.sbuf_tensor("b1", [1, 1], F32) as b1,
        ):
            @block.sync
            def _(s):
                wait_all(s, s2, skip=("QS", "QC"))
                dma(s, "QS", bqr[:, :], ffn_b)

            @block.vector
            def _(v):
                v.wait_ge(sem["QS"], cnt["QS"])
                inc(v.tensor_scalar(bq_rep[0:1, :], bqr[:, :], -1.0, None, op0=ALU.mult), "SV")
                inc(v.tensor_tensor(bq_rep[0:1, :], bq_rep[0:1, :], bqr[:, :], op=ALU.max), "SV")
                inc(v.reduce_max(b0[:, :], bq_rep[0:1, :], axis=AX.X), "SV")
                inc(v.tensor_copy(kab[0:1, 0:256], bqr[:, 0:256]), "SV")
                inc(v.tensor_scalar(b0[:, :], b0[:, :], EPS_AMAX, None, op0=ALU.max), "SV")
                inc(v.tensor_copy(kab[0:1, 256:512], bqr[:, 0:256]), "SV")
                inc(v.reciprocal(b1[:, :], b0[:, :]), "SV")
                inc(v.tensor_copy(kab[0:1, 512:768], bqr[:, 0:256]), "SV")
                inc(v.tensor_scalar(bqr[:, :], bqr[:, :], b1[:, :], None, op0=ALU.mult), "SV")
                inc(v.tensor_scalar(bqr[:, :], bqr[:, :], 127.0, MAGIC,
                                    op0=ALU.mult, op1=ALU.add), "SV")
                inc(v.tensor_scalar(bqr[:, :], bqr[:, :], MAGIC + 127.0, MAGIC - 127.0,
                                    op0=ALU.min, op1=ALU.max), "SV")
                inc(v.tensor_scalar(bqr[:, :], bqr[:, :], MAGIC, b0[:, :],
                                    op0=ALU.subtract, op1=ALU.mult), "SV")
                inc(v.tensor_scalar(bqr[:, :], bqr[:, :], 1.0 / 127.0, None, op0=ALU.mult), "SV")

            @block.gpsimd
            def _(g):
                g.wait_ge(sem["SV"], cnt["SV"])
                dma(g, "QG", bq_scr.ap(), bqr[:, :])

            @block.sync
            def _(s):
                s.wait_ge(sem["QG"], cnt["QG"])
                dma(s, "QS", bq_rep[:, :], bq_scr.ap().broadcast_to([128, DM]))

        for fc in range(KC):
            base_v, base_t, base_s = cnt["SV"], cnt["ST"], cnt["SS"]

            @block.sync
            def _(s, fc=fc, base_v=base_v):
                s.wait_ge(sem["SV"], base_v)
                dma(s, "QS", kqc[:, :], ffn_k_t[fc * 128:(fc + 1) * 128, :])

            @block.vector
            def _(v, fc=fc):
                v.wait_ge(sem["QS"], cnt["QS"])
                inc(v.tensor_scalar(kab[:, :], kqc[:, :], -1.0, None, op0=ALU.mult), "SV")
                inc(v.tensor_tensor(kab[:, :], kab[:, :], kqc[:, :], op=ALU.max), "SV")
                inc(v.reduce_max(k0[:, :], kab[:, :], axis=AX.X), "SV")
                inc(v.tensor_copy(kab[:, 0:256], kqc[:, 0:256]), "SV")
                inc(v.tensor_scalar(k0[:, :], k0[:, :], EPS_AMAX, None, op0=ALU.max), "SV")
                inc(v.tensor_copy(kab[:, 256:512], kqc[:, 0:256]), "SV")
                inc(v.reciprocal(k1[:, :], k0[:, :]), "SV")
                inc(v.tensor_copy(kab[:, 512:768], kqc[:, 0:256]), "SV")
                inc(v.tensor_scalar(kqc[:, :], kqc[:, :], k1[:, :], None, op0=ALU.mult), "SV")
                inc(v.tensor_scalar(kqc[:, :], kqc[:, :], 127.0, MAGIC,
                                    op0=ALU.mult, op1=ALU.add), "SV")
                inc(v.tensor_scalar(kqc[:, :], kqc[:, :], MAGIC + 127.0, MAGIC - 127.0,
                                    op0=ALU.min, op1=ALU.max), "SV")
                inc(v.tensor_scalar(kqc[:, :], kqc[:, :], MAGIC, k0[:, :],
                                    op0=ALU.subtract, op1=ALU.mult), "SV")
                inc(v.tensor_scalar(kqb[:, :], kqc[:, :], 1.0 / 127.0, None, op0=ALU.mult), "SV")

            @block.tensor
            def _(t, fc=fc):
                t.wait_ge(sem["SV"], cnt["SV"])
                for dc in range(KC):
                    if dc > 0:
                        t.wait_ge(sem["SS"], cnt["SS"])
                    if dc % 2 == 0:
                        inc(t.transpose(ptk[:, :], kqb[:, dc * 128:(dc + 1) * 128], idb[:, :]), "ST")
                    else:
                        inc(t.transpose(ptk2[:, :], kqb[:, dc * 128:(dc + 1) * 128], idb[:, :]), "ST")

                    @block.scalar
                    def _(s, fc=fc, dc=dc):
                        s.wait_ge(sem["ST"], cnt["ST"])
                        src = ptk if dc % 2 == 0 else ptk2
                        inc(s.copy(kq[:, dc * DM + fc * 128: dc * DM + (fc + 1) * 128],
                                   src[:, :]), "SS")
                        if fc == KC - 1 and dc == KC - 1:
                            sfence(s)

        if _DEBUG:
            @block.gpsimd
            def _(g):
                g.wait_ge(sem["SS"], cnt["SS"])
                dma(g, "QG", dbg["xqt"], xqT[:, :])
                dma(g, "QG", dbg["kq"], kq[:, :])

        ev_ospill = {}
        with (
            nc.sbuf_tensor("xt6", [128, DM], F32) as xt6,
            nc.sbuf_tensor("obufA", [128, DM], F32) as obufA,
            nc.sbuf_tensor("obufB", [128, DM], F32) as obufB,
            nc.sbuf_tensor("f0", [128, 1], F32) as f0c,
            nc.sbuf_tensor("f1c", [128, 1], F32) as f1c,
            nc.psum_tensor("pf", [128, DM], F32) as pf,
        ):
            obuf = [obufA, obufB]
            f1 = scr8
            for tt in range(8):
                base_v, base_t, base_q = cnt["SV"], cnt["ST"], cnt["QS"]

                @block.sync
                def _(s, tt=tt, base_v=base_v):
                    s.wait_ge(sem["SV"], base_v)
                    dma(s, "QS", xt6[:, :], x_in[HALO + tt * 128: HALO + (tt + 1) * 128, :])

                @block.tensor
                def _(t, tt=tt, base_v=base_v):
                    if tt == 0:
                        t.wait_ge(sem["SS"], cnt["SS"])
                    else:
                        t.wait_ge(sem["SV"], base_v)
                    last = None
                    for nh in range(4):
                        for dc in range(KC):
                            last = t.matmul(pf[:, nh * 512:(nh + 1) * 512],
                                            xqT[:, dc * POS + HALO + tt * 128:
                                                dc * POS + HALO + (tt + 1) * 128],
                                            kq[:, dc * DM + nh * 512: dc * DM + (nh + 1) * 512],
                                            start=(dc == 0), stop=(dc == KC - 1),
                                            skip_group_check=True)
                    inc(last, "ST")

                @block.vector
                def _(v, tt=tt):
                    v.wait_ge(sem["ST"], cnt["ST"])
                    v.wait_ge(sem["QS"], cnt["QS"])
                    inc(v.tensor_tensor(f1[:, :], pf[:, :], bq_rep[:, :], op=ALU.add), "SV")
                    inc(v.tensor_scalar(f1[:, :], f1[:, :], 0.0, None, op0=ALU.max), "SV")
                    inc(v.reduce_max(f0c[:, :], f1[:, :], axis=AX.X), "SV")
                    inc(v.tensor_copy(kab[:, 0:256], f1[:, 0:256]), "SV")
                    inc(v.tensor_scalar(f0c[:, :], f0c[:, :], EPS_AMAX, None, op0=ALU.max), "SV")

                @block.vector
                def _(v, tt=tt):
                    inc(v.tensor_copy(kab[:, 256:512], f1[:, 0:256]), "SV")
                    inc(v.reciprocal(f1c[:, :], f0c[:, :]), "SV")
                    inc(v.tensor_copy(kab[:, 512:768], f1[:, 0:256]), "SV")
                    inc(v.tensor_scalar(f1[:, :], f1[:, :], f1c[:, :], None, op0=ALU.mult), "SV")
                    inc(v.tensor_scalar(f1[:, :], f1[:, :], 127.0, MAGIC,
                                        op0=ALU.mult, op1=ALU.add), "SV")
                    inc(v.tensor_scalar(f1[:, :], f1[:, :], MAGIC + 127.0, MAGIC - 127.0,
                                        op0=ALU.min, op1=ALU.max), "SV")
                    inc(v.tensor_scalar(f1[:, :], f1[:, :], MAGIC, f0c[:, :],
                                        op0=ALU.subtract, op1=ALU.mult), "SV")
                    inc(v.tensor_scalar(f1[:, :], f1[:, :], 1.0 / 127.0, None, op0=ALU.mult), "SV")
                    if tt >= 2:
                        v.wait_ge(sem["QG"], ev_ospill[tt - 2])
                    inc(v.tensor_tensor(obuf[tt % 2][:, :], xt6[:, :], f1[:, :], op=ALU.add), "SV")

                @block.gpsimd
                def _(g, tt=tt):
                    g.wait_ge(sem["SV"], cnt["SV"])
                    dma(g, "QG", o_scr.ap()[tt * 128:(tt + 1) * 128, :], obuf[tt % 2][:, :])
                    ev_ospill[tt] = cnt["QG"]

                if _DEBUG:
                    @block.gpsimd
                    def _(g, tt=tt):
                        g.wait_ge(sem["SV"], cnt["SV"])
                        dma(g, "QG", dbg["ffnq"][tt * 128:(tt + 1) * 128, :], f1[:, :])

        s6b = snap()

    es_xq.close()

    @block.gpsimd
    def _(g):
        wait_all(g, s2)
        with allow_ici_all_to_all():
            inc(g.collective_compute(
                "AllToAll", ALU.bypass,
                replica_groups=[list(range(NCORES))],
                ins=[a2a_in.ap().opt()],
                outs=[a2a_out.ap().opt()],
            ), "SG")

    s3 = snap()

    with ExitStack() as es4:
        dtq = es4.enter_context(nc.sbuf_tensor("dtq", [128, BL * TQW], F32))
        dtldA = es4.enter_context(nc.sbuf_tensor("dtldA", [128, TQW], F32))
        dtldB = es4.enter_context(nc.sbuf_tensor("dtldB", [128, TQW], F32))
        dtuT = es4.enter_context(nc.sbuf_tensor("dtuT", [128, BL * 8 * 128], BF16))
        bcolq = es4.enter_context(nc.sbuf_tensor("bcolq", [128, 128], F32))
        ball = es4.enter_context(nc.sbuf_tensor("ball", [128, TQW], F32))
        dgqA = es4.enter_context(nc.sbuf_tensor("dgqA", [128, 8 * 128], BF16))
        dgqB = es4.enter_context(nc.sbuf_tensor("dgqB", [128, 8 * 128], BF16))
        crep = es4.enter_context(nc.sbuf_tensor("crep", [128, 2 * TQW], F32))
        da = es4.enter_context(nc.sbuf_tensor("da", [128, 2 * 2 * TQW], F32))
        hb = es4.enter_context(nc.sbuf_tensor("hb", [128, 2 * TQW], F32))
        chb0 = es4.enter_context(nc.sbuf_tensor("chb0", [128, TQW], F32))
        chb1 = es4.enter_context(nc.sbuf_tensor("chb1", [128, TQW], F32))
        chb2 = es4.enter_context(nc.sbuf_tensor("chb2", [128, TQW], F32))
        chb3 = es4.enter_context(nc.sbuf_tensor("chb3", [128, TQW], F32))
        yq = es4.enter_context(nc.sbuf_tensor("yq", [128, BL * TQW], F32))
        chbs = [chb0, chb1, chb2, chb3]
        dtld = [dtldA, dtldB]
        dgq = [dgqA, dgqB]
        TE_Y = (0, 1)
        ev_tts = {}
        ev_ch = {}
        ev_ymm = {}
        ev_vy = {}
        ev_dbu = {}
        ev_dg = {}
        ev_evac = {}
        ev_tr = {}
        qg_y = {}

        py0 = es4.enter_context(nc.psum_tensor("py0", [128, TQW], F32))
        py1 = es4.enter_context(nc.psum_tensor("py1", [128, TQW], F32))
        pb0 = es4.enter_context(nc.psum_tensor("pb0", [128, TQW], F32))
        pb1 = es4.enter_context(nc.psum_tensor("pb1", [128, TQW], F32))
        if True:
            pys = {0: py0, 1: py1}
            pdbu = [pb0, pb1]
            ev_bcol = {}
            ev_da_last = {}

            @block.gpsimd
            def _(g):
                inc(g.memset(ball[:, :], 0.0), "SG")
            sg_ball = cnt["SG"]

            for tq in range(TQ):
                @block.sync
                def _(s, tq=tq):
                    if tq == 0:
                        wait_all(s, s3, skip=("QS", "QC"))
                        s.wait_ge(sem["SG"], sg_ball)
                    else:
                        s.wait_ge(sem["SS"], ev_da_last[tq - 1])
                    for b in range(BL):
                        dma(s, "QS", dtq[:, b * TQW:(b + 1) * TQW],
                            a2a_out.ap()[tq, b * 128:(b + 1) * 128, :])
                    dma(s, "QS", ball[0:DST, :], a2a_out.ap()[tq, 2 * CPC:2 * CPC + DST, :])

                qs_dtq = cnt["QS"]

                @block.tensor
                def _(t, tq=tq):
                    t.wait_ge(sem["QS"], qs_dtq)
                    if tq == 0:
                        t.wait_ge(sem["SG"], cnt["SG"])
                    else:
                        t.wait_ge(sem["SD"], ev_tts[(tq - 1, DST - 1, BL - 1)])
                    for tt in range(8):
                        ins = t.transpose(pdbu[1][:, tt * 128:(tt + 1) * 128],
                                          ball[:, tt * 128:(tt + 1) * 128], idf32[:, :])
                    inc(ins, "ST")
                ev_btr = cnt["ST"]

                @block.scalar
                def _(se, tq=tq, ev_btr=ev_btr):
                    se.wait_ge(sem["ST"], ev_btr)
                    for tt in range(8):
                        se.copy(bcolq[:, tt * DST:(tt + 1) * DST],
                                pdbu[1][:, tt * 128:tt * 128 + DST])
                    sfence(se)
                    ev_bcol[tq] = cnt["SS"]

                for b in range(BL):
                    @block.sync
                    def _(s, tq=tq, b=b):
                        if (tq, b - 2) in ev_tr:
                            s.wait_ge(sem["SS"], ev_tr[(tq, b - 2)])
                        elif tq > 0 and b < 2:
                            s.wait_ge(sem["SS"], ev_tr[(tq - 1, 2 + b)])
                        dma(s, "QS", dtld[b % 2][:, :],
                            a2a_out.ap()[tq, CPC + b * 128:CPC + (b + 1) * 128, :])
                    qs_ld = cnt["QS"]

                    @block.tensor
                    def _(t, tq=tq, b=b, qs_ld=qs_ld):
                        t.wait_ge(sem["QS"], qs_ld)
                        if tq == 0 and b == 0:
                            pass
                        else:
                            pk = (tq, b - 1) if b >= 1 else (tq - 1, BL - 1)
                            if pk in ev_tr:
                                t.wait_ge(sem["SS"], ev_tr[pk])
                        if tq > 0 and b < 2:
                            t.wait_ge(sem["SD"], ev_tts[(tq - 1, DST - 1, BL - 1)])
                        for tt in range(8):
                            inc(t.transpose(pdbu[b % 2][:, tt * 128:(tt + 1) * 128],
                                            dtld[b % 2][:, tt * 128:(tt + 1) * 128],
                                            idf32[:, :]), "ST")

                    @block.scalar
                    def _(se, tq=tq, b=b):
                        se.wait_ge(sem["ST"], cnt["ST"])
                        se.copy(dtuT[:, b * TQW:(b + 1) * TQW], pdbu[b % 2][:, :])
                        sfence(se)
                        ev_tr[(tq, b)] = cnt["SS"]

                for si in range(DST):
                    spar = si % 2

                    @block.sync
                    def _(s, tq=tq, si=si, spar=spar):
                        ptq, psi = (tq, si - 2) if si >= 2 else (tq - 1, DST - 2 + si)
                        if ptq >= 0:
                            s.wait_ge(sem["SV"], ev_ch[(ptq, psi, BL - 1)])
                        dma(s, "QS", crep[:, spar * TQW:(spar + 1) * TQW],
                            a2a_out.ap()[tq, 2 * CPC + DST + si:2 * CPC + DST + si + 1, :]
                            .broadcast_to([128, TQW]))

                    qs_bc = cnt["QS"]

                    @block.scalar
                    def _(se, tq=tq, si=si, spar=spar):
                        ptq, psi = (tq, si - 2) if si >= 2 else (tq - 1, DST - 2 + si)
                        if ptq >= 0:
                            se.wait_ge(sem["ST"], ev_dbu[(ptq, psi, BL - 1)])
                        for tt in range(8):
                            se.activation(dgq[spar][:, tt * 128:(tt + 1) * 128], idb[:, :],
                                          ACTF.Copy, bias=0.0,
                                          scale=bcolq[:, tt * DST + si:tt * DST + si + 1])
                        sfence(se)
                        ev_dg[(tq, si)] = cnt["SS"]

                    if _DEBUG and tq == 0 and si == 0:
                        @block.sync
                        def _(s):
                            s.wait_ge(sem["SS"], ev_dg[(0, 0)])
                            s.wait_ge(sem["SS"], ev_tr[(0, BL - 1)])
                            dma(s, "QS", dbg["bcolq"], bcolq[:, :])
                            dma(s, "QS", dbg["dgq0"], dgq[0][:, :])
                            dma(s, "QS", dbg["dtut"], dtuT[:, :])

                    ev_da = {}

                    def emit_da(pid, tq=tq, si=si):
                        q = si * 2 + pid
                        ppar = q % 2

                        @block.scalar
                        def _(se, tq=tq, si=si, pid=pid, q=q, ppar=ppar):
                            ptq, q2 = (tq, q - 2) if q >= 2 else (tq - 1, 2 * DST - 2 + q)
                            if ptq >= 0:
                                se.wait_ge(sem["SD"], ev_tts[(ptq, q2 // 2, 2 * (q2 % 2) + 1)])
                            for bi in range(2):
                                b = 2 * pid + bi
                                ins = se.activation(
                                    da[:, (ppar * 2 + bi) * TQW:(ppar * 2 + bi + 1) * TQW],
                                    dtq[:, b * TQW:(b + 1) * TQW], ACTF.Exp, bias=0.0,
                                    scale=a_sb[:, b * DST + si:b * DST + si + 1])
                                if bi == 0:
                                    inc(ins, "SS")
                            sfence(se)
                            ev_da[q] = cnt["SS"]
                            ev_da_last[tq] = cnt["SS"]

                    def emit_dbu(pid, tq=tq, si=si):
                        @block.tensor
                        def _(t, tq=tq, si=si, pid=pid):
                            t.wait_ge(sem["SS"], ev_dg[(tq, si)])
                            for bi in range(2):
                                b = 2 * pid + bi
                                it = si * BL + b
                                pit = it - 2
                                if pit >= 0:
                                    t.wait_ge(sem["SD"], tq * 64 + pit + 1)
                                else:
                                    t.wait_ge(sem["SS"], ev_tr[(tq, BL - 1)])
                                last = None
                                for tt in range(8):
                                    last = t.matmul(
                                        pdbu[it % 2][:, tt * 128:(tt + 1) * 128],
                                        dtuT[:, b * TQW + tt * 128: b * TQW + (tt + 1) * 128],
                                        dgq[si % 2][:, tt * 128:(tt + 1) * 128],
                                        start=True, stop=True, skip_group_check=True)
                                inc(last, "ST")
                                ev_dbu[(tq, si, b)] = cnt["ST"]

                    def emit_ve(pid, tq=tq, si=si, spar=spar, qs_bc=qs_bc):
                        q = si * 2 + pid
                        ppar = q % 2

                        @block.vector
                        def _(v, tq=tq, si=si, pid=pid, q=q, ppar=ppar, spar=spar):
                            v.wait_ge(sem["SS"], ev_da[q])
                            if pid == 0:
                                v.wait_ge(sem["QS"], qs_bc)
                            for bi in range(2):
                                b = 2 * pid + bi
                                it = si * BL + b
                                col = b * DST + si
                                v.wait_ge(sem["ST"], ev_dbu[(tq, si, b)])
                                if it == 0 and tq > 0:
                                    lcol = (BL - 1) * DST + (DST - 1)
                                    inc(v.tensor_scalar(carry[:, lcol:lcol + 1],
                                                        hb[:, TQW + TQW - 1:2 * TQW],
                                                        1.0, None, op0=ALU.mult), "SV")
                                inc(v.tensor_tensor_scan(
                                    hb[:, (it % 2) * TQW:(it % 2 + 1) * TQW],
                                    da[:, (ppar * 2 + bi) * TQW:(ppar * 2 + bi + 1) * TQW],
                                    pdbu[it % 2][:, :], carry[:, col:col + 1],
                                    op0=ALU.mult, op1=ALU.add), "SD")
                                ev_tts[(tq, si, b)] = cnt["SD"]
                                assert cnt["SD"] == tq * 64 + it + 1
                                pit = it - 1
                                if pit >= 0 and tq < TQ - 1:
                                    psi2, pb2 = pit // BL, pit % BL
                                    pcol = pb2 * DST + psi2
                                    inc(v.tensor_scalar(carry[:, pcol:pcol + 1],
                                                        hb[:, (pit % 2) * TQW + TQW - 1:
                                                           (pit % 2) * TQW + TQW],
                                                        1.0, None, op0=ALU.mult), "SV")
                                if b in TE_Y:
                                    pk = (tq, si - 1, b)
                                    if pk in ev_ymm:
                                        v.wait_ge(sem["ST"], ev_ymm[pk])
                                    elif tq > 0:
                                        v.wait_ge(sem["ST"], ev_ymm[(tq - 1, DST - 1, b)])
                                cb = chbs[b]
                                inc(v.tensor_tensor(cb[:, :],
                                                    hb[:, (it % 2) * TQW:(it % 2 + 1) * TQW],
                                                    crep[:, spar * TQW:(spar + 1) * TQW],
                                                    op=ALU.mult), "SV")
                                ev_ch[(tq, si, b)] = cnt["SV"]
                                if b not in TE_Y:
                                    if si == 0 and tq > 0 and b == 2:
                                        v.wait_ge(sem["QG"], qg_y[tq - 1])
                                    if si == 0:
                                        inc(v.tensor_copy(yq[:, b * TQW:(b + 1) * TQW],
                                                          cb[:, :]), "SV")
                                    else:
                                        inc(v.tensor_tensor(yq[:, b * TQW:(b + 1) * TQW],
                                                            yq[:, b * TQW:(b + 1) * TQW],
                                                            cb[:, :], op=ALU.add), "SV")
                                    ev_vy[(tq, si, b)] = cnt["SV"]

                    def emit_ymm(pid, tq=tq, si=si):
                        @block.tensor
                        def _(t, tq=tq, si=si, pid=pid):
                            for bi in range(2):
                                b = 2 * pid + bi
                                if b not in TE_Y:
                                    continue
                                t.wait_ge(sem["SV"], ev_ch[(tq, si, b)])
                                if si == 0 and tq > 0:
                                    t.wait_ge(sem["SS"], ev_evac[tq - 1])
                                cb = chbs[b]
                                t.matmul(pys[b][:, 0:512], idf32[:, :], cb[:, 0:512],
                                         start=(si == 0), stop=(si == DST - 1),
                                         skip_group_check=True)
                                inc(t.matmul(pys[b][:, 512:TQW], idf32[:, :], cb[:, 512:TQW],
                                             start=(si == 0), stop=(si == DST - 1),
                                             skip_group_check=True), "ST")
                                ev_ymm[(tq, si, b)] = cnt["ST"]

                    emit_da(0)
                    emit_da(1)
                    emit_dbu(0)
                    emit_dbu(1)
                    emit_ve(0)
                    emit_ymm(0)
                    emit_ve(1)
                    emit_ymm(1)

                @block.scalar
                def _(s, tq=tq):
                    s.wait_ge(sem["ST"], ev_ymm[(tq, DST - 1, 1)])
                    if tq > 0:
                        s.wait_ge(sem["QG"], qg_y[tq - 1])
                    for b in TE_Y:
                        ins = s.copy(yq[:, b * TQW:(b + 1) * TQW], pys[b][:, :])
                        if b == 0:
                            inc(ins, "SS")
                    sfence(s)
                    ev_evac[tq] = cnt["SS"]

                @block.gpsimd
                def _(g, tq=tq):
                    g.wait_ge(sem["SS"], ev_evac[tq])
                    for b in range(BL):
                        if b not in TE_Y:
                            g.wait_ge(sem["SV"], ev_vy[(tq, DST - 1, b)])
                    for b in range(BL):
                        dma(g, "QG", y_in.ap()[tq, b * 128:(b + 1) * 128, :],
                            yq[:, b * TQW:(b + 1) * TQW])
                qg_y[tq] = cnt["QG"]

    s4 = snap()

    @block.gpsimd
    def _(g):
        wait_all(g, s4)
        with allow_ici_all_to_all():
            inc(g.collective_compute(
                "AllToAll", ALU.bypass,
                replica_groups=[list(range(NCORES))],
                ins=[y_in.ap().opt()],
                outs=[y_out.ap().opt()],
            ), "SG")

    s5 = snap()


    with (
        nc.sbuf_tensor("out_acc", [128, 8 * DM], F32) as out_acc,
        nc.sbuf_tensor("yg", [128, DCH * TOK], BF16) as yg,
        nc.sbuf_tensor("ycl", [128, 2 * TOK], F32) as ycl,
        nc.sbuf_tensor("ucl", [128, 2 * TOK], F32) as ucl,
        nc.sbuf_tensor("szl", [128, 2 * TOK], F32) as szl,
        nc.sbuf_tensor("gt", [128, TOK], F32) as gt,
        nc.sbuf_tensor("dcl", [128, 2], F32) as dcl,
        nc.sbuf_tensor("wovA", [128, 256], F32) as wovA,
        nc.sbuf_tensor("wovB", [128, 256], F32) as wovB,
        nc.sbuf_tensor("wobA", [128, 256], BF16) as wobA,
        nc.sbuf_tensor("wobB", [128, 256], BF16) as wobB,
        nc.sbuf_tensor("mst", [128, 2 * TOK], F32) as mst,
        nc.psum_tensor("pw0", [128, TOK], F32) as pw0,
        nc.psum_tensor("pw1", [128, TOK], F32) as pw1,
        nc.psum_tensor("ptm", [128, 128], F32) as ptm,
        nc.psum_tensor("ptm2", [128, 128], F32) as ptm2,
        nc.sbuf_tensor("idf", [128, 128], F32) as idf,
    ):
        pws = [pw0, pw1]

        @block.vector
        def _(v):
            inc(v.tensor_copy(idf[:, :], idb[:, :]), "SV")

        @block.sync
        def _(s):
            wait_all(s, s5, skip=("QS", "QC"))
            for tt in range(8):
                dma(s, "QS", out_acc[:, tt * DM:(tt + 1) * DM],
                    o_scr.ap()[tt * 128:(tt + 1) * 128, :])
        qs_oacc = cnt["QS"]

        @block.vector
        def _(v):
            v.wait_ge(sem["QS"], qs_oacc)
            inc(v.engine_nop(), "SV")
        sv_oacc = cnt["SV"]

        ev_g = {}
        for c in range(DCH):
            gp = c % 2

            @block.sync
            def _(s, c=c, gp=gp):
                if c == 0:
                    wait_all(s, s5, skip=("QS", "QC", "QG"))
                if c - 2 in ev_g:
                    s.wait_ge(sem["SV"], ev_g[c - 2])
                j, r = c // BL, (c % BL) * 128
                dma(s, "QS", ycl[:, gp * TOK:(gp + 1) * TOK], y_out.ap()[j, r:r + 128, :])
                dma(s, "QS", ucl[:, gp * TOK:(gp + 1) * TOK],
                    u_scr.ap()[c * 128:(c + 1) * 128, :])
                dma(s, "QS", szl[:, gp * TOK:(gp + 1) * TOK],
                    sz_scr.ap()[c * 128:(c + 1) * 128, HALO:HALO + TOK])
                dma(s, "QS", dcl[:, gp:gp + 1], d_vec[c * 128:(c + 1) * 128, :])

            @block.vector
            def _(v, c=c, gp=gp):
                v.wait_ge(sem["QS"], cnt["QS"])
                inc(v.scalar_tensor_tensor(gt[:, :], ucl[:, gp * TOK:(gp + 1) * TOK],
                                           dcl[:, gp:gp + 1],
                                           ycl[:, gp * TOK:(gp + 1) * TOK],
                                           op0=ALU.mult, op1=ALU.add), "SV")
                inc(v.tensor_tensor(yg[:, c * TOK:(c + 1) * TOK], gt[:, :],
                                    szl[:, gp * TOK:(gp + 1) * TOK],
                                    op=ALU.mult), "SV")
            ev_g[c] = cnt["SV"]

            if _DEBUG:
                @block.gpsimd
                def _(g, c=c):
                    g.wait_ge(sem["SV"], cnt["SV"] - 1)
                    dma(g, "QG", dbg["yown"][c * 128:(c + 1) * 128, :], gt[:, :])

        wovs = [wovA, wovB]
        wobs = [wobA, wobB]
        ev_wld = {}
        ev_wcast = {}
        ev_wmm = {}
        ev_wevac = {}
        ev_wadd = {}
        ev_wtr = {}
        for mq in range(8):
            for dc in range(DCH):
                gi = mq * DCH + dc
                wpar = gi % 2
                Q = "QS" if wpar == 0 else "QC"

                @block.sync if wpar == 0 else block.scalar
                def _(s, mq=mq, dc=dc, gi=gi, wpar=wpar, Q=Q):
                    if gi - 2 in ev_wcast:
                        s.wait_ge(sem["SV"], ev_wcast[gi - 2])
                    dma(s, Q, wovs[wpar][:, :],
                        w_out_t[dc * 128:(dc + 1) * 128, mq * 256:(mq + 1) * 256])
                ev_wld[gi] = cnt["QS" if wpar == 0 else "QC"]

                @block.vector
                def _(v, gi=gi, wpar=wpar, Q=Q):
                    v.wait_ge(sem[Q], ev_wld[gi])
                    if gi - 2 in ev_wmm:
                        v.wait_ge(sem["ST"], ev_wmm[gi - 2])
                    inc(v.tensor_copy(wobs[wpar][:, :], wovs[wpar][:, :]), "SV")
                ev_wcast[gi] = cnt["SV"]

                @block.tensor
                def _(t, mq=mq, dc=dc, gi=gi, wpar=wpar):
                    t.wait_ge(sem["SV"], ev_wcast[gi])
                    if dc == 0 and mq >= 1:
                        t.wait_ge(sem["SV"], ev_wevac[mq - 1])
                    last = None
                    for j in range(2):
                        for nh in range(2):
                            sl = slice(nh * 512, (nh + 1) * 512)
                            last = t.matmul(pws[j][:, sl], wobs[wpar][:, j * 128:(j + 1) * 128],
                                            yg[:, dc * TOK + nh * 512: dc * TOK + (nh + 1) * 512],
                                            start=(dc == 0), stop=(dc == DCH - 1),
                                            skip_group_check=True)
                    inc(last, "ST")
                ev_wmm[gi] = cnt["ST"]

            @block.vector
            def _(v, mq=mq):
                v.wait_ge(sem["ST"], ev_wmm[mq * DCH + DCH - 1])
                if mq >= 1:
                    v.wait_ge(sem["ST"], ev_wtr[(mq - 1, 1, 7)])
                for j in range(2):
                    inc(v.tensor_copy(mst[:, j * TOK:(j + 1) * TOK], pws[j][:, :]), "SV")
            ev_wevac[mq] = cnt["SV"]

            @block.tensor
            def _(t, mq=mq):
                t.wait_ge(sem["SV"], ev_wevac[mq])
                for j in range(2):
                    for tt in range(8):
                        k = j * 8 + tt
                        pk = (mq, j, tt - 2) if tt >= 2 else                              ((mq, j - 1, 6 + tt) if j == 1 else
                              ((mq - 1, 1, 6 + tt) if mq >= 1 else None))
                        if pk is not None and pk in ev_wadd:
                            t.wait_ge(sem["SV"], ev_wadd[pk])
                        ptx = ptm if k % 2 == 0 else ptm2
                        inc(t.transpose(ptx[:, :],
                                        mst[:, j * TOK + tt * 128: j * TOK + (tt + 1) * 128],
                                        idf[:, :]), "ST")
                        ev_wtr[(mq, j, tt)] = cnt["ST"]

                        @block.vector
                        def _(v, mq=mq, j=j, tt=tt, ptx=ptx):
                            v.wait_ge(sem["ST"], ev_wtr[(mq, j, tt)])
                            if mq == 0 and j == 0 and tt == 0:
                                v.wait_ge(sem["QS"], qs_oacc)
                            col = tt * DM + mq * 256 + j * 128
                            inc(v.tensor_tensor(out_acc[:, col:col + 128],
                                                out_acc[:, col:col + 128],
                                                ptx[:, :], op=ALU.add), "SV")
                            ev_wadd[(mq, j, tt)] = cnt["SV"]

        @block.gpsimd
        def _(g):
            g.wait_ge(sem["SV"], cnt["SV"])
            for tt in range(8):
                dma(g, "QG", out_d[tt * 128:(tt + 1) * 128, :],
                    out_acc[:, tt * DM:(tt + 1) * DM])
            g.wait_ge(sem["QG"], cnt["QG"])

    es.close()
    return nc


_NC_CACHE = None


def make_in_maps(L0_attn_x, L0_attn_W_in, L0_attn_conv_w, L0_attn_conv_b, L0_attn_W_x,
                 L0_attn_W_dt, L0_attn_b_dt, L0_attn_A_log, L0_attn_D, L0_attn_W_out,
                 L0_ffn_kernel, L0_ffn_bias):
    x = np.ascontiguousarray(np.asarray(L0_attn_x, dtype=np.float32).reshape(8192, DM))
    w_in = np.asarray(L0_attn_W_in, np.float32)
    w_in_4 = np.ascontiguousarray(w_in.reshape(2 * DCH, 128, KC, 128).transpose(3, 0, 2, 1))
    conv_w = np.ascontiguousarray(np.asarray(L0_attn_conv_w, np.float32))
    conv_b = np.ascontiguousarray(np.asarray(L0_attn_conv_b, np.float32).reshape(DIN, 1))
    w_x_t = np.ascontiguousarray(np.asarray(L0_attn_W_x, np.float32).T)
    w_dt_t = np.ascontiguousarray(np.asarray(L0_attn_W_dt, np.float32).T)
    b_dt_a = np.ascontiguousarray(np.asarray(L0_attn_b_dt, np.float32).reshape(DIN, 1))
    a_log = np.ascontiguousarray(np.asarray(L0_attn_A_log, np.float32))
    d_vec = np.ascontiguousarray(np.asarray(L0_attn_D, np.float32).reshape(DIN, 1))
    w_out_t = np.ascontiguousarray(np.asarray(L0_attn_W_out, np.float32).T)
    ffn_k = np.ascontiguousarray(np.asarray(L0_ffn_kernel, np.float32))
    ffn_k_t = np.ascontiguousarray(ffn_k.T)
    ffn_b = np.ascontiguousarray(np.asarray(L0_ffn_bias, np.float32).reshape(1, DM))

    in_maps = []
    for i in range(NCORES):
        lo = i * TOK
        xs = np.zeros((POS, DM), np.float32)
        if i > 0:
            xs[0:HALO] = x[lo - HALO:lo]
        xs[HALO:HALO + TOK] = x[lo:lo + TOK]
        in_maps.append({
            "x": xs,
            "w_in_4": w_in_4,
            "conv_w": conv_w,
            "conv_b": conv_b,
            "w_x_t": w_x_t,
            "w_dt_t": w_dt_t,
            "b_dt": b_dt_a,
            "a_log_sl": np.ascontiguousarray(a_log[i * CPC:(i + 1) * CPC]),
            "d_vec": d_vec,
            "w_out_t": w_out_t,
            "ffn_k_t": ffn_k_t,
            "ffn_k": ffn_k,
            "ffn_b": ffn_b,
        })
    return in_maps


def kernel(L0_attn_x, L0_attn_W_in, L0_attn_conv_w, L0_attn_conv_b, L0_attn_W_x,
           L0_attn_W_dt, L0_attn_b_dt, L0_attn_A_log, L0_attn_D, L0_attn_W_out,
           L0_ffn_kernel, L0_ffn_bias):
    global _NC_CACHE
    in_maps = make_in_maps(L0_attn_x, L0_attn_W_in, L0_attn_conv_w, L0_attn_conv_b,
                           L0_attn_W_x, L0_attn_W_dt, L0_attn_b_dt, L0_attn_A_log,
                           L0_attn_D, L0_attn_W_out, L0_ffn_kernel, L0_ffn_bias)
    if _NC_CACHE is None:
        _NC_CACHE = build_nc()
    res = run_bass_kernel_spmd(_NC_CACHE, in_maps, core_ids=list(range(NCORES)))
    out = np.concatenate([res.results[i]["out"] for i in range(NCORES)], axis=0)
    return out.reshape(1, 8192, DM).astype(np.float32)
